# version 17
# baseline (speedup 1.0000x reference)
import math
import numpy as np
import ml_dtypes
from contextlib import ExitStack

import concourse.bass as bass
import concourse.mybir as mybir
from concourse.bass_utils import run_bass_kernel_spmd

F32 = mybir.dt.float32
BF16 = mybir.dt.bfloat16
I32 = mybir.dt.int32
AF = mybir.ActivationFunctionType
ALU = mybir.AluOpType
AX = mybir.AxisListType
BFNP = ml_dtypes.bfloat16

DM = 1024
DF = 4096
NH = 16
DH = 64
TS = 4096
TH = 2048
TCAT = 4224
NT_SEQ = TS // 128
NT_CAT = TCAT // 128
EPS = 1e-5
ATTN_SCALE = float(DH) ** -0.5
RC = 12582912.0

ENGS = ("sync", "scalar", "vector", "tensor", "gpsimd")


class Sem:
    __slots__ = ("h", "n")

    def __init__(self, h):
        self.h = h
        self.n = 0

    def inc(self, k):
        self.n += k
        return self.n


SW_SEMS = frozenset((
    "att_out0", "att_out1", "f0_hst0", "f0_hst1", "f0c_os0", "f0c_os1",
    "f1_os0", "f1_os1", "gfm_st0", "gfm_st1", "l1_pst0", "l1_pst1",
    "l1_qst0", "l1_qst1", "l1_sst", "l1_wldg", "op_st0", "op_st1",
    "pf_st0_pw1", "pf_st0_pw2", "pf_st1_pw1", "pf_st1_pw2",
    "pf_st2_pw1", "pf_st2_pw2", "wld",
    "wq_st0", "wq_st1", "xnt_st0", "xnt_st1",
))


class KB:

    def __init__(self, nc):
        self.nc = nc
        self.ops = {e: [] for e in ENGS}
        self.es = ExitStack()
        self._semi = 0
        self.dsem = {e: self.sem() for e in ("sync", "scalar", "vector", "gpsimd")}
        self.bar = self.sem()
        self._scratch = {}
        self._sems = {}
        self._dma_sems = {}
        self._phase_es = None
        self._scratch_phase = {}
        self._sems_phase = {}
        self._sem_pool = []
        self._sem_pool_sw = []
        self._phase_pi = {"sw": 0, "hw": 0}
        self._sem_pool_used = 0

    def sem(self):
        self._semi += 1
        return Sem(self.es.enter_context(self.nc.semaphore(f"s{self._semi}")))

    def sb(self, name, shape, dtype):
        if self._phase_es is not None:
            return self._phase_es.enter_context(
                self.nc.sbuf_tensor(f"{name}_ph{self._phase_i}", shape, dtype))
        return self.es.enter_context(self.nc.sbuf_tensor(name, shape, dtype))

    def gsb(self, name, shape, dtype):
        key = (name, dtype)
        tab = self._scratch_phase if self._phase_es is not None else self._scratch
        h = tab.get(key)
        if h is None:
            h = self.sb(name, shape, dtype)
            tab[key] = h
        else:
            assert list(h.shape)[1] >= shape[1], (name, h.shape, shape)
        return h

    def begin_phase(self):
        assert self._phase_es is None
        self._phase_i = getattr(self, "_phase_i", 0) + 1
        self._phase_es = ExitStack()
        self._scratch_phase = {}
        self._sems_phase = {}
        self._phase_pi = {"sw": 0, "hw": 0}

    def end_phase(self):
        self._phase_es.close()
        self._phase_es = None
        self._scratch_phase = {}
        self._sems_phase = {}

    def on(self, eng, fn):
        self.ops[eng].append(fn)

    def dma(self, eng, out, in_, sem=None, **kw):
        if sem is not None and self._sems_phase:
            for k, v in self._sems_phase.items():
                if v is sem:
                    assert (k in SW_SEMS) == (eng == "gpsimd"), \
                        (k, eng, "sem class mismatch: add/remove from SW_SEMS")
                    break
        s = sem or self.dsem[eng]
        self._dma_sems[s] = eng
        v = s.inc(16)
        self.on(eng, lambda e, o=out, i=in_, sh=s.h, kw=kw: e.dma_start(out=o, in_=i, **kw).then_inc(sh, 16))
        return v

    def wait(self, eng, sem, v):
        self.on(eng, lambda e, sh=sem.h, vv=v: e.wait_ge(sh, vv))

    def step(self, eng, sem, fn):
        v = sem.inc(1)

        def run(e, fn=fn, sh=sem.h):
            inst = fn(e)
            inst.then_inc(sh, 1)
        self.on(eng, run)
        return v

    def dep(self, eng, sem, fn):
        self.wait(eng, sem, sem.n)
        return self.step(eng, sem, fn)

    def gsem(self, name):
        sw = name in SW_SEMS
        if self._phase_es is not None:
            s = self._sems_phase.get(name)
            if s is None:
                cls = "sw" if sw else "hw"
                pool = self._sem_pool_sw if sw else self._sem_pool
                i = self._phase_pi[cls]
                self._phase_pi[cls] = i + 1
                while len(pool) <= i:
                    pool.append(self.sem())
                s = pool[i]
                self._sems_phase[name] = s
            return s
        s = self._sems.get(name)
        if s is None:
            s = self.sem()
            self._sems[name] = s
        return s

    def barrier(self):
        for s, e in list(self._dma_sems.items()):
            self.wait(e, s, s.n)
        for e in ENGS:
            self.on(e, lambda eng, sh=self.bar.h: eng.nop().then_inc(sh, 1))
        self.bar.n += 5
        tgt = self.bar.n
        for e in ENGS:
            self.wait(e, self.bar, tgt)

    def emit(self):
        with self.nc.Block() as blk:
            for e in ENGS:
                fns = self.ops[e]
                if fns:
                    getattr(blk, e)(lambda eng, fns=fns: [f(eng) for f in fns])



def prepare_core_inputs(inp):
    x = np.ascontiguousarray(inp["L0_attn_x"], dtype=np.float32)
    per_core = []
    triu = np.triu(np.ones((128, 128), dtype=np.float32))
    ident16 = np.eye(128, dtype=np.float32).astype(BFNP)
    shared = {
        "w_qkv": np.ascontiguousarray(inp["L0_attn_w_qkv"], dtype=np.float32),
        "w1_0t": np.ascontiguousarray(inp["L0_ffn_w1"].T, dtype=np.float32),
        "convw": np.ascontiguousarray(
            inp["L0_ffn_conv_w"][:, 0, :].reshape(32, 128, 3).transpose(1, 0, 2).reshape(128, 96),
            dtype=np.float32),
        "convb": np.ascontiguousarray(
            inp["L0_ffn_conv_b"].reshape(32, 128).T, dtype=np.float32),
        "w2_0t": np.ascontiguousarray(inp["L0_ffn_w2"].T, dtype=np.float32),
        "wq1t": np.ascontiguousarray(inp["L1_attn_wq_w"].T, dtype=np.float32),
        "wk1t": np.ascontiguousarray(inp["L1_attn_wk_w"].T, dtype=np.float32),
        "wv1t": np.ascontiguousarray(inp["L1_attn_wv_w"].T, dtype=np.float32),
        "wo1t": np.ascontiguousarray(inp["L1_attn_wo_w"].T, dtype=np.float32),
        "bq1p": np.ascontiguousarray(inp["L1_attn_wq_b"].reshape(8, 128).T, dtype=np.float32),
        "bk1p": np.ascontiguousarray(inp["L1_attn_wk_b"].reshape(8, 128).T, dtype=np.float32),
        "bk1r": np.ascontiguousarray(inp["L1_attn_wk_b"][None, :], dtype=np.float32),
        "bv1r": np.ascontiguousarray(inp["L1_attn_wv_b"][None, :], dtype=np.float32),
        "bo1r": np.ascontiguousarray(inp["L1_attn_wo_b"][None, :], dtype=np.float32),
        "w1_1t": np.ascontiguousarray(inp["L1_ffn_w1"].T, dtype=np.float32),
        "b1p": np.ascontiguousarray(inp["L1_ffn_b1"].reshape(32, 128).T, dtype=np.float32),
        "w2_1t": np.ascontiguousarray(inp["L1_ffn_w2"].T, dtype=np.float32),
        "b2r": np.ascontiguousarray(inp["L1_ffn_b2"][None, :], dtype=np.float32),
        "triu": triu,
        "ident16": ident16,
        "ident32": np.eye(128, dtype=np.float32),
    }
    for c in range(8):
        b, h = c // 2, c % 2
        xs = x[b]
        own = xs[h * TH:(h + 1) * TH]
        pre = xs[0:TH]
        halo = np.zeros((4, DM), dtype=np.float32)
        if h == 0:
            halo[1] = xs[TH]
        else:
            halo[0] = xs[TH - 1]
        halo[3] = xs[TH]
        xcat = np.zeros((TCAT, DM), dtype=np.float32)
        xcat[0:TH] = own
        xcat[TH:TS] = pre
        xcat[TS:TS + 4] = halo
        d = dict(shared)
        d["x_seq_t"] = np.ascontiguousarray(xs.T)
        d["x_seq"] = xs
        d["x_cat"] = xcat
        d["x_cat_t"] = np.ascontiguousarray(xcat.T)
        d["recvmask"] = np.full((64, 1), float(h), dtype=np.float32)
        per_core.append(d)
    return per_core



IN_SPECS = {
    "x_seq_t": ([DM, TS], F32),
    "x_seq": ([TS, DM], F32),
    "x_cat": ([TCAT, DM], F32),
    "x_cat_t": ([DM, TCAT], F32),
    "w_qkv": ([DM, 3 * DM], F32),
    "w1_0t": ([DM, DF], F32),
    "convw": ([128, 96], F32),
    "convb": ([128, 32], F32),
    "w2_0t": ([DF, DM], F32),
    "wq1t": ([DM, DM], F32),
    "wk1t": ([DM, DM], F32),
    "wv1t": ([DM, DM], F32),
    "wo1t": ([DM, DM], F32),
    "bq1p": ([128, 8], F32),
    "bk1p": ([128, 8], F32),
    "bk1r": ([1, DM], F32),
    "bv1r": ([1, DM], F32),
    "bo1r": ([1, DM], F32),
    "w1_1t": ([DM, DF], F32),
    "b1p": ([128, 32], F32),
    "w2_1t": ([DF, DM], F32),
    "b2r": ([1, DM], F32),
    "recvmask": ([64, 1], F32),
    "triu": ([128, 128], F32),
    "ident16": ([128, 128], BF16),
    "ident32": ([128, 128], F32),
}

SCR_SPECS = {
    "k0t": ([DM, TS], BF16),
    "v0": ([NT_SEQ, 128, NH, DH], BF16),
    "q0t": ([DM, TS], BF16),
    "xqt0": ([DM, TCAT], BF16),
    "st0": ([6, TCAT], F32),
    "attn0o": ([TS, DM], F32),
    "h1d": ([DF, 4100], F32),
    "x1c": ([TS, DM], F32),
    "q1t": ([DM, TH], BF16),
    "k1t": ([DM, TH], BF16),
    "k1m": ([TS, DM], BF16),
    "v1m": ([TS, DM], BF16),
    "xqt1": ([DM, TH], BF16),
    "st1": ([6, TH], F32),
    "spre": ([64, DM], F32),
    "snap": ([16, 64, DM], BF16),
    "attn1o": ([TH, DM], F32),
    "h1d1": ([DF, TH], F32),
    "dsg1": ([128, TH // 128], F32),
    "y_own": ([TH, DM], F32),
}


def declare(nc, names, kind, specs=None):
    specs = specs or {**IN_SPECS, **SCR_SPECS}
    out = {}
    for n in names:
        shape, dt = specs[n]
        out[n] = nc.dram_tensor(n, shape, dt, kind=kind).ap()
    return out


def bcast_row(row_ap, t0, tw):
    r = row_ap[0:1, t0:t0 + tw].partition_broadcast(128)
    while len(r.shape) > 2:
        r = r.squeeze(1)
    return r


def emit_stat_rows(K, st_sb, NT, js, rows_d, ident32_sb):
    s_t = K.gsem("srow_t")
    s_e = K.gsem("srow_e")
    s_d = K.gsem("srow_d")
    stage = K.gsb("strow_stage", [128, 128], F32)
    for jn, j in enumerate(js):
        pb = K.pall[0:NT, 3584:3712]
        K.wait("tensor", s_e, s_e.n)

        K.step("tensor", s_t,
               lambda eng, j=j, pb=pb: eng.transpose(pb, st_sb[:, j * NT:(j + 1) * NT], ident32_sb[:]))
        K.wait("vector", s_t, s_t.n)
        if jn > 0:
            K.wait("vector", s_d, s_d.n)
        K.step("vector", s_e, lambda e, pb=pb, NT=NT: e.tensor_copy(stage[0:NT, :], pb))
        K.wait("scalar", s_e, s_e.n)
        K.dma("scalar", rows_d[j:j + 1, :].rearrange("o (i q) -> o i q", q=128).squeeze(0),
              stage[0:NT, :], sem=s_d)
    K.wait("scalar", s_d, s_d.n)



def emit_stats(K, x_ap, n_tiles, st_sb, with_quant, sT):
    xt0 = K.gsb("st_xt", [128, 2 * DM], F32)
    xt = [xt0[:, 0:DM], xt0[:, DM:2 * DM]]
    sq0 = K.gsb("st_sq", [128, 2 * DM], F32)
    sqb = [sq0[:, 0:DM], sq0[:, DM:2 * DM]]
    NT = n_tiles
    s_dv = K.gsem("st_dv")
    s_pl = K.gsem("st_pl")
    s_sq = K.gsem("st_sq")
    sld = [K.gsem("st_ld0"), K.gsem("st_ld1")]
    base_dv, base_pl, base_sq = s_dv.n, s_pl.n, s_sq.n
    for i in range(NT):
        b = i % 2
        if i >= 2:
            K.wait("sync", s_dv, base_dv + i - 1)
            K.wait("sync", s_pl, base_pl + i - 1)
            K.wait("sync", s_sq, base_sq + i - 1)
        v = K.dma("sync", xt[b], x_ap[i * 128:(i + 1) * 128, :], sem=sld[b])
        K.wait("vector", sld[b], v)
        K.wait("gpsimd", sld[b], v)
        if i >= 2:
            K.wait("gpsimd", s_sq, base_sq + i - 1)

        def pw(eng, i=i, b=b):
            return eng.tensor_tensor(sqb[b], xt[b], xt[b], ALU.mult)
        K.step("gpsimd", s_pl, pw)

        def dw(eng, i=i, b=b):
            eng.reduce_sum(st_sb[:, 6 * NT + i:6 * NT + i + 1], xt[b], axis=AX.X)
            eng.reduce_max(st_sb[:, 8 * NT + i:8 * NT + i + 1], xt[b], axis=AX.X)
            return eng.tensor_reduce(st_sb[:, 9 * NT + i:9 * NT + i + 1], xt[b],
                                     AX.X, ALU.min)
        K.step("vector", s_dv, dw)
        K.wait("vector", s_pl, s_pl.n)

        def dw2(eng, i=i, b=b):
            return eng.reduce_sum(st_sb[:, 7 * NT + i:7 * NT + i + 1], sqb[b], axis=AX.X)
        K.step("vector", s_sq, dw2)
    sc = K.gsem("st_sc")
    M = lambda j: st_sb[:, j * NT:(j + 1) * NT]
    K.wait("vector", s_dv, s_dv.n)
    K.wait("vector", s_sq, s_sq.n)
    K.dep("vector", sc, lambda e: e.tensor_scalar(M(0), M(6), 1.0 / DM, None, ALU.mult))
    K.dep("vector", sc, lambda e: e.tensor_scalar(M(7), M(7), 1.0 / DM, None, ALU.mult))
    K.dep("vector", sc, lambda e: e.tensor_tensor(M(6), M(0), M(0), ALU.mult))
    K.dep("vector", sc, lambda e: e.tensor_tensor(M(1), M(7), M(6), ALU.subtract))
    K.dep("vector", sc, lambda e: e.tensor_scalar(M(1), M(1), EPS, None, ALU.add))
    s_act = K.gsem("st_act")
    K.wait("scalar", sc, sc.n)
    K.dep("scalar", s_act, lambda e: e.activation(M(1), M(1), AF.Ln))
    K.dep("scalar", s_act, lambda e: e.activation(M(1), M(1), AF.Exp, scale=-0.5))
    K.wait("vector", s_act, s_act.n)
    K.dep("vector", sc, lambda e: e.tensor_tensor(M(3), M(0), M(1), ALU.mult))
    if with_quant:
        K.dep("vector", sc, lambda e: e.tensor_tensor(M(8), M(8), M(0), ALU.subtract))
        K.dep("vector", sc, lambda e: e.tensor_tensor(M(9), M(0), M(9), ALU.subtract))
        K.dep("vector", sc, lambda e: e.tensor_tensor(M(8), M(8), M(9), ALU.max))
        K.dep("vector", sc, lambda e: e.tensor_tensor(M(8), M(8), M(1), ALU.mult))
        K.dep("vector", sc, lambda e: e.tensor_scalar(M(8), M(8), 1e-5, None, ALU.max))
        K.dep("vector", sc, lambda e: e.tensor_scalar(M(5), M(8), 1.0 / 127.0, None, ALU.mult))
        K.dep("vector", sc, lambda e: e.reciprocal(M(8), M(8)))
        K.dep("vector", sc, lambda e: e.tensor_scalar(M(8), M(8), 127.0, None, ALU.mult))
        K.dep("vector", sc, lambda e: e.tensor_tensor(M(2), M(8), M(1), ALU.mult))
        K.dep("vector", sc, lambda e: e.tensor_tensor(M(4), M(0), M(2), ALU.mult))


def emit_xnt(K, xt_ap, st_d, n_tiles, xnt_sb, xqt_stage, xqt_d, with_quant, sT):
    NT = n_tiles
    T = NT * 128
    CH = 11 if NT % 11 == 0 else 8
    NCH = NT // CH
    TW = CH * 128
    jrows = [1, 3, 2, 4] if with_quant else [1, 3]
    bc = K.gsb("xnt_bc", [128, 4 * 1408], F32)
    xf0 = K.gsb("xnt_xf", [128, 2 * 1408], F32)
    xf = [xf0[:, 0:TW], xf0[:, 1408:1408 + TW]]
    t10 = K.gsb("xnt_t1", [128, 2 * 1408], F32)
    t1b = [t10[:, 0:TW], t10[:, 1408:1408 + TW]]
    qt0 = K.gsb("xnt_qt", [128, 2 * 1408], F32) if with_quant else None
    qi0 = K.gsb("xnt_qi", [128, 2 * 1408], I32) if with_quant else None
    s_done = K.gsem("xnt_done")
    s_ev = K.gsem("xnt_ev")
    s_ep = K.gsem("xnt_ep")
    s_rc = K.gsem("xnt_rc")
    s_qi = K.gsem("xnt_qic")
    s_bc = K.gsem("xnt_bcld")
    sld = [K.gsem("xnt_ld0"), K.gsem("xnt_ld1")]
    sst = [K.gsem("xnt_st0"), K.gsem("xnt_st1")]
    base = s_done.n
    b_ev, b_ep, b_rc, b_qi = s_ev.n, s_ep.n, s_rc.n, s_qi.n
    it = 0
    qdma = {}
    for tc in range(NCH):
        if tc > 0:
            K.wait("scalar", s_ev, s_ev.n)
            K.wait("scalar", s_ep, s_ep.n)
            if with_quant:
                K.wait("scalar", s_done, base + it)
        vs = 0
        for jj, j in enumerate(jrows):
            src_ = bcast_row(st_d[j:j + 1, :], tc * TW, TW)
            vs = K.dma("scalar", bc[:, jj * 1408:jj * 1408 + TW], src_, sem=s_bc)
        for fc in range(8):
            b = it % 2
            eng = "vector"
            se = s_ev
            be = b_ev
            nops = 4 if with_quant else 2
            if it >= 2:
                K.wait("sync", se, be + (it - 2) * nops + (2 if with_quant else 1))
            v = K.dma("sync", xf[b], xt_ap[fc * 128:(fc + 1) * 128, tc * TW:(tc + 1) * TW], sem=sld[b])
            K.wait(eng, sld[b], v)
            if fc == 0:
                K.wait("vector", s_bc, vs)
                K.wait("gpsimd", s_bc, vs)
            if with_quant and it >= 2:
                K.wait("vector", sst[b], qdma[it - 2])
                K.wait(eng, s_qi, b_qi + it - 1)
            r_bc = bc[:, 0:TW]
            mr_bc = bc[:, 1408:1408 + TW]
            out = xnt_sb[:, fc * T + tc * TW: fc * T + (tc + 1) * TW]
            if it >= 2:
                K.wait(eng, se, be + (it - 2) * nops + (3 if with_quant else 2))
            K.step(eng, se,
                   lambda e, b=b: e.tensor_tensor(t1b[b], xf[b], r_bc, ALU.mult))
            if with_quant:
                al_bc = bc[:, 2 * 1408:2 * 1408 + TW]
                mal_bc = bc[:, 3 * 1408:3 * 1408 + TW]
                qtmp = qt0[:, b * 1408:b * 1408 + TW]
                qi = qi0[:, b * 1408:b * 1408 + TW]
                K.step(eng, se,
                       lambda e, b=b, qtmp=qtmp: e.tensor_tensor(qtmp, xf[b], al_bc, ALU.mult))
                K.dep(eng, se,
                      lambda e, b=b, out=out: e.tensor_tensor(out, t1b[b], mr_bc, ALU.subtract))
                K.dep(eng, se,
                      lambda e, qtmp=qtmp: e.tensor_tensor(qtmp, qtmp, mal_bc, ALU.subtract))
                K.wait("scalar", se, se.n)
                K.step("scalar", s_rc,
                       lambda e, qtmp=qtmp: e.activation(qtmp, qtmp, AF.Copy, bias=RC))
                K.dep("scalar", s_rc,
                      lambda e, qtmp=qtmp: e.activation(qtmp, qtmp, AF.Copy, bias=-RC))
                K.wait("vector", s_rc, b_rc + 2 * it + 2)
                K.step("vector", s_qi, lambda e, qtmp=qtmp, qi=qi: e.tensor_copy(qi, qtmp))
                K.wait("vector", s_qi, s_qi.n)

                def last(e, b=b, qi=qi):
                    return e.tensor_copy(xqt_stage[:, b * 1408:b * 1408 + TW], qi)
                K.step("vector", s_done, last)
            else:
                K.dep(eng, se,
                      lambda e, b=b, out=out: e.tensor_tensor(out, t1b[b], mr_bc, ALU.subtract))
            if with_quant:
                K.wait("gpsimd", s_done, s_done.n)
                qdma[it] = K.dma("gpsimd", xqt_d[fc * 128:(fc + 1) * 128, tc * TW:(tc + 1) * TW],
                                 xqt_stage[:, b * 1408:b * 1408 + TW], sem=sst[b])
            it += 1
    K.wait("vector", s_done, s_done.n)
    K.wait("vector", s_ev, s_ev.n)
    K.wait("vector", s_ep, s_ep.n)


def emit_gemm_fm(K, w_sb, w_col0, w_stride, xnt_sb, xnt_T, out_d, n_fo, t_lo, t_hi,
                 out_bf, stage, sT, out_row0=0):
    s_mm = K.gsem("gfm_mm")
    s_ev = K.gsem("gfm_ev")
    sst = [K.gsem("gfm_st0"), K.gsem("gfm_st1")]
    NTC2 = (t_hi - t_lo) // 512
    pall = K.pall
    base_ev = s_ev.n
    dmas = {}
    it = 0
    for fo in range(n_fo):
        for tch in range(NTC2):
            t0 = t_lo + tch * 512
            slot = it % 2
            pb = pall[:, slot * 512:(slot + 1) * 512]
            if it >= 2:
                K.wait("tensor", s_ev, base_ev + it - 1)

            def mm(eng, fo=fo, t0=t0, pb=pb):
                for fc in range(7):
                    eng.matmul(pb,
                               w_sb[:, fc * w_stride + w_col0 + fo * 128: fc * w_stride + w_col0 + (fo + 1) * 128],
                               xnt_sb[:, fc * xnt_T + t0: fc * xnt_T + t0 + 512],
                               start=(fc == 0), stop=False)
                return eng.matmul(pb,
                                  w_sb[:, 7 * w_stride + w_col0 + fo * 128: 7 * w_stride + w_col0 + (fo + 1) * 128],
                                  xnt_sb[:, 7 * xnt_T + t0: 7 * xnt_T + t0 + 512],
                                  start=False, stop=True)
            K.step("tensor", s_mm, mm)
            K.wait("scalar", s_mm, s_mm.n)
            if it >= 2:
                K.wait("scalar", sst[slot], dmas[it - 2])
            sb = stage[:, slot * 512:(slot + 1) * 512]
            K.step("scalar", s_ev, lambda e, pb=pb, sb=sb: e.copy(sb, pb))
            K.wait("gpsimd", s_ev, s_ev.n)
            dmas[it] = K.dma("gpsimd", out_d[out_row0 + fo * 128: out_row0 + (fo + 1) * 128,
                                             t0 - t_lo: t0 - t_lo + 512], sb, sem=sst[slot])
            it += 1


def emit_att0(K, T, hook=None):
    nc = K.nc
    pall = K.pall
    vaug = K.gsb("vaug", [128, 32 * 16 * 65], BF16)
    ktq = K.gsb("ktq", [128, 4 * TS], BF16)
    pt = K.gsb("att_pt", [128, 2 * 1024], BF16)
    ev = K.gsb("att_ev", [128, 2 * 512], F32)
    ast = K.gsb("att_st", [128, 2 * 512], F32)
    arc = K.gsb("att_rc", [128, 2], F32)
    s_vld = K.gsem("att_vld")
    s_ldk = [K.gsem("att_ldk0"), K.gsem("att_ldk1")]
    s_ldq = [K.gsem("att_ldq0"), K.gsem("att_ldq1")]
    s_sc = K.gsem("att_sc")
    s_ex = K.gsem("att_ex")
    s_pv = K.gsem("att_pv")
    s_evd = K.gsem("att_evd")
    s_tr = K.gsem("att_tr")
    s_ns = K.gsem("att_ns")
    s_out = [K.gsem("att_out0"), K.gsem("att_out1")]

    for kc in range(32):
        dst = vaug[:, kc * 1040:(kc + 1) * 1040].rearrange("p (h e) -> p h e", e=65)[:, :, 0:64]
        K.dma("sync", dst, T["v0"][kc], sem=s_vld)
    ones_view = vaug[:].rearrange("p (g e) -> p g e", e=65)[:, :, 64:65]
    K.wait("vector", s_vld, s_vld.n)
    smem = K.gsem("att_ones")
    K.step("vector", smem, lambda e: e.memset(ones_view, 1.0))
    K.wait("tensor", smem, smem.n)

    b_sc, b_ex, b_pv, b_evd, b_tr, b_ns = s_sc.n, s_ex.n, s_pv.n, s_evd.n, s_tr.n, s_ns.n
    QG = 8
    ldk = {}
    ldq = {}
    odma = {}
    for hp in range(2):
        ldk[hp] = K.dma("sync", ktq[:, (hp % 2) * TS:(hp % 2) * TS + TS],
                        T["k0t"][hp * 128:(hp + 1) * 128, :], sem=s_ldk[hp % 2])
        ldq[hp] = K.dma("sync", ktq[:, (2 + hp % 2) * TS:(2 + hp % 2) * TS + TS],
                        T["q0t"][hp * 128:(hp + 1) * 128, :], sem=s_ldq[hp % 2])
    it_hq = 0
    for hp in range(8):
        sl = hp % 2
        kt = ktq[:, sl * TS:(sl + 1) * TS]
        qt = ktq[:, (2 + sl) * TS:(2 + sl) * TS + TS]
        K.wait("tensor", s_ldk[sl], ldk[hp])
        K.wait("tensor", s_ldq[sl], ldq[hp])
        if hp + 2 < 8:
            K.wait("sync", s_sc, b_sc + (hp + 1) * QG * 32)
            ldk[hp + 2] = K.dma("sync", ktq[:, sl * TS:(sl + 1) * TS],
                                T["k0t"][(hp + 2) * 128:(hp + 3) * 128, :], sem=s_ldk[sl])
            ldq[hp + 2] = K.dma("sync", ktq[:, (2 + sl) * TS:(2 + sl) * TS + TS],
                                T["q0t"][(hp + 2) * 128:(hp + 3) * 128, :], sem=s_ldq[sl])
        for qg in range(QG):
            kcn = 32

            def emit_pv(pvkc, hp=hp):
                K.wait("tensor", s_ex, b_ex + it_hq * kcn + pvkc + 1)
                if pvkc == 0 and it_hq >= 1:
                    K.wait("tensor", s_evd, b_evd + it_hq)

                def pv_mm(eng, kc=pvkc, ss=pvkc % 2, hp=hp):
                    eng.matmul(pall[0:65, 2048:2560],
                               vaug[:, kc * 1040 + (2 * hp) * 65: kc * 1040 + (2 * hp) * 65 + 65],
                               pt[:, ss * 1024:ss * 1024 + 512],
                               start=(kc == 0), stop=(kc == kcn - 1))
                    return eng.matmul(pall[0:65, 2560:3072],
                                      vaug[:, kc * 1040 + (2 * hp + 1) * 65: kc * 1040 + (2 * hp + 1) * 65 + 65],
                                      pt[:, ss * 1024 + 512:ss * 1024 + 1024],
                                      start=(kc == 0), stop=(kc == kcn - 1))
                K.step("tensor", s_pv, pv_mm)

            for kc in range(kcn):
                ss = kc % 2
                pS = pall[:, ss * 1024:(ss + 1) * 1024]
                if kc >= 2:
                    K.wait("tensor", s_ex, b_ex + it_hq * kcn + kc - 1)

                def sc_mm(eng, kt=kt, qt=qt, kc=kc, qg=qg, pS=pS):
                    eng.matmul(pS[:, 0:512], kt[0:64, kc * 128:(kc + 1) * 128],
                               qt[0:64, qg * 512:(qg + 1) * 512],
                               start=True, stop=True, tile_position=(0, 0))
                    return eng.matmul(pS[:, 512:1024], kt[64:128, kc * 128:(kc + 1) * 128],
                                      qt[64:128, qg * 512:(qg + 1) * 512],
                                      start=True, stop=True, tile_position=(64, 0))
                K.step("tensor", s_sc, sc_mm)
                K.wait("scalar", s_sc, s_sc.n)
                if kc >= 2:
                    K.wait("scalar", s_pv, b_pv + it_hq * kcn + kc - 1)

                def ex_op(eng, ss=ss, pS=pS):
                    return eng.activation(pt[:, ss * 1024:(ss + 1) * 1024], pS,
                                          AF.Exp, scale=ATTN_SCALE)
                K.step("scalar", s_ex, ex_op)
                if kc >= 1:
                    emit_pv(kc - 1)
            emit_pv(kcn - 1)
            K.wait("vector", s_pv, s_pv.n)
            if it_hq >= 1:
                K.wait("vector", s_ns, b_ns + it_hq * 16)
                K.wait("vector", s_tr, b_tr + it_hq * 8)

            def evac(e):
                e.tensor_copy(ev[0:65, 0:512], pall[0:65, 2048:2560])
                return e.tensor_copy(ev[0:65, 512:1024], pall[0:65, 2560:3072])
            K.step("vector", s_evd, evac)
            K.wait("tensor", s_evd, s_evd.n)
            so = it_hq % 2
            if it_hq >= 2:
                K.wait("vector", s_out[so], odma[it_hq - 2])
            if it_hq >= 1:
                K.wait("tensor", s_ns, b_ns + it_hq * 16)
            for X in range(2):
                for j in range(4):
                    tb = (X * 4 + j) % 2
                    pT = pall[0:128, 3072 + tb * 512: 3072 + tb * 512 + 65]
                    if X * 4 + j >= 2:
                        K.wait("tensor", s_ns, b_ns + it_hq * 16 + 2 * (X * 4 + j - 1))

                    def tr(e, X=X, j=j, pT=pT):
                        return e.transpose(pT, ev[0:65, X * 512 + j * 128: X * 512 + (j + 1) * 128],
                                           T["i32sb"][0:65, 0:65])
                    K.step("tensor", s_tr, tr)
                    K.wait("vector", s_tr, s_tr.n)
                    tb2 = (X * 4 + j) % 2
                    K.step("vector", s_ns,
                           lambda e, pT=pT, tb2=tb2: e.reciprocal(arc[:, tb2:tb2 + 1], pT[:, 64:65]))

                    def nrm(e, X=X, j=j, pT=pT, so=so, tb2=tb2):
                        return e.tensor_scalar(ast[:, so * 512 + j * 128 + X * 64: so * 512 + j * 128 + X * 64 + 64],
                                               pT[:, 0:64], arc[:, tb2:tb2 + 1], None, ALU.mult)
                    K.dep("vector", s_ns, nrm)
            K.wait("gpsimd", s_ns, s_ns.n)
            dst = T["attn0o"][qg * 512:(qg + 1) * 512, hp * 128:(hp + 1) * 128]                 .rearrange("(j p) c -> p j c", p=128)
            odma[it_hq] = K.dma(
                "gpsimd", dst,
                ast[:, so * 512:(so + 1) * 512].rearrange("p (j c) -> p j c", c=128),
                sem=s_out[so])
            if hook is not None:
                hook(it_hq)
            it_hq += 1
    K.barrier()


def build_att0():
    nc = bass.Bass("TRN2", target_bir_lowering=False, debug=False)
    ins = ["k0t", "v0", "q0t", "ident32"]
    T = declare(nc, ["k0t", "v0", "q0t"], "ExternalInput")
    T.update(declare(nc, ["ident32"], "ExternalInput", {**IN_SPECS}))
    T.update(declare(nc, ["attn0o"], "ExternalOutput"))
    K = KB(nc)
    K.pall = nc.alloc_psum_tensor("pall", [128, 4096], F32)
    i32sb = K.gsb("ident32_sb", [128, 128], F32)
    K.dma("sync", i32sb[:], T["ident32"])
    T["i32sb"] = i32sb
    K.barrier()
    emit_att0(K, T)
    K.emit()
    return nc, ins, ["attn0o"]


def emit_wquant(K, w_ap, nchunks, ncols, wq_sb, tag, dram_out=None):
    wbuf = K.gsb("scr32k", [128, 2 * DF], F32)
    wsum = K.gsb("wq_sum", [128, 40], F32)
    onesc = K.gsb("ones_col", [128, 1], F32)
    onesr = K.gsb("ones_row", [1, 128], F32)
    scl = K.gsb(f"wsc_{tag}", [128, 2], F32)
    s_ld = [K.gsem("wq_ld0"), K.gsem("wq_ld1")]
    s_dv = K.gsem("wq_dv")
    s_mm = K.gsem("wq_mm")
    smi = K.gsem("wq_misc")
    b_dv = s_dv.n
    K.step("vector", smi, lambda e: e.memset(onesc[:], 1.0))
    K.step("vector", smi, lambda e: e.memset(onesr[:], 1.0))
    lds = {}
    for i in range(nchunks):
        b = i % 2
        if i >= 2:
            K.wait("sync", s_dv, b_dv + i - 1)
        lds[i] = K.dma("sync", wbuf[:, b * DF:b * DF + ncols],
                       w_ap[i * 128:(i + 1) * 128, :], sem=s_ld[b])
        K.wait("vector", s_ld[b], lds[i])
        K.step("vector", s_dv,
               lambda e, i=i, b=b: e.tensor_reduce(wsum[:, i:i + 1], wbuf[:, b * DF:b * DF + ncols],
                                                   AX.X, ALU.add, apply_absolute_value=True))
    K.wait("vector", s_dv, s_dv.n)
    sc2 = K.gsem("wq_sc")
    K.dep("vector", sc2, lambda e: e.reduce_sum(wsum[:, 36:37], wsum[:, 0:nchunks], axis=AX.X))
    K.wait("tensor", sc2, sc2.n)
    K.step("tensor", s_mm, lambda e: e.matmul(K.pall[0:1, 3584:3585], wsum[:, 36:37], onesc[:],
                                              start=True, stop=True))
    K.wait("vector", s_mm, s_mm.n)
    K.dep("vector", sc2, lambda e: e.tensor_scalar(wsum[0:1, 37:38], K.pall[0:1, 3584:3585],
                                                   1.0 / (nchunks * 128 * ncols), None, ALU.mult))
    K.dep("vector", sc2, lambda e: e.tensor_scalar(wsum[0:1, 37:38], wsum[0:1, 37:38], 1e-5, None, ALU.max))
    K.dep("vector", sc2, lambda e: e.reciprocal(wsum[0:1, 38:39], wsum[0:1, 37:38]))
    K.wait("tensor", sc2, sc2.n)
    K.step("tensor", s_mm, lambda e: e.matmul(K.pall[0:128, 3712:3713], onesr[:], wsum[0:1, 38:39],
                                              start=True, stop=True))
    K.wait("vector", s_mm, s_mm.n)
    K.dep("vector", sc2, lambda e: e.tensor_copy(scl[:, 0:1], K.pall[0:128, 3712:3713]))
    K.wait("tensor", sc2, sc2.n)
    K.step("tensor", s_mm, lambda e: e.matmul(K.pall[0:128, 3712:3713], onesr[:], wsum[0:1, 37:38],
                                              start=True, stop=True))
    K.wait("vector", s_mm, s_mm.n)
    K.dep("vector", sc2, lambda e: e.tensor_copy(scl[:, 1:2], K.pall[0:128, 3712:3713]))
    qtmp = K.gsb("scr16k", [128, 4096], F32)[:]
    qi = K.gsb("qi_arena", [128, DF], I32)[:]
    sq = K.gsem("wq_q")
    K.wait("vector", sc2, sc2.n)
    NSTEP = 7
    bq = sq.n
    qdd = {}
    K.wait("sync", s_dv, s_dv.n)
    for i in range(nchunks):
        b = i % 2
        if i >= 2:
            K.wait("sync", sq, bq + (i - 1) * NSTEP)
        lds[i] = K.dma("sync", wbuf[:, b * DF:b * DF + ncols],
                       w_ap[i * 128:(i + 1) * 128, :], sem=s_ld[b])
        K.wait("vector", s_ld[b], lds[i])
        K.dep("vector", sq, lambda e, b=b: e.tensor_scalar(
            qtmp[:, 0:ncols], wbuf[:, b * DF:b * DF + ncols], scl[:, 0:1], None, ALU.mult))
        K.dep("vector", sq, lambda e: e.tensor_scalar(qtmp[:, 0:ncols], qtmp[:, 0:ncols], RC, None, ALU.add))
        K.dep("vector", sq, lambda e: e.tensor_scalar(qtmp[:, 0:ncols], qtmp[:, 0:ncols], -RC, None, ALU.add))
        K.dep("vector", sq, lambda e: e.tensor_copy(qi[:, 0:ncols], qtmp[:, 0:ncols]))
        K.dep("vector", sq, lambda e: e.tensor_scalar(qi[:, 0:ncols], qi[:, 0:ncols], 1, None, ALU.min))
        K.dep("vector", sq, lambda e: e.tensor_scalar(qi[:, 0:ncols], qi[:, 0:ncols], -1, None, ALU.max))
        if dram_out is None:
            K.dep("vector", sq, lambda e, i=i: e.tensor_copy(
                wq_sb[:, i * ncols:(i + 1) * ncols], qi[:, 0:ncols]))
        else:
            qst = K.gsb("scr4k", [128, 1024], F32)[:].bitcast(BF16)
            sqs = [K.gsem("wq_st0"), K.gsem("wq_st1")]
            bq2 = i % 2
            if i >= 2:
                K.wait("vector", sqs[bq2], qdd.get(i - 2, 0))
            K.dep("vector", sq, lambda e, i=i, bq2=bq2: e.tensor_copy(
                qst[:, bq2 * 1024:bq2 * 1024 + ncols], qi[:, 0:ncols]))
            K.wait("gpsimd", sq, sq.n)
            qdd[i] = K.dma("gpsimd", dram_out[i * 128:(i + 1) * 128, :],
                           qst[:, bq2 * 1024:bq2 * 1024 + ncols], sem=sqs[bq2])
    K.wait("vector", sq, sq.n)
    return scl


def wquant_pf_slices(K, T, w_ap, nchunks, ncols, scl, tag, wq_sb=None, dram_out=None,
                     pre_wait=None, pre_stores=()):
    nc = K.nc
    wbuf = K.gsb("wqpf_buf", [128, 2 * DF], F32)
    wsum = K.gsb(f"wqpf_sum_{tag}", [128, nchunks + 1], F32)
    wtot = K.gsb(f"wqpf_tot_{tag}", [1, 1], F32)
    srow = K.gsb(f"wqpf_row_{tag}", [1, 2], F32)
    wrow_d = nc.dram_tensor(f"wqpf_d_{tag}", [1, 2], F32, kind="Internal").ap()
    qtmp = K.gsb("wqpf_qt", [128, DF], F32)
    qi = K.gsb("wqpf_qi", [128, DF], I32)
    s_ld = [K.gsem(f"pf_ld0_{tag}"), K.gsem(f"pf_ld1_{tag}")]
    s_dv = K.gsem(f"pf_dv_{tag}")
    s_pl = K.gsem(f"pf_pl_{tag}")
    s_rw = K.gsem(f"pf_rw_{tag}")
    s_q = K.gsem(f"pf_q_{tag}")
    s_st = [K.gsem(f"pf_st0_{tag}"), K.gsem(f"pf_st1_{tag}"), K.gsem(f"pf_st2_{tag}")]
    b_dv = s_dv.n
    lds = {}
    out = []
    p1g = 2 if nchunks == 8 else 4

    def mk_p1(i0, n):
        def go():
            for i in range(i0, i0 + n):
                b = i % 2
                if i < 2 and pre_wait is not None:
                    K.wait("scalar", pre_wait, pre_wait.n)
                if i >= 2:
                    K.wait("scalar", s_dv, b_dv + i - 1)
                lds[i] = K.dma("scalar", wbuf[:, b * DF:b * DF + ncols],
                               w_ap[i * 128:(i + 1) * 128, :], sem=s_ld[b])
                K.wait("vector", s_ld[b], lds[i])
                K.step("vector", s_dv,
                       lambda e, i=i, b=b: e.tensor_reduce(wsum[:, i:i + 1],
                                                           wbuf[:, b * DF:b * DF + ncols],
                                                           AX.X, ALU.add, apply_absolute_value=True))
        return go
    for i0 in range(0, nchunks, p1g):
        out.append(mk_p1(i0, min(p1g, nchunks - i0)))

    def finale():
        K.wait("vector", s_dv, s_dv.n)
        K.step("vector", s_dv, lambda e: e.reduce_sum(wsum[:, nchunks:nchunks + 1],
                                                      wsum[:, 0:nchunks], axis=AX.X))
        K.wait("gpsimd", s_dv, s_dv.n)
        K.step("gpsimd", s_pl, lambda e: e.tensor_reduce(wtot[:], wsum[:, nchunks:nchunks + 1],
                                                         AX.C, ALU.add))
        K.wait("vector", s_pl, s_pl.n)
        K.dep("vector", s_rw, lambda e: e.tensor_scalar(
            srow[0:1, 1:2], wtot[:], 1.0 / (nchunks * 128 * ncols), None, ALU.mult))
        K.dep("vector", s_rw, lambda e: e.tensor_scalar(srow[0:1, 1:2], srow[0:1, 1:2],
                                                        1e-5, None, ALU.max))
        K.dep("vector", s_rw, lambda e: e.reciprocal(srow[0:1, 0:1], srow[0:1, 1:2]))
        K.wait("gpsimd", s_rw, s_rw.n)
        v1 = K.dma("gpsimd", wrow_d, srow[:], sem=s_st[0])
        K.wait("gpsimd", s_st[0], v1)
        v2 = K.dma("gpsimd", scl[:], bcast_row(wrow_d, 0, 2), sem=s_st[0])
        K.wait("vector", s_st[0], v2)
    out.append(finale)

    p2g = 1 if ncols >= 4096 else 2
    bq = s_q.n
    NST = 7 if dram_out is None else 7
    qdd = {}

    def mk_p2(i0, n):
        def go():
            for i in range(i0, i0 + n):
                b = i % 2
                if i >= 2:
                    K.wait("scalar", s_q, bq + (i - 1) * NST)
                lds[i] = K.dma("scalar", wbuf[:, b * DF:b * DF + ncols],
                               w_ap[i * 128:(i + 1) * 128, :], sem=s_ld[b])
                K.wait("vector", s_ld[b], lds[i])
                K.dep("vector", s_q, lambda e, b=b: e.tensor_scalar(
                    qtmp[:, 0:ncols], wbuf[:, b * DF:b * DF + ncols], scl[:, 0:1], None, ALU.mult))
                K.dep("vector", s_q, lambda e: e.tensor_scalar(qtmp[:, 0:ncols], qtmp[:, 0:ncols],
                                                               RC, None, ALU.add))
                K.dep("vector", s_q, lambda e: e.tensor_scalar(qtmp[:, 0:ncols], qtmp[:, 0:ncols],
                                                               -RC, None, ALU.add))
                K.dep("vector", s_q, lambda e: e.tensor_copy(qi[:, 0:ncols], qtmp[:, 0:ncols]))
                K.dep("vector", s_q, lambda e: e.tensor_scalar(qi[:, 0:ncols], qi[:, 0:ncols],
                                                               1, None, ALU.min))
                K.dep("vector", s_q, lambda e: e.tensor_scalar(qi[:, 0:ncols], qi[:, 0:ncols],
                                                               -1, None, ALU.max))
                if dram_out is None:
                    K.dep("vector", s_q, lambda e, i=i: e.tensor_copy(
                        wq_sb[:, i * ncols:(i + 1) * ncols], qi[:, 0:ncols]))
                else:
                    qst = K.gsb("wqpf_qst", [128, 4096], F32)[:].bitcast(BF16)
                    bq2 = i % 2
                    if i < 2:
                        for ps_ in pre_stores:
                            K.wait("vector", ps_, ps_.n)
                    if i >= 2:
                        K.wait("vector", s_st[1 + bq2], qdd.get(i - 2, 0))
                    K.dep("vector", s_q, lambda e, i=i, bq2=bq2: e.tensor_copy(
                        qst[:, bq2 * 4096:bq2 * 4096 + ncols], qi[:, 0:ncols]))
                    K.wait("gpsimd", s_q, s_q.n)
                    qdd[i] = K.dma("gpsimd", dram_out[i * 128:(i + 1) * 128, :],
                                   qst[:, bq2 * 4096:bq2 * 4096 + ncols], sem=s_st[1 + bq2])
        return go
    first_p2 = len(out)

    def p2head():
        K.wait("vector", s_rw, s_rw.n)
        K.wait("scalar", s_dv, s_dv.n)
        mk_p2(0, p2g)()
    out.append(p2head)
    for i0 in range(p2g, nchunks, p2g):
        out.append(mk_p2(i0, min(p2g, nchunks - i0)))
    return out, s_q, s_st


def emit_ffn0(K, T):
    nc = K.nc
    pall = K.pall
    if "w1qd_g" in T:
        w2qd = T["w2qd_g"]
        scl1 = T["scl_w1"]
        scl2 = T["scl_w2"]
        w1q = K.gsb("w1q", [128, 8 * DF], BF16)
        s_wql = [K.gsem(f"f0_wq{s}") for s in range(8)]
        wl_v = {}
        for slab in range(8):
            for fc in range(8):
                wl_v[slab] = K.dma("sync" if fc % 2 == 0 else "scalar",
                                   w1q[:, fc * DF + slab * 512: fc * DF + slab * 512 + 512],
                                   T["w1qd_g"][fc * 128:(fc + 1) * 128,
                                               slab * 512:(slab + 1) * 512],
                                   sem=s_wql[slab])
    else:
        w1q = K.gsb("w1q", [128, 8 * DF], BF16)
        w2qd = nc.dram_tensor("w2qd", [DF, 1024], BF16, kind="Internal").ap()
        scl1 = emit_wquant(K, T["w1_0t"], 8, DF, w1q, "w1")
        K.barrier()
        scl2 = emit_wquant(K, T["w2_0t"], 32, 1024, None, "w2", dram_out=w2qd)
        K.barrier()
    s2row = nc.dram_tensor("s2row", [1, TS], F32, kind="Internal").ap()

    scr16k = K.gsb("scr16k", [128, 4096], F32)
    arena3 = K.gsb("scr8k", [128, 2048], F32)
    arena4 = K.gsb("scr4k", [128, 1024], F32)
    xqs = scr16k[:].bitcast(BF16)
    dsb = arena4[:]
    hst = arena3[:, 0:1024]
    s_xl = [K.gsem("f0_xl0"), K.gsem("f0_xl1")]
    s_dl = [K.gsem("f0_dl0"), K.gsem("f0_dl1")]
    s_hm = K.gsem("f0_hmm")
    s_he = K.gsem("f0_hev")
    s_hd = K.gsem("f0_hdv")
    s_hs = [K.gsem("f0_hst0"), K.gsem("f0_hst1")]
    b_he, b_hd, b_hm = s_he.n, s_hd.n, s_hm.n

    stripes = [(s * 512, 512) for s in range(8)] + [(4096, 128)]
    it = 0
    xl = {}
    dl = {}
    hs = {}
    for si, (t0, w) in enumerate(stripes):
        sl = si % 2
        if si >= 2:
            K.wait("sync", s_hm, b_hm + (si - 1) * 32)
        for fc in range(8):
            xl[(si, fc)] = K.dma("sync", xqs[:, sl * 4096 + fc * 512: sl * 4096 + fc * 512 + w],
                                 T["xqt0"][fc * 128:(fc + 1) * 128, t0:t0 + w], sem=s_xl[sl])
        dl[si] = K.dma("scalar", dsb[:, sl * 512:sl * 512 + w],
                       bcast_row(T["st0"][5:6, :], t0, w), sem=s_dl[sl])
        for cc in range(32):
            ps = pall[:, (it % 2) * 512:(it % 2) * 512 + 512]
            if it >= 2:
                K.wait("tensor", s_he, b_he + it - 1)
            K.wait("tensor", s_xl[sl], xl[(si, 7)])
            if si == 0 and "w1qd_g" in T:
                K.wait("tensor", s_wql[cc // 4], wl_v[cc // 4])

            def mm(e, sl=sl, cc=cc, w=w, ps=ps):
                inst = None
                for fc in range(8):
                    inst = e.matmul(ps[:, 0:w], w1q[:, fc * DF + cc * 128:(fc * DF + cc * 128) + 128],
                                    xqs[:, sl * 4096 + fc * 512: sl * 4096 + fc * 512 + w],
                                    start=(fc == 0), stop=(fc == 7))
                return inst
            K.step("tensor", s_hm, mm)
            K.wait("scalar", s_hm, s_hm.n)
            hb = hst[:, (it % 2) * 512:(it % 2) * 512 + 512]
            if it >= 2:
                K.wait("scalar", s_hd, b_hd + it - 1)
                K.wait("scalar", s_hs[it % 2], hs[it - 2])

            def ev(e, ps=ps, hb=hb, w=w):
                return e.activation(hb[:, 0:w], ps[:, 0:w], AF.Copy, scale=scl1[:, 1:2])
            K.step("scalar", s_he, ev)
            K.wait("vector", s_he, s_he.n)
            if it >= 2:
                K.wait("vector", s_hs[it % 2], hs[it - 2])
            K.wait("vector", s_dl[sl], dl[si])

            def dsc(e, hb=hb, sl=sl, w=w):
                return e.tensor_tensor(hb[:, 0:w], hb[:, 0:w], dsb[:, sl * 512:sl * 512 + w], ALU.mult)
            K.step("vector", s_hd, dsc)
            K.wait("gpsimd", s_hd, s_hd.n)
            if t0 < TS:
                base = 1 + t0 if t0 < TH else 2051 + (t0 - TH)
                hs[it] = K.dma("gpsimd", T["h1d"][cc * 128:(cc + 1) * 128, base:base + w],
                               hb[:, 0:w], sem=s_hs[it % 2])
            else:
                for j, dc in enumerate((0, 2049, 2050, 4099)):
                    K.dma("gpsimd", T["h1d"][cc * 128:(cc + 1) * 128, dc:dc + 1],
                          hb[:, j:j + 1], sem=s_hs[it % 2],
                          allow_slow_non_contiguous=True)
                hs[it] = s_hs[it % 2].n
            it += 1
    K.barrier()

    cwsb = K.gsb("convw_sb", [128, 96], F32)
    cbsb = K.gsb("convb_sb", [128, 32], F32)
    swl = K.gsem("f0c_wl")
    K.dma("sync", cwsb[:], T["convw"], sem=swl)
    K.dma("sync", cbsb[:], T["convb"], sem=swl)
    w2sb = w1q
    s_w2r = K.gsem("f0c_w2r")
    w2n = 0
    for cc in range(32):
        w2n = K.dma("sync" if cc % 2 == 0 else "scalar",
                    w2sb[:, cc * 1024:(cc + 1) * 1024],
                    w2qd[cc * 128:(cc + 1) * 128, :], sem=s_w2r)
    hw = K.gsb("hwin", [128, 4 * 514], F32)
    yb = K.gsb("convy", [128, 4 * 512], F32)
    mxb = K.gsb("mxbuf", [128, 128], F32)
    gfm2 = K.gsb("gfm2", [128, 4 * DF], F32)[:].bitcast(BF16)
    gmax = K.gsb("gmax", [128, 16], F32)
    s2b = K.gsb("scr4k", [128, 1024], F32)[:, 0:512]
    qtsl = K.gsb("qi_arena", [128, DF], I32)[:, 1024:2048].bitcast(F32)
    qisl = K.gsb("qi_arena", [128, DF], I32)[:, 2048:3072]
    rst = K.gsb("scr16k", [128, 4096], F32)[:]
    ost = K.gsb("scr8k", [128, 2048], F32)[:]
    s_wl = [K.gsem("f0c_l0"), K.gsem("f0c_l1"), K.gsem("f0c_l2"), K.gsem("f0c_l3")]
    s_cv = K.gsem("f0c_cv")
    s_cvp = K.gsem("f0c_cvp")
    s_sl = K.gsem("f0c_silu")
    s_tp = K.gsem("f0c_tp")
    s_mx = K.gsem("f0c_mx")
    s_mxp = K.gsem("f0c_mxp")
    s_qq = K.gsem("f0c_qq")
    s_qa = K.gsem("f0c_qa")
    s_qap = K.gsem("f0c_qap")
    s_rc = K.gsem("f0c_rc")
    s_qi = K.gsem("f0c_qi")
    s_qb = K.gsem("f0c_qb")
    s_f2 = K.gsem("f0c_f2")
    s_oe = K.gsem("f0c_oe")
    s_ov = K.gsem("f0c_ov")
    s_os = [K.gsem("f0c_os0"), K.gsem("f0c_os1")]
    s_rl = [K.gsem("f0c_rl0"), K.gsem("f0c_rl1")]
    s_sc2 = K.gsem("f0c_s2")
    s_s2l = K.gsem("f0c_s2l")
    K.wait("vector", swl, swl.n)
    K.wait("gpsimd", swl, swl.n)
    K.wait("scalar", swl, swl.n)
    ods = {}
    itq = 0
    b_oe0, b_ov0 = s_oe.n, s_ov.n
    pend = []
    pi = [0]

    def drain(n):
        while pi[0] < len(pend) and n > 0:
            pend[pi[0]]()
            pi[0] += 1
            n -= 1
    for st_i in range(8):
        set_i, st = st_i // 4, st_i % 4
        base = 0 if set_i == 0 else 2050
        t0g = set_i * TH + st * 512
        sl = st_i % 2
        so2 = st_i % 2
        gfm = gfm2[:, so2 * 16384:(so2 + 1) * 16384]
        gmx = gmax[:, so2 * 8:(so2 + 1) * 8]
        sf2h = s_f2.n
        if st_i >= 2:
            K.wait("scalar", s_f2, sf2h)
            K.wait("vector", s_f2, sf2h)
            K.wait("gpsimd", s_f2, sf2h)
        if st_i > 0:
            K.wait("tensor", s_mx, s_mx.n)
        b_sl = s_sl.n
        for cc in range(32):
            drain(1)
            slot = cc % 4
            wb = hw[:, slot * 514:slot * 514 + 514]
            if cc >= 4 or st_i > 0:
                K.wait("sync" if slot < 2 else "scalar", s_sl, b_sl + cc - 3)
            lw = K.dma("sync" if slot < 2 else "scalar",
                       wb, T["h1d"][cc * 128:(cc + 1) * 128,
                                    base + st * 512: base + st * 512 + 514],
                       sem=s_wl[slot])
            K.wait("scalar", s_wl[slot], lw)
            yb_ = yb[:, slot * 512:slot * 512 + 512]

            def cv1(e, wb=wb, yb_=yb_, cc=cc):
                return e.activation(yb_, wb[:, 0:512], AF.Copy, scale=cwsb[:, cc * 3:cc * 3 + 1])
            K.step("scalar", s_cvp, cv1)
            K.wait("vector", s_cvp, s_cvp.n)

            def cv2(e, wb=wb, yb_=yb_, cc=cc):
                return e.scalar_tensor_tensor(yb_, wb[:, 1:513], cwsb[:, cc * 3 + 1:cc * 3 + 2],
                                              yb_, ALU.mult, ALU.add)
            K.step("vector", s_cv, cv2)

            def cv3(e, wb=wb, yb_=yb_, cc=cc):
                return e.scalar_tensor_tensor(yb_, wb[:, 2:514], cwsb[:, cc * 3 + 2:cc * 3 + 3],
                                              yb_, ALU.mult, ALU.add)
            K.dep("vector", s_cv, cv3)
            K.wait("scalar", s_cv, s_cv.n)

            def sil(e, yb_=yb_, cc=cc, gfm=gfm):
                return e.activation(gfm[:, cc * 512:(cc + 1) * 512], yb_, AF.Silu,
                                    bias=cbsb[:, cc:cc + 1])
            K.step("scalar", s_sl, sil)
        b_tp, b_mx = s_tp.n, s_mx.n
        b_mxp = s_mxp.n
        for cc in range(32):
            meng = "vector"
            msem = s_mx
            tb = cc % 2
            pT = pall[:, 2048 + tb * 512: 2048 + tb * 512 + 512].bitcast(BF16)
            K.wait("tensor", s_sl, b_sl + cc + 1)
            if cc >= 2:
                K.wait("tensor", s_mx, b_mx + cc - 1)

            def tp(e, cc=cc, pT=pT, gfm=gfm):
                inst = None
                for j in range(4):
                    inst = e.transpose(pT[:, j * 128:(j + 1) * 128],
                                       gfm[:, cc * 512 + j * 128: cc * 512 + (j + 1) * 128],
                                       T["i16sb"][:])
                return inst
            K.step("tensor", s_tp, tp)
            drain(2)
            K.wait(meng, s_tp, s_tp.n)

            def mx(e, cc=cc, pT=pT):
                e.tensor_reduce(mxb[:, cc:cc + 1], pT[:, 0:128], AX.X, ALU.max, apply_absolute_value=True)
                e.tensor_reduce(mxb[:, 32 + cc:33 + cc], pT[:, 128:256], AX.X, ALU.max, apply_absolute_value=True)
                e.tensor_reduce(mxb[:, 64 + cc:65 + cc], pT[:, 256:384], AX.X, ALU.max, apply_absolute_value=True)
                return e.tensor_reduce(mxb[:, 96 + cc:97 + cc], pT[:, 384:512], AX.X, ALU.max,
                                       apply_absolute_value=True)
            K.step(meng, msem, mx)
        K.wait("vector", s_mx, s_mx.n)
        smf = s_mx

        def fold(e, gmx=gmx):
            e.tensor_reduce(gmx[:, 0:1], mxb[:, 0:32], AX.X, ALU.max)
            e.tensor_reduce(gmx[:, 1:2], mxb[:, 32:64], AX.X, ALU.max)
            e.tensor_reduce(gmx[:, 2:3], mxb[:, 64:96], AX.X, ALU.max)
            return e.tensor_reduce(gmx[:, 3:4], mxb[:, 96:128], AX.X, ALU.max)
        K.step("vector", smf, fold)
        drain(99)
        K.dep("vector", s_mx, lambda e, gmx=gmx: e.tensor_scalar(gmx[:, 0:4], gmx[:, 0:4],
                                                                 1e-5, None, ALU.max))
        K.wait("tensor", s_mx, s_mx.n)
        K.wait("tensor", s_sc2, s_sc2.n)
        K.step("tensor", s_sc2, lambda e, gmx=gmx: e.transpose(pall[0:4, 3072:3200], gmx[:, 0:4],
                                                               T["i32sb"][:]))
        K.wait("vector", s_sc2, s_sc2.n)
        K.dep("vector", s_sc2, lambda e: e.reciprocal(pall[0:4, 3072:3200], pall[0:4, 3072:3200]))
        K.dep("vector", s_sc2, lambda e: e.tensor_scalar(pall[0:4, 3072:3200], pall[0:4, 3072:3200],
                                                         127.0, None, ALU.mult))
        K.dep("vector", s_sc2, lambda e: e.tensor_copy(s2b[0:4, 0:128], pall[0:4, 3072:3200]))
        K.wait("scalar", s_sc2, s_sc2.n)
        K.dma("scalar", s2row[0:1, t0g:t0g + 512].rearrange("o (j q) -> o j q", q=128).squeeze(0),
              s2b[0:4, 0:128], sem=s_s2l)
        K.wait("scalar", s_s2l, s_s2l.n)
        K.wait("scalar", s_qa, s_qa.n)
        K.wait("scalar", s_qap, s_qap.n)
        vv = K.dma("scalar", s2b[:, 0:512], bcast_row(s2row, t0g, 512), sem=s_s2l)
        K.wait("vector", s_s2l, vv)
        K.wait("gpsimd", s_s2l, vv)
        b_qq = s_qq.n
        b_qa, b_qap, b_rc, b_qi, b_qb = s_qa.n, s_qap.n, s_rc.n, s_qi.n, s_qb.n
        for cc in range(32):
            pool_q = cc % 4 in (1, 3)
            eq = "gpsimd" if pool_q else "vector"
            sq = s_qap if pool_q else s_qa
            qts = qtsl[:, (cc % 2) * 512:(cc % 2) * 512 + 512]
            qis = qisl[:, (cc % 2) * 512:(cc % 2) * 512 + 512]
            K.wait(eq, s_sl, b_sl + cc + 1)
            if cc >= 2 or st_i > 0:
                K.wait(eq, s_qi, b_qi + cc - 1)

            def qa(e, cc=cc, qts=qts, gfm=gfm):
                return e.tensor_tensor(qts, gfm[:, cc * 512:(cc + 1) * 512], s2b[:, 0:512], ALU.mult)
            K.step(eq, sq, qa)
            K.wait("scalar", sq, sq.n)
            K.step("scalar", s_rc, lambda e, qts=qts: e.activation(qts, qts, AF.Copy, bias=RC))
            K.dep("scalar", s_rc, lambda e, qts=qts: e.activation(qts, qts, AF.Copy, bias=-RC))
            K.wait("vector", s_rc, b_rc + 2 * cc + 2)
            if cc >= 2 or st_i > 0:
                K.wait("vector", s_qb, b_qb + cc - 1)
            K.step("vector", s_qi, lambda e, qts=qts, qis=qis: e.tensor_copy(qis, qts))
            K.wait("vector", s_qi, s_qi.n)
            K.step("vector", s_qb,
                   lambda e, cc=cc, qis=qis, gfm=gfm: e.tensor_copy(gfm[:, cc * 512:(cc + 1) * 512], qis))
        for tt_ in range(4):
            K.dep("vector", s_qq, lambda e, tt_=tt_, gmx=gmx: e.tensor_scalar(
                gmx[:, 4 + tt_:5 + tt_], gmx[:, tt_:tt_ + 1], 1.0 / 127.0, None, ALU.mult))
            K.dep("vector", s_qq, lambda e, tt_=tt_, gmx=gmx: e.tensor_tensor(
                gmx[:, 4 + tt_:5 + tt_], gmx[:, 4 + tt_:5 + tt_], scl2[:, 1:2], ALU.mult))
        qqv = s_qq.n
        b_qb_s = b_qb
        itq_s = itq
        gfm_s = gfm
        gmx_s = gmx
        t0g_s = t0g
        pend = []
        pi[0] = 0

        def mkf2(cc, r, itq_s=itq_s, b_qb_s=b_qb_s, gfm_s=gfm_s):
            def go():
                if cc == 0:
                    if itq_s == 0 and r == 0:
                        K.wait("tensor", s_w2r, w2n)
                    K.wait("tensor", s_oe, s_oe.n)
                K.wait("tensor", s_qb, b_qb_s + cc + 1)

                def f2(e, cc=cc, r=r):
                    inst = None
                    for tt_ in (2 * r, 2 * r + 1):
                        for eh in range(2):
                            inst = e.matmul(
                                pall[:, (tt_ % 2) * 1024 + eh * 512: (tt_ % 2) * 1024 + eh * 512 + 512],
                                gfm_s[:, cc * 512 + tt_ * 128: cc * 512 + tt_ * 128 + 128],
                                w2sb[:, cc * 1024 + eh * 512: cc * 1024 + eh * 512 + 512],
                                start=(cc == 0), stop=(cc == 31))
                    return inst
                K.step("tensor", s_f2, f2)
            return go

        def mkout(tt_, itq_s=itq_s, gmx_s=gmx_s, t0g_s=t0g_s, qqv=qqv):
            def go():
                po = pall[:, (tt_ % 2) * 1024:(tt_ % 2) * 1024 + 1024]
                trow = t0g_s + tt_ * 128
                so = tt_ % 2
                gidx = itq_s * 4 + tt_
                if gidx >= 2:
                    K.wait("scalar", s_ov, b_ov0 + 2 * gidx - 2)
                lr = K.dma("scalar", rst[:, so * 2048:so * 2048 + 1024],
                           T["x_cat"][trow:trow + 128, :], sem=s_rl[so])
                lr2 = K.dma("scalar", rst[:, so * 2048 + 1024:so * 2048 + 2048],
                            T["attn0o"][trow:trow + 128, :], sem=s_rl[so])
                K.wait("scalar", s_f2, s_f2.n)
                K.wait("scalar", s_qq, qqv)
                if gidx >= 2:
                    K.wait("scalar", s_os[so], ods[gidx - 2])

                def oev(e, po=po, so=so, tt_=tt_):
                    return e.activation(ost[:, so * 1024:(so + 1) * 1024], po, AF.Copy,
                                        scale=gmx_s[:, 4 + tt_:5 + tt_])
                K.step("scalar", s_oe, oev)
                K.wait("vector", s_oe, s_oe.n)
                K.wait("vector", s_rl[so], lr2)

                K.dep("vector", s_ov, lambda e, so=so: e.tensor_tensor(
                    ost[:, so * 1024:(so + 1) * 1024], ost[:, so * 1024:(so + 1) * 1024],
                    rst[:, so * 2048:so * 2048 + 1024], ALU.add))
                K.dep("vector", s_ov, lambda e, so=so: e.tensor_tensor(
                    ost[:, so * 1024:(so + 1) * 1024], ost[:, so * 1024:(so + 1) * 1024],
                    rst[:, so * 2048 + 1024:so * 2048 + 2048], ALU.add))
                K.wait("gpsimd", s_ov, s_ov.n)
                ods[gidx] = K.dma("gpsimd", T["x1c"][trow:trow + 128, :],
                                  ost[:, so * 1024:(so + 1) * 1024], sem=s_os[so])
            return go
        for cc in range(32):
            pend.append(mkf2(cc, 0))
        pend.append(mkout(0))
        pend.append(mkout(1))
        for cc in range(32):
            pend.append(mkf2(cc, 1))
        pend.append(mkout(2))
        pend.append(mkout(3))
        itq += 1
    drain(999)
    K.barrier()


def build_ffn0():
    nc = bass.Bass("TRN2", target_bir_lowering=False, debug=False)
    ins = ["xqt0", "st0", "w1_0t", "convw", "convb", "w2_0t", "x_cat", "attn0o",
           "ident32", "ident16"]
    T = declare(nc, ["xqt0", "st0", "attn0o"], "ExternalInput")
    T.update(declare(nc, ["w1_0t", "convw", "convb", "w2_0t", "x_cat", "ident32",
                          "ident16"], "ExternalInput", {**IN_SPECS}))
    T.update(declare(nc, ["x1c"], "ExternalOutput"))
    T.update(declare(nc, ["h1d"], "Internal"))
    K = KB(nc)
    K.pall = nc.alloc_psum_tensor("pall", [128, 4096], F32)
    i32sb = K.gsb("ident32_sb", [128, 128], F32)
    i16sb = K.gsb("ident16_sb", [128, 128], BF16)
    sldi = K.gsem("ident_ld")
    K.dma("sync", i32sb[:], T["ident32"], sem=sldi)
    K.dma("sync", i16sb[:], T["ident16"], sem=sldi)
    T["i32sb"] = i32sb
    T["i16sb"] = i16sb
    K.barrier()
    emit_ffn0(K, T)
    K.emit()
    return nc, ins, ["x1c"]


def emit_p0(K, T, upto=99):
    nc = K.nc
    wq_sb = K.gsb("wbig", [128, 8 * 3072], BF16)
    s_wld = K.gsem("wld")
    for fc in range(8):
        K.dma("gpsimd", wq_sb[:, fc * 3072:(fc + 1) * 3072], T["w_qkv"][fc * 128:(fc + 1) * 128, :],
              sem=s_wld)
    if "i32sb" in T:
        ident32_sb = T["i32sb"]
    else:
        ident32_sb = K.gsb("ident32_sb", [128, 128], F32)
        K.dma("sync", ident32_sb[:], T["ident32"])
    st_seq = K.gsb("st_rows", [128, 17 * NT_CAT], F32)
    emit_stats(K, T["x_seq"], NT_SEQ, st_seq, False, "s")
    stseq_d = nc.dram_tensor("stseq_d", [6, TS], F32, kind="Internal").ap()
    K.barrier()
    if upto < 2: return
    emit_stat_rows(K, st_seq, NT_SEQ, [1, 3], stseq_d, ident32_sb)
    K.barrier()
    if upto < 3: return
    xnt_seq = K.gsb("xnt_arena", [128, 8 * TCAT], BF16)
    emit_xnt(K, T["x_seq_t"], stseq_d, NT_SEQ, xnt_seq, None, None, False, "s")
    K.barrier()
    if upto < 4: return
    stage = K.gsb("stage_bf", [128, 1024], BF16)
    emit_gemm_fm(K, wq_sb, DM, 3072, xnt_seq, TS, T["k0t"], 8, 0, TS, True, stage, "k")
    K.barrier()
    if upto < 5: return
    s_mm = K.gsem("v0_mm")
    s_ev = K.gsem("v0_ev")
    sstv = [K.gsem("v0_st0"), K.gsem("v0_st1")]
    it = 0
    b_v0e = s_ev.n
    vdmas = {}
    v0v = T["v0"].rearrange("kc k h d -> kc k (h d)")
    for tc in range(NT_SEQ):
        for fo2 in range(2):
            slot = it % 2
            pb = K.pall[:, 1024 + slot * 512: 1024 + (slot + 1) * 512]
            if it >= 2:
                K.wait("tensor", s_ev, b_v0e + it - 1)

            def mm(eng, tc=tc, fo2=fo2, pb=pb):
                inst = None
                for fc in range(8):
                    inst = eng.matmul(pb,
                               xnt_seq[:, fc * TS + tc * 128: fc * TS + (tc + 1) * 128],
                               wq_sb[:, fc * 3072 + 2048 + fo2 * 512: fc * 3072 + 2048 + (fo2 + 1) * 512],
                               start=(fc == 0), stop=(fc == 7))
                return inst
            K.step("tensor", s_mm, mm)
            K.wait("vector", s_mm, s_mm.n)
            if it >= 2:
                K.wait("vector", sstv[slot], vdmas[it - 2])
            sb = stage[:, slot * 512:(slot + 1) * 512]
            K.step("vector", s_ev, lambda e, pb=pb, sb=sb: e.tensor_copy(sb, pb))
            K.wait("sync", s_ev, s_ev.n)
            vdmas[it] = K.dma("sync", v0v[tc, :, fo2 * 512:(fo2 + 1) * 512], sb, sem=sstv[slot])
            it += 1
    K.barrier()
    if upto < 6: return
    st_cat = K.gsb("st_rows", [128, 17 * NT_CAT], F32)
    emit_stats(K, T["x_cat"], NT_CAT, st_cat, True, "c")
    K.barrier()
    emit_stat_rows(K, st_cat, NT_CAT, [0, 1, 2, 3, 4, 5], T["st0"], ident32_sb)
    K.barrier()
    if upto < 7: return
    xnt_cat = K.gsb("xnt_arena", [128, 8 * TCAT], BF16)
    xq_stage = K.gsb("xq_stage", [128, 2 * 1408], BF16)
    emit_xnt(K, T["x_cat_t"], T["st0"], NT_CAT, xnt_cat, xq_stage, T["xqt0"], True, "c")
    K.barrier()
    if upto < 8: return
    emit_gemm_fm(K, wq_sb, 0, 3072, xnt_cat, TCAT, T["q0t"], 8, 0, TS, True, stage, "q")
    K.barrier()


def bias_bcast(K, row_ap, dst_sb, onesr, tag):
    s = K.gsem("bb_mm")
    sl = K.gsem(f"bb_ld_{tag}")
    brow = K.gsb(f"br_{tag}", [1, DM], F32)
    v = K.dma("sync", brow[:], row_ap, sem=sl)
    K.wait("tensor", sl, v)
    K.wait("tensor", K.gsem("bb_ev"), K.gsem("bb_ev").n)
    for half in range(2):
        K.step("tensor", s, lambda e, half=half: e.matmul(
            K.pall[:, 3072 + half * 512:3072 + (half + 1) * 512], onesr[:],
            brow[:, half * 512:(half + 1) * 512], start=True, stop=True))
    K.wait("vector", s, s.n)
    se = K.gsem("bb_ev")
    K.step("vector", se, lambda e: e.tensor_copy(dst_sb[:, 0:512], K.pall[:, 3072:3584]))
    K.step("vector", se, lambda e: e.tensor_copy(dst_sb[:, 512:1024], K.pall[:, 3584:4096]))


def emit_l1p(K, T):
    nc = K.nc
    pall = K.pall
    NT = 32
    st = K.gsb("st_rows", [128, 17 * NT_CAT], F32)
    emit_stats(K, T["x1c"], NT, st, True, "l1")
    M = lambda j: st[:, j * NT:(j + 1) * NT]
    sc = K.gsem("st_sc")
    s_act = K.gsem("st_act")
    K.dep("vector", sc, lambda e: e.tensor_tensor(M(10), M(1), M(1), ALU.mult))
    K.dep("vector", sc, lambda e: e.tensor_scalar(M(10), M(10), -EPS, None, ALU.mult))
    K.dep("vector", sc, lambda e: e.tensor_scalar(M(10), M(10), 1.0 + EPS, None, ALU.add))
    K.wait("scalar", sc, sc.n)
    K.dep("scalar", s_act, lambda e: e.activation(M(11), M(10), AF.Ln))
    K.dep("scalar", s_act, lambda e: e.activation(M(11), M(11), AF.Exp, scale=-0.5))
    K.wait("vector", s_act, s_act.n)
    K.dep("vector", sc, lambda e: e.tensor_tensor(M(12), M(5), M(11), ALU.mult))
    K.dep("vector", sc, lambda e: e.tensor_scalar(M(12), M(12), 127.0, None, ALU.mult))
    K.dep("vector", sc, lambda e: e.tensor_scalar(M(12), M(12), 1e-5, None, ALU.max))
    K.dep("vector", sc, lambda e: e.tensor_scalar(M(14), M(12), 1.0 / 127.0, None, ALU.mult))
    K.dep("vector", sc, lambda e: e.reciprocal(M(12), M(12)))
    K.dep("vector", sc, lambda e: e.tensor_scalar(M(12), M(12), 127.0, None, ALU.mult))
    K.dep("vector", sc, lambda e: e.tensor_tensor(M(12), M(12), M(11), ALU.mult))
    K.dep("vector", sc, lambda e: e.tensor_tensor(M(12), M(12), M(1), ALU.mult))
    K.dep("vector", sc, lambda e: e.tensor_tensor(M(13), M(0), M(12), ALU.mult))
    K.dep("vector", sc, lambda e: e.tensor_scalar(M(15), M(3), -1.0, None, ALU.mult))
    K.dep("vector", sc, lambda e: e.tensor_scalar(M(16), M(13), -1.0, None, ALU.mult))
    s_t = K.gsem("srow_t"); s_e = K.gsem("srow_e"); s_d = K.gsem("srow_d")
    stage = K.gsb("strow_stage", [128, 128], F32)
    K.wait("tensor", sc, sc.n)
    K.wait("tensor", s_e, s_e.n)
    K.step("tensor", s_t, lambda e: e.transpose(pall[0:16, 3584:3712],
                                                st[:, 14 * NT:14 * NT + 16], T["i32sb"][:]))
    K.wait("vector", s_t, s_t.n)
    K.wait("vector", s_d, s_d.n)
    K.step("vector", s_e, lambda e: e.tensor_copy(stage[0:16, :], pall[0:16, 3584:3712]))
    K.wait("scalar", s_e, s_e.n)
    K.dma("scalar", T["st1"][5:6, :].rearrange("o (i q) -> o i q", q=128).squeeze(0),
          stage[0:16, :], sem=s_d)
    K.barrier()
    xnt1 = K.gsb("xnt_arena", [128, 8 * TCAT], BF16)
    x1b = K.gsb("scr16k", [128, 4096], F32)
    tokb = K.gsb("scr8k", [128, 2048], F32)[:].bitcast(BF16)
    qtm = K.gsb("scr4k", [128, 1024], F32)[:]
    qi3 = K.gsb("qi_arena", [128, DF], I32)[:, 0:1024]
    xstg = K.gsb("l1_xstg", [128, 2048], BF16)
    s_xl = [K.gsem("l1_xl0"), K.gsem("l1_xl1")]
    s_xn = K.gsem("l1_xn")
    s_qc = K.gsem("l1_qc")
    s_tp = K.gsem("l1_tp")
    s_te = K.gsem("l1_te")
    s_qtp = K.gsem("l1_qtp")
    s_qte = K.gsem("l1_qte")
    s_qst = [K.gsem("l1_qst0"), K.gsem("l1_qst1")]
    s_rc1 = K.gsem("l1_rc")
    b_xn, b_tp, b_te, b_qc, b_qtp, b_qte = s_xn.n, s_tp.n, s_te.n, s_qc.n, s_qtp.n, s_qte.n
    xls = {}
    qsts = {}
    for i in range(NT):
        bs = i % 2
        own = i < 16
        if i >= 2:
            K.wait("sync", s_xn, b_xn + 2 * (i - 1))
            K.wait("sync", s_qc, s_qc.n)
        xls[i] = K.dma("sync", x1b[:, bs * 1024:(bs + 1) * 1024],
                       T["x1c"][i * 128:(i + 1) * 128, :], sem=s_xl[bs])
        K.wait("vector", s_xl[bs], xls[i])
        if i >= 2:
            K.wait("vector", s_tp, b_tp + (i - 1) * 2)
        K.dep("vector", s_xn, lambda e, i=i, bs=bs: e.tensor_scalar(
            qtm, x1b[:, bs * 1024:(bs + 1) * 1024], M(1)[:, i:i + 1], None, ALU.mult))
        K.dep("vector", s_xn, lambda e, i=i, bs=bs: e.tensor_scalar(
            tokb[:, bs * 2048:bs * 2048 + 1024], qtm, M(3)[:, i:i + 1], None, ALU.subtract))
        K.wait("tensor", s_xn, s_xn.n)
        for half in range(2):
            pT = pall[:, half * 512:half * 512 + 256].bitcast(BF16)
            if i * 2 + half >= 2:
                K.wait("tensor", s_te, b_te + i * 2 + half - 1)

            def tp4(e, i=i, bs=bs, half=half, pT=pT):
                inst = None
                for j in range(4):
                    fc = half * 4 + j
                    inst = e.transpose(pT[:, j * 128:(j + 1) * 128],
                                       tokb[:, bs * 2048 + fc * 128: bs * 2048 + (fc + 1) * 128],
                                       T["i16sb"][:])
                return inst
            K.step("tensor", s_tp, tp4)
            K.wait("vector", s_tp, s_tp.n)
            dv = xnt1[:, 0:8 * TS].rearrange("p (fc t) -> p fc t", t=TS)[
                :, half * 4:(half + 1) * 4, i * 128:(i + 1) * 128]
            K.step("vector", s_te, lambda e, pT=pT, dv=dv: e.tensor_copy(
                dv, pT.rearrange("p (fc t) -> p fc t", t=128)))
        if own:
            K.wait("vector", s_xl[bs], xls[i])
            K.wait("vector", s_xn, s_xn.n)
            if i >= 2:
                K.wait("vector", s_qtp, b_qtp + (i - 1) * 2)
            K.dep("vector", s_qc, lambda e, i=i, bs=bs: e.tensor_scalar(
                qtm, x1b[:, bs * 1024:(bs + 1) * 1024], M(12)[:, i:i + 1], None, ALU.mult))
            K.dep("vector", s_qc, lambda e, i=i: e.tensor_scalar(
                qtm, qtm, M(13)[:, i:i + 1], None, ALU.subtract))
            K.wait("scalar", s_qc, s_qc.n)
            K.step("scalar", s_rc1, lambda e: e.activation(qtm, qtm, AF.Copy, bias=RC))
            K.dep("scalar", s_rc1, lambda e: e.activation(qtm, qtm, AF.Copy, bias=-RC))
            K.wait("vector", s_rc1, s_rc1.n)
            K.dep("vector", s_qc, lambda e: e.tensor_copy(qi3, qtm))
            K.dep("vector", s_qc, lambda e, bs=bs: e.tensor_copy(
                tokb[:, bs * 2048 + 1024: bs * 2048 + 2048], qi3))
            K.wait("tensor", s_qc, s_qc.n)
            for half in range(2):
                pT = pall[:, 1024 + half * 512: 1024 + half * 512 + 256].bitcast(BF16)
                if i * 2 + half >= 2:
                    K.wait("tensor", s_qte, b_qte + i * 2 + half - 1)

                def tq4(e, i=i, bs=bs, half=half, pT=pT):
                    inst = None
                    for j in range(4):
                        fc = half * 4 + j
                        inst = e.transpose(pT[:, j * 128:(j + 1) * 128],
                                           tokb[:, bs * 2048 + 1024 + fc * 128:
                                                bs * 2048 + 1024 + (fc + 1) * 128],
                                           T["i16sb"][:])
                    return inst
                K.step("tensor", s_qtp, tq4)
                K.wait("vector", s_qtp, s_qtp.n)
                if i >= 2 and half == 0:
                    K.wait("vector", s_qst[bs], qsts[i - 2])
                K.step("vector", s_qte, lambda e, pT=pT, bs=bs, half=half: e.tensor_copy(
                    xstg[:, bs * 1024 + half * 512: bs * 1024 + (half + 1) * 512], pT))
            K.wait("gpsimd", s_qte, s_qte.n)
            dstq = T["xqt1"][:, i * 128:(i + 1) * 128].rearrange("(fc p) t -> p fc t", p=128)
            qsts[i] = K.dma("gpsimd", dstq,
                            xstg[:, bs * 1024:(bs + 1) * 1024].rearrange("p (fc t) -> p fc t", t=128),
                            sem=s_qst[bs])
    K.barrier()
    wb = K.gsb("wbig", [128, 8 * 3072], BF16)
    onesr = K.gsb("ones_row", [1, 128], F32)
    smi = K.gsem("wq_misc")
    K.step("vector", smi, lambda e: e.memset(onesr[:], 1.0))
    bkbc = K.gsb("bkbc", [128, DM], F32)
    bvbc = K.gsb("bvbc", [128, DM], F32)
    K.wait("tensor", smi, smi.n)
    bias_bcast(K, T["bk1r"], bkbc, onesr, "bk")
    bias_bcast(K, T["bv1r"], bvbc, onesr, "bv")
    bqp = K.gsb("bqp", [128, 16], F32)
    swld = K.gsem("l1_wld")
    K.dma("sync", bqp[:, 0:8], T["bq1p"], sem=swld)
    K.dma("sync", bqp[:, 8:16], T["bk1p"], sem=swld)
    s_pm = K.gsem("l1_pmm")
    s_pe = K.gsem("l1_pev")
    s_ps = [K.gsem("l1_pst0"), K.gsem("l1_pst1")]
    pstg = K.gsb("l1_xstg", [128, 2048], BF16)
    swldg = K.gsem("l1_wldg")
    for wname, dram, bcol0 in (("wq1t", "q1t", 0), ("wk1t", "k1t", 8)):
        for fc in range(8):
            K.dma("gpsimd", wb[:, fc * 1024:(fc + 1) * 1024],
                  T[wname][fc * 128:(fc + 1) * 128, :], sem=swldg)
        K.barrier()
        K.wait("scalar", swld, swld.n)
        it = 0
        pds = {}
        for fo in range(8):
            for tch in range(4):
                sl = it % 2
                pb = pall[:, 1024 + sl * 512:1024 + (sl + 1) * 512]
                if it >= 2:
                    K.wait("tensor", s_pe, s_pe.n - 1)

                def mm(e, fo=fo, tch=tch, pb=pb):
                    inst = None
                    for fc in range(8):
                        inst = e.matmul(pb, wb[:, fc * 1024 + fo * 128:fc * 1024 + (fo + 1) * 128],
                                        xnt1[:, fc * TS + tch * 512:fc * TS + (tch + 1) * 512],
                                        start=(fc == 0), stop=(fc == 7))
                    return inst
                K.step("tensor", s_pm, mm)
                K.wait("vector", s_pm, s_pm.n)
                if it >= 2:
                    K.wait("vector", s_ps[sl], pds[it - 2])

                def ev(e, pb=pb, sl=sl, fo=fo, bcol0=bcol0):
                    return e.tensor_scalar(pstg[:, sl * 512:(sl + 1) * 512], pb,
                                           bqp[:, bcol0 + fo:bcol0 + fo + 1], None, ALU.add)
                K.step("vector", s_pe, ev)
                K.wait("gpsimd", s_pe, s_pe.n)
                pds[it] = K.dma("gpsimd", T[dram][fo * 128:(fo + 1) * 128, tch * 512:(tch + 1) * 512],
                                pstg[:, sl * 512:(sl + 1) * 512], sem=s_ps[sl])
                it += 1
    s_pv = K.gsem("l1_pv")
    for wname, dram, bbc in (("wk1t", "k1m", bkbc), ("wv1t", "v1m", bvbc)):
        for fc in range(8):
            K.dma("gpsimd", wb[:, fc * 1024:(fc + 1) * 1024],
                  T[wname][fc * 128:(fc + 1) * 128, :], sem=swldg)
        K.barrier()
        it = 0
        pds = {}
        for tc in range(NT):
            for fo2 in range(2):
                sl = it % 2
                pb = pall[:, 2048 + sl * 512:2048 + (sl + 1) * 512]
                if it >= 2:
                    K.wait("tensor", s_pv, s_pv.n - 1)

                def mm(e, tc=tc, fo2=fo2, pb=pb):
                    inst = None
                    for fc in range(8):
                        inst = e.matmul(pb, xnt1[:, fc * TS + tc * 128:fc * TS + (tc + 1) * 128],
                                        wb[:, fc * 1024 + fo2 * 512:fc * 1024 + (fo2 + 1) * 512],
                                        start=(fc == 0), stop=(fc == 7))
                    return inst
                K.step("tensor", s_pm, mm)
                K.wait("vector", s_pm, s_pm.n)
                if it >= 2:
                    K.wait("vector", s_ps[sl], pds[it - 2])

                def ev(e, pb=pb, sl=sl, fo2=fo2, bbc=bbc):
                    return e.tensor_tensor(pstg[:, sl * 512:(sl + 1) * 512], pb,
                                           bbc[:, fo2 * 512:(fo2 + 1) * 512], ALU.add)
                K.step("vector", s_pv, ev)
                K.wait("gpsimd", s_pv, s_pv.n)
                pds[it] = K.dma("gpsimd", T[dram][tc * 128:(tc + 1) * 128, fo2 * 512:(fo2 + 1) * 512],
                                pstg[:, sl * 512:(sl + 1) * 512], sem=s_ps[sl])
                it += 1
    K.barrier()
    kvb = K.gsb("scr16k", [128, 4096], F32)[:].bitcast(BF16)
    s_kl = [K.gsem("l1_kl0"), K.gsem("l1_kl1")]
    s_sm = K.gsem("l1_smm")
    s_sa = K.gsem("l1_sacc")
    rms = K.gsb("recv_sb", [64, 1], F32)
    acc = K.gsb("spre_acc", [64, DM], F32)
    K.dma("sync", rms[:], T["recvmask"], sem=swld)
    sini = K.gsem("l1_sini")
    K.step("vector", sini, lambda e: e.memset(acc[:], 0.0))
    K.wait("tensor", sini, sini.n)
    pview = pall[0:64, :].rearrange("p (b c) -> p b c", c=512)[:, :, 0:64]
    kls = {}
    b_sm, b_sa = s_sm.n, s_sa.n
    for ci in range(16):
        bs = ci % 2
        if ci >= 2:
            K.wait("sync", s_sm, b_sm + 2 * (ci - 1))
        K.dma("sync", kvb[:, bs * 4096:bs * 4096 + 1024],
              T["k1m"][2048 + ci * 128:2048 + (ci + 1) * 128, :], sem=s_kl[bs])
        kls[ci] = K.dma("sync", kvb[:, bs * 4096 + 1024:bs * 4096 + 2048],
                        T["v1m"][2048 + ci * 128:2048 + (ci + 1) * 128, :], sem=s_kl[bs])
        K.wait("tensor", s_kl[bs], kls[ci])
        for r in range(2):
            if ci + r > 0:
                K.wait("tensor", s_sa, b_sa + 2 * ci + r - 1 + 1)

            def smm(e, ci=ci, bs=bs, r=r):
                inst = None
                for hl in range(8):
                    h = r * 8 + hl
                    inst = e.matmul(pall[0:64, hl * 512:hl * 512 + 64],
                                    kvb[:, bs * 4096 + h * 64:bs * 4096 + (h + 1) * 64],
                                    kvb[:, bs * 4096 + 1024 + h * 64:bs * 4096 + 1024 + (h + 1) * 64],
                                    start=True, stop=True)
                return inst
            K.step("tensor", s_sm, smm)
            K.wait("vector", s_sm, s_sm.n)
            av = acc[:, r * 512:(r + 1) * 512].rearrange("p (h c) -> p h c", c=64)
            K.dep("vector", s_sa, lambda e, av=av: e.tensor_tensor(av, av, pview, ALU.add))
    sse = K.gsem("l1_sev")
    K.wait("vector", s_sa, s_sa.n)
    K.wait("vector", swld, swld.n)
    spb = K.gsb("spre_sb", [64, DM], F32)
    K.dep("vector", sse, lambda e: e.tensor_scalar(spb[:], acc[:], rms[:], None, ALU.mult))
    K.wait("gpsimd", sse, sse.n)
    K.dma("gpsimd", T["spre"], spb[:], sem=K.gsem("l1_sst"))
    K.barrier()


def build_l1p():
    nc = bass.Bass("TRN2", target_bir_lowering=False, debug=False)
    ins = ["x1c", "wq1t", "wk1t", "wv1t", "bq1p", "bk1p", "bk1r", "bv1r",
           "recvmask", "ident32", "ident16"]
    T = declare(nc, ["x1c"], "ExternalInput")
    T.update(declare(nc, ins[1:], "ExternalInput", {**IN_SPECS}))
    T.update(declare(nc, ["q1t", "k1t", "k1m", "v1m", "xqt1", "st1", "spre"],
                     "ExternalOutput"))
    K = KB(nc)
    K.pall = nc.alloc_psum_tensor("pall", [128, 4096], F32)
    i32sb = K.gsb("ident32_sb", [128, 128], F32)
    i16sb = K.gsb("ident16_sb", [128, 128], BF16)
    sldi = K.gsem("ident_ld")
    K.dma("sync", i32sb[:], T["ident32"], sem=sldi)
    K.dma("sync", i16sb[:], T["ident16"], sem=sldi)
    T["i32sb"] = i32sb
    T["i16sb"] = i16sb
    K.barrier()
    emit_l1p(K, T)
    K.emit()
    return nc, ins, ["q1t", "k1t", "k1m", "v1m", "xqt1", "st1", "spre"]


def emit_att1(K, T, upto=99):
    nc = K.nc
    pall = K.pall
    k1sb = K.gsb("att1_k1t", [128, 8 * 2048], BF16)
    at1 = K.gsb("attn1T", [128, 8 * 2048], BF16)
    ptt = K.gsb("att1_pt", [128, 512], BF16)
    kv = K.gsb("att1_kv", [128, 2 * 2048], BF16)
    mask2 = K.gsb("mask2", [128, 256], F32)
    sext = K.gsb("sext", [64, DM], F32)
    snap = K.gsb("snap", [64, DM], BF16)
    s_ld = K.gsem("a1_ld")
    s_kl = [K.gsem("a1_kl0"), K.gsem("a1_kl1")]
    s_sc = K.gsem("a1_sc")
    s_mk = K.gsem("a1_mk")
    s_pv = K.gsem("a1_pv")
    s_ev = K.gsem("a1_ev")
    s_st = K.gsem("a1_st")
    s_sa = K.gsem("a1_sa")
    sacc = K.gsb("sacc", [64, DM], F32)
    q1t_sb = K.gsb("att1_q1t", [128, 8 * 2048], BF16)
    q1tB = K.gsb("att1_q1tB", [64, 8 * 2048], BF16)
    for fc in range(8):
        K.dma("sync", q1t_sb[:, fc * 2048:(fc + 1) * 2048],
              T["q1t"][fc * 128:(fc + 1) * 128, :], sem=s_ld)
        K.dma("sync", q1tB[:, fc * 2048:(fc + 1) * 2048],
              T["q1t"][fc * 128 + 64:(fc + 1) * 128, :], sem=s_ld)
        K.dma("sync", k1sb[:, fc * 2048:(fc + 1) * 2048],
              T["k1t"][fc * 128:(fc + 1) * 128, :], sem=s_ld)
    K.dma("sync", mask2[:, 0:128], T["triu"], sem=s_ld)
    K.dma("sync", mask2[:, 128:256], T["triu"], sem=s_ld)
    K.dma("sync", sext[:], T["spre"], sem=s_ld)
    K.wait("vector", s_ld, s_ld.n)
    sini = K.gsem("a1_ini")
    K.step("vector", sini, lambda e: e.tensor_copy(snap[:], sext[:]))
    K.dep("vector", sini, lambda e: e.memset(sacc[:], 0.0))
    K.wait("tensor", sini, sini.n)
    pview = pall[0:64, :].rearrange("p (b c) -> p b c", c=512)[:, :, 0:64]
    b_sc, b_mk, b_pv, b_ev, b_st, b_sa = s_sc.n, s_mk.n, s_pv.n, s_ev.n, s_st.n, s_sa.n
    if upto < 1:
        K.barrier()
        return
    kls = {}
    it = 0
    NCI = 16 if upto >= 2 else 2
    for ci in range(NCI):
        bs = ci % 2
        if ci >= 2 and upto >= 4:
            K.wait("sync", s_st, b_st + 2 * (ci - 1))
        if ci >= 2 and upto >= 3:
            K.wait("sync", s_pv, b_pv + (ci - 1) * 8)
        K.dma("sync", kv[:, bs * 2048:bs * 2048 + 1024],
              T["k1m"][ci * 128:(ci + 1) * 128, :], sem=s_kl[bs])
        kls[ci] = K.dma("sync", kv[:, bs * 2048 + 1024:bs * 2048 + 2048],
                        T["v1m"][ci * 128:(ci + 1) * 128, :], sem=s_kl[bs])
        K.wait("tensor", s_kl[bs], kls[ci])
        if ci >= 1 and upto >= 4:
            K.wait("tensor", s_sa, b_sa + 3 * ci)
        for hp in range(8):
            ss = it % 2
            pSv = pall[:, ss * 1024:(ss + 1) * 1024].rearrange(
                "p (b c) -> p b c", c=512)[:, :, 0:128]
            if it >= 2:
                K.wait("tensor", s_mk, b_mk + it - 1)

            def scm(e, ci=ci, hp=hp, ss=ss):
                e.matmul(pall[:, ss * 1024:ss * 1024 + 128],
                         k1sb[0:64, hp * 2048 + ci * 128:hp * 2048 + (ci + 1) * 128],
                         q1t_sb[0:64, hp * 2048 + ci * 128:hp * 2048 + (ci + 1) * 128],
                         start=True, stop=True, tile_position=(0, 0))
                return e.matmul(pall[:, ss * 1024 + 512:ss * 1024 + 640],
                                k1sb[64:128, hp * 2048 + ci * 128:hp * 2048 + (ci + 1) * 128],
                                q1t_sb[64:128, hp * 2048 + ci * 128:hp * 2048 + (ci + 1) * 128],
                                start=True, stop=True, tile_position=(64, 0))
            K.step("tensor", s_sc, scm)
            K.wait("vector", s_sc, s_sc.n)
            if it >= 2 and upto >= 3:
                K.wait("vector", s_pv, b_pv + it - 1)
            K.step("vector", s_mk, lambda e, pSv=pSv, ss=ss: e.tensor_tensor(
                ptt[:, ss * 256:(ss + 1) * 256].rearrange("p (b c) -> p b c", c=128),
                pSv, mask2[:].rearrange("p (b c) -> p b c", c=128), ALU.mult))
            poA = pall[0:64, 2048 + ss * 1024:2048 + ss * 1024 + 128]
            poB = pall[0:64, 2560 + ss * 1024:2560 + ss * 1024 + 128]
            if upto < 3:
                it += 1
                continue
            K.wait("tensor", s_mk, s_mk.n)
            if it >= 2:
                K.wait("tensor", s_ev, b_ev + it - 1)

            def pvm(e, ci=ci, hp=hp, ss=ss, bs=bs, poA=poA, poB=poB):
                hA, hB = 2 * hp, 2 * hp + 1
                e.matmul(poA, kv[:, bs * 2048 + 1024 + hA * 64:bs * 2048 + 1024 + hA * 64 + 64],
                         ptt[:, ss * 256:ss * 256 + 128], start=True, stop=False)
                e.matmul(poA, snap[:, hA * 64:(hA + 1) * 64],
                         q1t_sb[0:64, hp * 2048 + ci * 128:hp * 2048 + (ci + 1) * 128],
                         start=False, stop=True)
                e.matmul(poB, kv[:, bs * 2048 + 1024 + hB * 64:bs * 2048 + 1024 + hB * 64 + 64],
                         ptt[:, ss * 256 + 128:ss * 256 + 256], start=True, stop=False)
                return e.matmul(poB, snap[:, hB * 64:(hB + 1) * 64],
                                q1tB[:, hp * 2048 + ci * 128:hp * 2048 + (ci + 1) * 128],
                                start=False, stop=True)
            K.step("tensor", s_pv, pvm)
            K.wait("vector", s_pv, s_pv.n)

            def ev2(e, ci=ci, hp=hp, poA=poA, poB=poB):
                e.tensor_copy(at1[0:64, hp * 2048 + ci * 128:hp * 2048 + (ci + 1) * 128], poA)
                return e.tensor_copy(at1[64:128, hp * 2048 + ci * 128:hp * 2048 + (ci + 1) * 128], poB)
            K.step("vector", s_ev, ev2)
            it += 1
        if upto < 4:
            continue
        K.wait("tensor", s_ev, s_ev.n)
        K.wait("tensor", s_mk, s_mk.n)
        for r in range(2):
            if ci + r > 0:
                K.wait("tensor", s_sa, b_sa + 3 * ci + (1 if r == 1 else -1))

            def stm(e, ci=ci, bs=bs, r=r):
                inst = None
                for hl in range(8):
                    h = r * 8 + hl
                    inst = e.matmul(pall[0:64, hl * 512:hl * 512 + 64],
                                    kv[:, bs * 2048 + h * 64:bs * 2048 + (h + 1) * 64],
                                    kv[:, bs * 2048 + 1024 + h * 64:bs * 2048 + 1024 + (h + 1) * 64],
                                    start=True, stop=True)
                return inst
            K.step("tensor", s_st, stm)
            K.wait("vector", s_st, s_st.n)
            av = sacc[:, r * 512:(r + 1) * 512].rearrange("p (h c) -> p h c", c=64)
            K.dep("vector", s_sa, lambda e, av=av: e.tensor_tensor(av, av, pview, ALU.add))
        K.wait("vector", s_pv, s_pv.n)
        K.dep("vector", s_sa, lambda e: e.tensor_tensor(snap[:], sacc[:], sext[:], ALU.add))
    K.barrier()
    if upto < 5:
        return
    wb = K.gsb("wo_sb", [128, 8 * 1024], BF16)
    onesr = K.gsb("ones_row", [1, 128], F32)
    smi = K.gsem("wq_misc")
    K.step("vector", smi, lambda e: e.memset(onesr[:], 1.0))
    K.wait("tensor", smi, smi.n)
    bobc = K.gsb("bobc", [128, DM], F32)
    bias_bcast(K, T["bo1r"], bobc, onesr, "bo")
    swld = K.gsem("l1_wldg")
    for fc in range(8):
        K.dma("gpsimd", wb[:, fc * 1024:(fc + 1) * 1024],
              T["wo1t"][fc * 128:(fc + 1) * 128, :], sem=swld)
    K.barrier()
    s_pm = K.gsem("op_mm")
    s_oe = K.gsem("op_ev")
    s_os = [K.gsem("op_st0"), K.gsem("op_st1")]
    ost = K.gsb("scr8k", [128, 2048], F32)[:]
    ods = {}
    K.wait("tensor", K.gsem("bb_ev"), K.gsem("bb_ev").n)
    for tc in range(16):
        sl = tc % 2
        pb2 = pall[:, 2048 + sl * 1024:2048 + (sl + 1) * 1024]
        if tc >= 2:
            K.wait("tensor", s_oe, s_oe.n - 1)

        def mm(e, tc=tc, pb2=pb2):
            inst = None
            for fc in range(8):
                for eh in range(2):
                    inst = e.matmul(pb2[:, eh * 512:(eh + 1) * 512],
                                    at1[:, fc * 2048 + tc * 128:fc * 2048 + (tc + 1) * 128],
                                    wb[:, fc * 1024 + eh * 512:fc * 1024 + (eh + 1) * 512],
                                    start=(fc == 0), stop=(fc == 7))
            return inst
        K.step("tensor", s_pm, mm)
        K.wait("vector", s_pm, s_pm.n)
        if tc >= 2:
            K.wait("vector", s_os[sl], ods[tc - 2])
        K.dep("vector", s_oe, lambda e, pb2=pb2, sl=sl: e.tensor_tensor(
            ost[:, sl * 1024:(sl + 1) * 1024], pb2, bobc[:], ALU.add))
        K.wait("gpsimd", s_oe, s_oe.n)
        ods[tc] = K.dma("gpsimd", T["attn1o"][tc * 128:(tc + 1) * 128, :],
                        ost[:, sl * 1024:(sl + 1) * 1024], sem=s_os[sl])
    K.barrier()


def build_att1(upto=99):
    nc = bass.Bass("TRN2", target_bir_lowering=False, debug=False)
    ins = ["q1t", "k1t", "k1m", "v1m", "spre", "wo1t", "bo1r", "triu"]
    T = declare(nc, ["q1t", "k1t", "k1m", "v1m", "spre"], "ExternalInput")
    T.update(declare(nc, ["wo1t", "bo1r", "triu"], "ExternalInput", {**IN_SPECS}))
    T.update(declare(nc, ["attn1o"], "ExternalOutput"))
    K = KB(nc)
    K.pall = nc.alloc_psum_tensor("pall", [128, 4096], F32)
    emit_att1(K, T, upto)
    K.emit()
    return nc, ins, ["attn1o"]


def emit_ffn1(K, T, upto=99):
    nc = K.nc
    pall = K.pall
    w1q = K.gsb("w1q", [128, 8 * DF], BF16)
    w2qd = nc.dram_tensor("w2qd1", [DF, 1024], BF16, kind="Internal").ap()
    scl1 = emit_wquant(K, T["w1_1t"], 8, DF, w1q, "w11")
    K.barrier()
    scl2 = emit_wquant(K, T["w2_1t"], 32, 1024, None, "w21", dram_out=w2qd)
    K.barrier()
    s2row = nc.dram_tensor("s2row1", [1, TH], F32, kind="Internal").ap()
    onesr = K.gsb("ones_row", [1, 128], F32)
    smi = K.gsem("wq_misc")
    K.step("vector", smi, lambda e: e.memset(onesr[:], 1.0))
    K.wait("tensor", smi, smi.n)
    b2bc = K.gsb("b2bc", [128, DM], F32)
    bias_bcast(K, T["b2r"], b2bc, onesr, "b2")
    xq1sb = K.gsb("att1_q1t", [128, 8 * 2048], BF16)
    b1sb = K.gsb("b1sb", [128, 32], F32)
    swld = K.gsem("l1_wld")
    for fc in range(8):
        K.dma("sync", xq1sb[:, fc * 2048:(fc + 1) * 2048],
              T["xqt1"][fc * 128:(fc + 1) * 128, :], sem=swld)
    K.dma("sync", b1sb[:], T["b1p"], sem=swld)
    K.barrier()
    dsb = K.gsb("scr4k", [128, 1024], F32)[:, 0:512]
    hst = K.gsb("scr8k", [128, 2048], F32)[:, 0:1024]
    gfm = K.gsb("scr32k", [128, 2 * DF], F32)[:].bitcast(BF16)
    gmax = K.gsb("gmax", [128, 24], F32)
    sqs = K.gsb("ff1_sq", [128, 2 * 512], F32)
    s2b = K.gsb("scr4k", [128, 1024], F32)[:, 512:1024]
    qtsl = K.gsb("qi_arena", [128, DF], I32)[:, 1024:2048].bitcast(F32)
    qisl = K.gsb("qi_arena", [128, DF], I32)[:, 2048:3072]
    w2s = K.gsb("w2s", [128, 2048], BF16)
    gtok = K.gsb("f1_gtok", [128, 2 * 512], BF16)
    mxb = K.gsb("mxbuf", [128, 128], F32)
    sqb2 = K.gsb("f1_sqb2", [128, 128], F32)
    rst = K.gsb("scr16k", [128, 4096], F32)[:]
    ost2 = K.gsb("f1_ost", [128, 2048], F32)
    s_dl = K.gsem("f1_dl")
    s_hm = K.gsem("f1_hmm")
    s_he = K.gsem("f1_hev")
    s_hd = K.gsem("f1_hdv")
    s_sl = K.gsem("f1_gelu")
    s_tp = K.gsem("f1_tp")
    s_mx = K.gsem("f1_mx")
    s_qq = K.gsem("f1_qq")
    s_qa = K.gsem("f1_qa")
    s_qap = K.gsem("f1_qap")
    s_rc = K.gsem("f1_rc")
    s_qi = K.gsem("f1_qi")
    s_qb = K.gsem("f1_qb")
    s_f2 = K.gsem("f1_f2")
    s_oe = K.gsem("f1_oe")
    s_ov = K.gsem("f1_ov")
    s_sc2 = K.gsem("f1_s2")
    s_s2l = K.gsem("f1_s2l")
    s_w2l = [K.gsem("f1_w2l0"), K.gsem("f1_w2l1")]
    s_os = [K.gsem("f1_os0"), K.gsem("f1_os1")]
    s_rl = [K.gsem("f1_rl0"), K.gsem("f1_rl1")]
    ods = {}
    b_oe0, b_ov0 = s_oe.n, s_ov.n
    b_f2_prev = 0
    K.wait("tensor", K.gsem("bb_ev"), K.gsem("bb_ev").n)
    if upto < 2:
        K.barrier()
        return
    NSI = 4 if upto >= 5 else 1
    for si in range(NSI):
        t0 = si * 512
        pf2 = b_f2_prev
        b_f2_prev = s_f2.n
        if si > 0:
            K.wait("tensor", s_oe, b_oe0 + si * 4)
        dv = K.dma("scalar", dsb, bcast_row(T["st1"][5:6, :], t0, 512), sem=s_dl)
        b_sl = s_sl.n
        b_he = s_he.n
        b_hd = s_hd.n
        for cc in range(32):
            ps = pall[:, (cc % 2) * 512:(cc % 2) * 512 + 512]
            hsl = hst[:, (cc % 2) * 512:(cc % 2) * 512 + 512]
            if cc >= 2:
                K.wait("tensor", s_he, b_he + cc - 1)

            def mm(e, cc=cc, t0=t0, ps=ps):
                inst = None
                for fc in range(8):
                    inst = e.matmul(ps, w1q[:, fc * DF + cc * 128:fc * DF + cc * 128 + 128],
                                    xq1sb[:, fc * 2048 + t0:fc * 2048 + t0 + 512],
                                    start=(fc == 0), stop=(fc == 7))
                return inst
            K.step("tensor", s_hm, mm)
            K.wait("scalar", s_hm, s_hm.n)
            if cc >= 2:
                K.wait("scalar", s_sl, b_sl + cc - 1)

            def ev(e, ps=ps, hsl=hsl):
                return e.activation(hsl, ps, AF.Copy, scale=scl1[:, 1:2])
            K.step("scalar", s_he, ev)
            K.wait("vector", s_he, s_he.n)
            if cc == 0:
                K.wait("vector", s_dl, dv)
            K.step("vector", s_hd, lambda e, hsl=hsl: e.tensor_tensor(hsl, hsl, dsb, ALU.mult))
            K.dep("vector", s_hd, lambda e, cc=cc, hsl=hsl: e.tensor_scalar(
                hsl, hsl, b1sb[:, cc:cc + 1], None, ALU.add))
            K.wait("scalar", s_hd, b_hd + 2 * cc + 2)
            if si > 0:
                K.wait("scalar", s_f2, pf2 + cc + 1)

            def gl(e, cc=cc, hsl=hsl):
                return e.activation(gfm[:, cc * 512:(cc + 1) * 512], hsl, AF.Gelu)
            K.step("scalar", s_sl, gl)
        if upto < 3:
            continue
        b_tp, b_mx = s_tp.n, s_mx.n
        s_pe2 = K.gsem("f1_pev")
        s_sq2 = K.gsem("f1_sq2")
        s_ssm = K.gsem("f1_ssm")
        b_pe2, b_sq2, b_ssm = s_pe2.n, s_sq2.n, s_ssm.n
        for cc in range(32):
            tb = cc % 2
            pT = pall[:, 2048 + tb * 512:2048 + tb * 512 + 256].bitcast(BF16)
            gsl = gtok[:, tb * 512:tb * 512 + 512]
            ssl = sqs[:, tb * 512:tb * 512 + 512]
            K.wait("tensor", s_sl, b_sl + cc + 1)
            if cc >= 2:
                K.wait("tensor", s_pe2, b_pe2 + cc - 1)

            def tp(e, cc=cc, pT=pT):
                inst = None
                for j in range(4):
                    inst = e.transpose(pT[:, j * 128:(j + 1) * 128],
                                       gfm[:, cc * 512 + j * 128:cc * 512 + (j + 1) * 128],
                                       T["i16sb"][:])
                return inst
            K.step("tensor", s_tp, tp)
            K.wait("vector", s_tp, s_tp.n)
            if cc >= 2:
                K.wait("vector", s_mx, b_mx + cc - 1)
                K.wait("vector", s_sq2, b_sq2 + cc - 1)
            elif si > 0:
                K.wait("vector", s_sq2, b_sq2)
            K.step("vector", s_pe2, lambda e, pT=pT, gsl=gsl: e.tensor_copy(gsl, pT[:, 0:512]))
            K.wait("vector", s_pe2, s_pe2.n)

            def mx(e, cc=cc, gsl=gsl):
                inst = None
                for j in range(4):
                    inst = e.tensor_reduce(mxb[:, j * 32 + cc:j * 32 + cc + 1],
                                           gsl[:, j * 128:(j + 1) * 128],
                                           AX.X, ALU.max, apply_absolute_value=True)
                return inst
            K.step("vector", s_mx, mx)
            K.wait("scalar", s_pe2, s_pe2.n)
            if cc >= 2:
                K.wait("scalar", s_ssm, b_ssm + cc - 1)
            elif si > 0:
                K.wait("scalar", s_ssm, b_ssm)
            K.step("scalar", s_sq2, lambda e, gsl=gsl, ssl=ssl: e.activation(ssl, gsl, AF.Square))
            K.wait("vector", s_sq2, s_sq2.n)

            def sms(e, cc=cc, ssl=ssl):
                inst = None
                for j in range(4):
                    inst = e.reduce_sum(sqb2[:, j * 32 + cc:j * 32 + cc + 1],
                                        ssl[:, j * 128:(j + 1) * 128], axis=AX.X)
                return inst
            K.step("vector", s_ssm, sms)
        K.wait("vector", s_mx, s_mx.n)
        K.wait("vector", s_ssm, s_ssm.n)

        def fold1(e):
            inst = None
            for j in range(4):
                e.tensor_reduce(gmax[:, j:j + 1], mxb[:, j * 32:(j + 1) * 32], AX.X, ALU.max)
                inst = e.reduce_sum(gmax[:, 8 + j:9 + j], sqb2[:, j * 32:(j + 1) * 32], axis=AX.X)
            return inst
        K.step("vector", s_mx, fold1)
        K.dep("vector", s_mx, lambda e: e.tensor_scalar(gmax[:, 16:20], gmax[:, 8:12], 1.0 / DF, None, ALU.mult))
        K.dep("vector", s_mx, lambda e: e.tensor_scalar(gmax[:, 16:20], gmax[:, 16:20], EPS, None, ALU.add))
        s_ac = K.gsem("f1_ac")
        K.wait("scalar", s_mx, s_mx.n)
        K.dep("scalar", s_ac, lambda e: e.activation(gmax[:, 16:20], gmax[:, 16:20], AF.Ln))
        K.dep("scalar", s_ac, lambda e: e.activation(gmax[:, 16:20], gmax[:, 16:20], AF.Exp, scale=-0.5))
        K.wait("vector", s_ac, s_ac.n)
        K.dep("vector", s_mx, lambda e: e.tensor_tensor(gmax[:, 4:8], gmax[:, 0:4], gmax[:, 16:20], ALU.mult))
        K.dep("vector", s_mx, lambda e: e.tensor_scalar(gmax[:, 4:8], gmax[:, 4:8], 1e-5, None, ALU.max))
        K.dep("vector", s_mx, lambda e: e.tensor_scalar(gmax[:, 20:24], gmax[:, 4:8], 1.0 / 127.0, None, ALU.mult))
        for j in range(4):
            K.dep("vector", s_mx, lambda e, j=j: e.tensor_tensor(
                gmax[:, 20 + j:21 + j], gmax[:, 20 + j:21 + j], scl2[:, 1:2], ALU.mult))
        K.dep("vector", s_mx, lambda e: e.reciprocal(gmax[:, 4:8], gmax[:, 4:8]))
        K.dep("vector", s_mx, lambda e: e.tensor_scalar(gmax[:, 4:8], gmax[:, 4:8], 127.0, None, ALU.mult))
        K.dep("vector", s_mx, lambda e: e.tensor_tensor(gmax[:, 4:8], gmax[:, 4:8], gmax[:, 16:20], ALU.mult))
        K.wait("tensor", s_mx, s_mx.n)
        K.step("tensor", s_sc2, lambda e: e.transpose(pall[0:4, 3072:3200], gmax[:, 4:8], T["i32sb"][:]))
        K.wait("vector", s_sc2, s_sc2.n)
        K.dep("vector", s_sc2, lambda e: e.tensor_copy(s2b[0:4, 0:128], pall[0:4, 3072:3200]))
        K.wait("scalar", s_sc2, s_sc2.n)
        K.dma("scalar", s2row[0:1, t0:t0 + 512].rearrange("o (j q) -> o j q", q=128).squeeze(0),
              s2b[0:4, 0:128], sem=s_s2l)
        K.wait("scalar", s_s2l, s_s2l.n)
        K.wait("scalar", s_qa, s_qa.n)
        K.wait("scalar", s_qap, s_qap.n)
        vv = K.dma("scalar", s2b[:, 0:512], bcast_row(s2row, t0, 512), sem=s_s2l)
        K.wait("vector", s_s2l, vv)
        if upto < 4:
            continue
        b_qa, b_qap, b_rc, b_qi, b_qb = s_qa.n, s_qap.n, s_rc.n, s_qi.n, s_qb.n
        K.wait("gpsimd", s_s2l, vv)
        for cc in range(32):
            pool_q = cc % 4 in (1, 3)
            eq = "gpsimd" if pool_q else "vector"
            sq_ = s_qap if pool_q else s_qa
            qts = qtsl[:, (cc % 2) * 512:(cc % 2) * 512 + 512]
            qis = qisl[:, (cc % 2) * 512:(cc % 2) * 512 + 512]
            K.wait(eq, s_sl, b_sl + cc + 1)
            if cc >= 2 or si > 0:
                K.wait(eq, s_qi, b_qi + cc - 1)

            def qa(e, cc=cc, qts=qts):
                return e.tensor_tensor(qts, gfm[:, cc * 512:(cc + 1) * 512], s2b[:, 0:512], ALU.mult)
            K.step(eq, sq_, qa)
            K.wait("scalar", sq_, sq_.n)
            K.step("scalar", s_rc, lambda e, qts=qts: e.activation(qts, qts, AF.Copy, bias=RC))
            K.dep("scalar", s_rc, lambda e, qts=qts: e.activation(qts, qts, AF.Copy, bias=-RC))
            K.wait("vector", s_rc, b_rc + 2 * cc + 2)
            if cc >= 2 or si > 0:
                K.wait("vector", s_qb, b_qb + cc - 1)
            K.step("vector", s_qi, lambda e, qts=qts, qis=qis: e.tensor_copy(qis, qts))
            K.wait("vector", s_qi, s_qi.n)
            K.step("vector", s_qb,
                   lambda e, cc=cc, qis=qis: e.tensor_copy(gfm[:, cc * 512:(cc + 1) * 512], qis))
        if si >= 1:
            K.wait("tensor", s_oe, b_oe0 + si * 4)
        b_w2 = s_f2.n
        for cc in range(32):
            b2 = cc % 2
            if cc >= 2:
                K.wait("sync" if b2 == 0 else "scalar", s_f2, b_w2 + cc - 1)
            lw2 = K.dma("sync" if b2 == 0 else "scalar", w2s[:, b2 * 1024:(b2 + 1) * 1024],
                        w2qd[cc * 128:(cc + 1) * 128, :], sem=s_w2l[b2])
            K.wait("tensor", s_w2l[b2], lw2)
            K.wait("tensor", s_qb, b_qb + cc + 1)

            def f2(e, cc=cc, b2=b2):
                inst = None
                for tt_ in range(4):
                    for eh in range(2):
                        inst = e.matmul(pall[:, tt_ * 1024 + eh * 512:tt_ * 1024 + eh * 512 + 512],
                                        gfm[:, cc * 512 + tt_ * 128:cc * 512 + tt_ * 128 + 128],
                                        w2s[:, b2 * 1024 + eh * 512:b2 * 1024 + eh * 512 + 512],
                                        start=(cc == 0), stop=(cc == 31))
                return inst
            K.step("tensor", s_f2, f2)
        for tt_ in range(4):
            po = pall[:, tt_ * 1024:(tt_ + 1) * 1024]
            trow = t0 + tt_ * 128
            so = tt_ % 2
            if si * 4 + tt_ >= 2:
                K.wait("sync", s_ov, b_ov0 + 3 * (si * 4 + tt_) - 3)
            lr = K.dma("sync", rst[:, so * 2048:so * 2048 + 1024],
                       T["x1c"][trow:trow + 128, :], sem=s_rl[so])
            lr2 = K.dma("sync", rst[:, so * 2048 + 1024:so * 2048 + 2048],
                        T["attn1o"][trow:trow + 128, :], sem=s_rl[so])
            K.wait("scalar", s_f2, s_f2.n)
            K.wait("scalar", s_mx, s_mx.n)
            if si * 4 + tt_ >= 2:
                K.wait("scalar", s_os[so], ods[si * 4 + tt_ - 2])

            osl = ost2[:, so * 1024:(so + 1) * 1024]

            def oev(e, po=po, tt_=tt_, osl=osl):
                return e.activation(osl, po, AF.Copy, scale=gmax[:, 20 + tt_:21 + tt_])
            K.step("scalar", s_oe, oev)
            K.wait("vector", s_oe, s_oe.n)
            K.wait("vector", s_rl[so], lr2)
            K.dep("vector", s_ov, lambda e, osl=osl: e.tensor_tensor(osl, osl, b2bc[:], ALU.add))
            K.dep("vector", s_ov, lambda e, so=so, osl=osl: e.tensor_tensor(
                osl, osl, rst[:, so * 2048:so * 2048 + 1024], ALU.add))
            K.dep("vector", s_ov, lambda e, so=so, osl=osl: e.tensor_tensor(
                osl, osl, rst[:, so * 2048 + 1024:so * 2048 + 2048], ALU.add))
            K.wait("gpsimd", s_ov, s_ov.n)
            ods[si * 4 + tt_] = K.dma("gpsimd", T["y_own"][trow:trow + 128, :], osl,
                                      sem=s_os[so])
    K.barrier()


def build_ffn1(upto=99):
    nc = bass.Bass("TRN2", target_bir_lowering=False, debug=False)
    ins = ["xqt1", "st1", "w1_1t", "b1p", "w2_1t", "b2r", "x1c", "attn1o",
           "ident32", "ident16"]
    T = declare(nc, ["xqt1", "st1", "attn1o", "x1c"], "ExternalInput")
    T.update(declare(nc, ["w1_1t", "b1p", "w2_1t", "b2r", "ident32", "ident16"],
                     "ExternalInput", {**IN_SPECS}))
    T.update(declare(nc, ["y_own"], "ExternalOutput"))
    K = KB(nc)
    K.pall = nc.alloc_psum_tensor("pall", [128, 4096], F32)
    i32sb = K.gsb("ident32_sb", [128, 128], F32)
    i16sb = K.gsb("ident16_sb", [128, 128], BF16)
    sldi = K.gsem("ident_ld")
    K.dma("sync", i32sb[:], T["ident32"], sem=sldi)
    K.dma("sync", i16sb[:], T["ident16"], sem=sldi)
    T["i32sb"] = i32sb
    T["i16sb"] = i16sb
    K.barrier()
    emit_ffn1(K, T, upto)
    K.emit()
    return nc, ins, ["y_own"]


def build_p0(upto=99):
    nc = bass.Bass("TRN2", target_bir_lowering=False, debug=False)
    ins = ["x_seq", "x_seq_t", "x_cat", "x_cat_t", "w_qkv", "ident32"]
    T = declare(nc, ins, "ExternalInput")
    T.update(declare(nc, ["k0t", "v0", "q0t", "xqt0", "st0"], "ExternalOutput"))
    K = KB(nc)
    K.pall = nc.alloc_psum_tensor("pall", [128, 4096], F32)
    emit_p0(K, T, upto)
    K.emit()
    return nc, ins, ["k0t", "v0", "q0t", "xqt0", "st0"]


def build_fused():
    nc = bass.Bass("TRN2", target_bir_lowering=False, debug=False)
    T = declare(nc, list(IN_SPECS.keys()), "ExternalInput")
    scr = [k for k in SCR_SPECS if k != "y_own"]
    T.update(declare(nc, scr, "Internal"))
    T.update(declare(nc, ["y_own"], "ExternalOutput"))
    K = KB(nc)
    K.pall = nc.alloc_psum_tensor("pall", [128, 4096], F32)
    i32sb = K.sb("ident32_sb", [128, 128], F32)
    i16sb = K.sb("ident16_sb", [128, 128], BF16)
    sldi = K.gsem("ident_ld")
    K.dma("sync", i32sb[:], T["ident32"], sem=sldi)
    K.dma("sync", i16sb[:], T["ident16"], sem=sldi)
    T["i32sb"] = i32sb
    T["i16sb"] = i16sb
    scl1_g = K.sb("scl_w1_g", [128, 2], F32)
    scl2_g = K.sb("scl_w2_g", [128, 2], F32)
    w1qd_g = nc.dram_tensor("w1qd", [1024, DF], BF16, kind="Internal").ap()
    w2qd_g = nc.dram_tensor("w2qd", [DF, 1024], BF16, kind="Internal").ap()
    T["scl_w1"] = scl1_g[:]
    T["scl_w2"] = scl2_g[:]
    T["w1qd_g"] = w1qd_g
    T["w2qd_g"] = w2qd_g
    K.barrier()
    for emitter in (emit_p0, emit_att0, emit_ffn0, emit_l1p, emit_att1, emit_ffn1):
        K.begin_phase()
        if emitter is emit_att0:
            sl1, sq1, st1_ = wquant_pf_slices(K, T, T["w1_0t"], 8, DF, T["scl_w1"], "pw1",
                                              dram_out=w1qd_g)
            sl2, _, _ = wquant_pf_slices(K, T, T["w2_0t"], 32, 1024, T["scl_w2"], "pw2",
                                         dram_out=w2qd_g, pre_wait=sq1,
                                         pre_stores=(st1_[1], st1_[2]))
            slices = sl1 + sl2
            idx = [0]

            def hook(it):
                if idx[0] < len(slices):
                    slices[idx[0]]()
                    idx[0] += 1
            emit_att0(K, T, hook)
            while idx[0] < len(slices):
                slices[idx[0]]()
                idx[0] += 1
        else:
            emitter(K, T)
        K.end_phase()
    K.emit()
    ins = list(IN_SPECS.keys())
    return nc, ins, ["y_own"]



def _run(build, cores, extra):
    nc, ins, outs = build()
    in_maps = []
    for c, ex in zip(cores, extra):
        d = {}
        for k in ins:
            v = ex.get(k) if k in ex else c.get(k)
            d[k] = v
        in_maps.append(d)
    res = run_bass_kernel_spmd(nc, in_maps, core_ids=list(range(8)))
    return res.results


FUSED = True


def kernel(**inputs):
    cores = prepare_core_inputs(inputs)
    if FUSED:
        r = _run(build_fused, cores, [{} for _ in range(8)])
        out = np.zeros((4, 4096, 1024), dtype=np.float32)
        for c in range(8):
            b, h = c // 2, c % 2
            out[b, h * TH:(h + 1) * TH] = np.asarray(r[c]["y_own"], dtype=np.float32)
        return out
    none8 = [{} for _ in range(8)]
    r1 = _run(build_p0, cores, none8)
    r2 = _run(build_att0, cores, r1)
    r3 = _run(build_ffn0, cores, [{**a, **b} for a, b in zip(r1, r2)])
    r4 = _run(build_l1p, cores, r3)
    r5 = _run(build_att1, cores, r4)
    r6 = _run(build_ffn1, cores, [{**a, **b, **c} for a, b, c in zip(r3, r4, r5)])
    out = np.zeros((4, 4096, 1024), dtype=np.float32)
    for c in range(8):
        b, h = c // 2, c % 2
        out[b, h * TH:(h + 1) * TH] = np.asarray(r6[c]["y_own"], dtype=np.float32)
    return out


# version 18
# speedup vs baseline: 1.0252x; 1.0252x over previous
import math
import numpy as np
import ml_dtypes
from contextlib import ExitStack

import concourse.bass as bass
import concourse.mybir as mybir
from concourse.bass_utils import run_bass_kernel_spmd

F32 = mybir.dt.float32
BF16 = mybir.dt.bfloat16
I32 = mybir.dt.int32
AF = mybir.ActivationFunctionType
ALU = mybir.AluOpType
AX = mybir.AxisListType
BFNP = ml_dtypes.bfloat16

DM = 1024
DF = 4096
NH = 16
DH = 64
TS = 4096
TH = 2048
TCAT = 4224
NT_SEQ = TS // 128
NT_CAT = TCAT // 128
EPS = 1e-5
ATTN_SCALE = float(DH) ** -0.5
RC = 12582912.0

ENGS = ("sync", "scalar", "vector", "tensor", "gpsimd")


class Sem:
    __slots__ = ("h", "n")

    def __init__(self, h):
        self.h = h
        self.n = 0

    def inc(self, k):
        self.n += k
        return self.n


SW_SEMS = frozenset((
    "att_out0", "att_out1", "f0_hst0", "f0_hst1", "f0c_os0", "f0c_os1",
    "f1_os0", "f1_os1", "gfm_st0", "gfm_st1", "l1_pst0", "l1_pst1",
    "l1_qst0", "l1_qst1", "l1_sst", "l1_wldg", "op_st0", "op_st1",
    "pf_st0_pw1", "pf_st0_pw2", "pf_st1_pw1", "pf_st1_pw2",
    "pf_st2_pw1", "pf_st2_pw2", "wld",
    "wq_st0", "wq_st1", "xnt_st0", "xnt_st1",
))


class KB:

    def __init__(self, nc):
        self.nc = nc
        self.ops = {e: [] for e in ENGS}
        self.es = ExitStack()
        self._semi = 0
        self.dsem = {e: self.sem() for e in ("sync", "scalar", "vector", "gpsimd")}
        self.bar = self.sem()
        self._scratch = {}
        self._sems = {}
        self._dma_sems = {}
        self._phase_es = None
        self._scratch_phase = {}
        self._sems_phase = {}
        self._sem_pool = []
        self._sem_pool_sw = []
        self._phase_pi = {"sw": 0, "hw": 0}
        self._sem_pool_used = 0

    def sem(self):
        self._semi += 1
        return Sem(self.es.enter_context(self.nc.semaphore(f"s{self._semi}")))

    def sb(self, name, shape, dtype):
        if self._phase_es is not None:
            return self._phase_es.enter_context(
                self.nc.sbuf_tensor(f"{name}_ph{self._phase_i}", shape, dtype))
        return self.es.enter_context(self.nc.sbuf_tensor(name, shape, dtype))

    def gsb(self, name, shape, dtype):
        key = (name, dtype)
        tab = self._scratch_phase if self._phase_es is not None else self._scratch
        h = tab.get(key)
        if h is None:
            h = self.sb(name, shape, dtype)
            tab[key] = h
        else:
            assert list(h.shape)[1] >= shape[1], (name, h.shape, shape)
        return h

    def begin_phase(self):
        assert self._phase_es is None
        self._phase_i = getattr(self, "_phase_i", 0) + 1
        self._phase_es = ExitStack()
        self._scratch_phase = {}
        self._sems_phase = {}
        self._phase_pi = {"sw": 0, "hw": 0}

    def end_phase(self):
        self._phase_es.close()
        self._phase_es = None
        self._scratch_phase = {}
        self._sems_phase = {}

    def on(self, eng, fn):
        self.ops[eng].append(fn)

    def dma(self, eng, out, in_, sem=None, **kw):
        if sem is not None and self._sems_phase:
            for k, v in self._sems_phase.items():
                if v is sem:
                    assert (k in SW_SEMS) == (eng == "gpsimd"), \
                        (k, eng, "sem class mismatch: add/remove from SW_SEMS")
                    break
        s = sem or self.dsem[eng]
        self._dma_sems[s] = eng
        v = s.inc(16)
        self.on(eng, lambda e, o=out, i=in_, sh=s.h, kw=kw: e.dma_start(out=o, in_=i, **kw).then_inc(sh, 16))
        return v

    def wait(self, eng, sem, v):
        self.on(eng, lambda e, sh=sem.h, vv=v: e.wait_ge(sh, vv))

    def step(self, eng, sem, fn):
        v = sem.inc(1)

        def run(e, fn=fn, sh=sem.h):
            inst = fn(e)
            inst.then_inc(sh, 1)
        self.on(eng, run)
        return v

    def dep(self, eng, sem, fn):
        self.wait(eng, sem, sem.n)
        return self.step(eng, sem, fn)

    def gsem(self, name):
        sw = name in SW_SEMS
        if self._phase_es is not None:
            s = self._sems_phase.get(name)
            if s is None:
                cls = "sw" if sw else "hw"
                pool = self._sem_pool_sw if sw else self._sem_pool
                i = self._phase_pi[cls]
                self._phase_pi[cls] = i + 1
                while len(pool) <= i:
                    pool.append(self.sem())
                s = pool[i]
                self._sems_phase[name] = s
            return s
        s = self._sems.get(name)
        if s is None:
            s = self.sem()
            self._sems[name] = s
        return s

    def barrier(self):
        for s, e in list(self._dma_sems.items()):
            self.wait(e, s, s.n)
        for e in ENGS:
            self.on(e, lambda eng, sh=self.bar.h: eng.nop().then_inc(sh, 1))
        self.bar.n += 5
        tgt = self.bar.n
        for e in ENGS:
            self.wait(e, self.bar, tgt)

    def emit(self):
        with self.nc.Block() as blk:
            for e in ENGS:
                fns = self.ops[e]
                if fns:
                    getattr(blk, e)(lambda eng, fns=fns: [f(eng) for f in fns])



def prepare_core_inputs(inp):
    x = np.ascontiguousarray(inp["L0_attn_x"], dtype=np.float32)
    per_core = []
    triu = np.triu(np.ones((128, 128), dtype=np.float32))
    ident16 = np.eye(128, dtype=np.float32).astype(BFNP)
    shared = {
        "w_qkv": np.ascontiguousarray(inp["L0_attn_w_qkv"], dtype=np.float32),
        "w1_0t": np.ascontiguousarray(inp["L0_ffn_w1"].T, dtype=np.float32),
        "convw": np.ascontiguousarray(
            inp["L0_ffn_conv_w"][:, 0, :].reshape(32, 128, 3).transpose(1, 0, 2).reshape(128, 96),
            dtype=np.float32),
        "convb": np.ascontiguousarray(
            inp["L0_ffn_conv_b"].reshape(32, 128).T, dtype=np.float32),
        "w2_0t": np.ascontiguousarray(inp["L0_ffn_w2"].T, dtype=np.float32),
        "wq1t": np.ascontiguousarray(inp["L1_attn_wq_w"].T, dtype=np.float32),
        "wk1t": np.ascontiguousarray(inp["L1_attn_wk_w"].T, dtype=np.float32),
        "wv1t": np.ascontiguousarray(inp["L1_attn_wv_w"].T, dtype=np.float32),
        "wo1t": np.ascontiguousarray(inp["L1_attn_wo_w"].T, dtype=np.float32),
        "bq1p": np.ascontiguousarray(inp["L1_attn_wq_b"].reshape(8, 128).T, dtype=np.float32),
        "bk1p": np.ascontiguousarray(inp["L1_attn_wk_b"].reshape(8, 128).T, dtype=np.float32),
        "bk1r": np.ascontiguousarray(inp["L1_attn_wk_b"][None, :], dtype=np.float32),
        "bv1r": np.ascontiguousarray(inp["L1_attn_wv_b"][None, :], dtype=np.float32),
        "bo1r": np.ascontiguousarray(inp["L1_attn_wo_b"][None, :], dtype=np.float32),
        "w1_1t": np.ascontiguousarray(inp["L1_ffn_w1"].T, dtype=np.float32),
        "b1p": np.ascontiguousarray(inp["L1_ffn_b1"].reshape(32, 128).T, dtype=np.float32),
        "w2_1t": np.ascontiguousarray(inp["L1_ffn_w2"].T, dtype=np.float32),
        "b2r": np.ascontiguousarray(inp["L1_ffn_b2"][None, :], dtype=np.float32),
        "triu": triu,
        "ident16": ident16,
        "ident32": np.eye(128, dtype=np.float32),
    }
    for c in range(8):
        b, h = c // 2, c % 2
        xs = x[b]
        own = xs[h * TH:(h + 1) * TH]
        pre = xs[0:TH]
        halo = np.zeros((4, DM), dtype=np.float32)
        if h == 0:
            halo[1] = xs[TH]
        else:
            halo[0] = xs[TH - 1]
        halo[3] = xs[TH]
        xcat = np.zeros((TCAT, DM), dtype=np.float32)
        xcat[0:TH] = own
        xcat[TH:TS] = pre
        xcat[TS:TS + 4] = halo
        d = dict(shared)
        d["x_seq_t"] = np.ascontiguousarray(xs.T)
        d["x_seq"] = xs
        d["x_cat"] = xcat
        d["x_cat_t"] = np.ascontiguousarray(xcat.T)
        d["recvmask"] = np.full((64, 1), float(h), dtype=np.float32)
        per_core.append(d)
    return per_core



IN_SPECS = {
    "x_seq_t": ([DM, TS], F32),
    "x_seq": ([TS, DM], F32),
    "x_cat": ([TCAT, DM], F32),
    "x_cat_t": ([DM, TCAT], F32),
    "w_qkv": ([DM, 3 * DM], F32),
    "w1_0t": ([DM, DF], F32),
    "convw": ([128, 96], F32),
    "convb": ([128, 32], F32),
    "w2_0t": ([DF, DM], F32),
    "wq1t": ([DM, DM], F32),
    "wk1t": ([DM, DM], F32),
    "wv1t": ([DM, DM], F32),
    "wo1t": ([DM, DM], F32),
    "bq1p": ([128, 8], F32),
    "bk1p": ([128, 8], F32),
    "bk1r": ([1, DM], F32),
    "bv1r": ([1, DM], F32),
    "bo1r": ([1, DM], F32),
    "w1_1t": ([DM, DF], F32),
    "b1p": ([128, 32], F32),
    "w2_1t": ([DF, DM], F32),
    "b2r": ([1, DM], F32),
    "recvmask": ([64, 1], F32),
    "triu": ([128, 128], F32),
    "ident16": ([128, 128], BF16),
    "ident32": ([128, 128], F32),
}

SCR_SPECS = {
    "k0t": ([DM, TS], BF16),
    "v0": ([NT_SEQ, 128, NH, DH], BF16),
    "q0t": ([DM, TS], BF16),
    "xqt0": ([DM, TCAT], BF16),
    "st0": ([6, TCAT], F32),
    "attn0o": ([TS, DM], F32),
    "h1d": ([DF, 4100], F32),
    "x1c": ([TS, DM], F32),
    "q1t": ([DM, TH], BF16),
    "k1t": ([DM, TH], BF16),
    "k1m": ([TS, DM], BF16),
    "v1m": ([TS, DM], BF16),
    "xqt1": ([DM, TH], BF16),
    "st1": ([6, TH], F32),
    "spre": ([64, DM], F32),
    "snap": ([16, 64, DM], BF16),
    "attn1o": ([TH, DM], F32),
    "h1d1": ([DF, TH], F32),
    "dsg1": ([128, TH // 128], F32),
    "y_own": ([TH, DM], F32),
}


def declare(nc, names, kind, specs=None):
    specs = specs or {**IN_SPECS, **SCR_SPECS}
    out = {}
    for n in names:
        shape, dt = specs[n]
        out[n] = nc.dram_tensor(n, shape, dt, kind=kind).ap()
    return out


def bcast_row(row_ap, t0, tw):
    r = row_ap[0:1, t0:t0 + tw].partition_broadcast(128)
    while len(r.shape) > 2:
        r = r.squeeze(1)
    return r


def emit_stat_rows(K, st_sb, NT, js, rows_d, ident32_sb):
    s_t = K.gsem("srow_t")
    s_e = K.gsem("srow_e")
    s_d = K.gsem("srow_d")
    stage = K.gsb("strow_stage", [128, 128], F32)
    for jn, j in enumerate(js):
        pb = K.pall[0:NT, 3584:3712]
        K.wait("tensor", s_e, s_e.n)

        K.step("tensor", s_t,
               lambda eng, j=j, pb=pb: eng.transpose(pb, st_sb[:, j * NT:(j + 1) * NT], ident32_sb[:]))
        K.wait("vector", s_t, s_t.n)
        if jn > 0:
            K.wait("vector", s_d, s_d.n)
        K.step("vector", s_e, lambda e, pb=pb, NT=NT: e.tensor_copy(stage[0:NT, :], pb))
        K.wait("scalar", s_e, s_e.n)
        K.dma("scalar", rows_d[j:j + 1, :].rearrange("o (i q) -> o i q", q=128).squeeze(0),
              stage[0:NT, :], sem=s_d)
    K.wait("scalar", s_d, s_d.n)



def emit_stats(K, x_ap, n_tiles, st_sb, with_quant, sT):
    xt0 = K.gsb("st_xt", [128, 2 * DM], F32)
    xt = [xt0[:, 0:DM], xt0[:, DM:2 * DM]]
    sq0 = K.gsb("st_sq", [128, 2 * DM], F32)
    sqb = [sq0[:, 0:DM], sq0[:, DM:2 * DM]]
    NT = n_tiles
    s_dv = K.gsem("st_dv")
    s_pl = K.gsem("st_pl")
    s_sq = K.gsem("st_sq")
    sld = [K.gsem("st_ld0"), K.gsem("st_ld1")]
    base_dv, base_pl, base_sq = s_dv.n, s_pl.n, s_sq.n
    for i in range(NT):
        b = i % 2
        if i >= 2:
            K.wait("sync", s_dv, base_dv + i - 1)
            K.wait("sync", s_pl, base_pl + i - 1)
            K.wait("sync", s_sq, base_sq + i - 1)
        v = K.dma("sync", xt[b], x_ap[i * 128:(i + 1) * 128, :], sem=sld[b])
        K.wait("vector", sld[b], v)
        K.wait("gpsimd", sld[b], v)
        if i >= 2:
            K.wait("gpsimd", s_sq, base_sq + i - 1)

        def pw(eng, i=i, b=b):
            return eng.tensor_tensor(sqb[b], xt[b], xt[b], ALU.mult)
        K.step("gpsimd", s_pl, pw)

        def dw(eng, i=i, b=b):
            eng.reduce_sum(st_sb[:, 6 * NT + i:6 * NT + i + 1], xt[b], axis=AX.X)
            eng.reduce_max(st_sb[:, 8 * NT + i:8 * NT + i + 1], xt[b], axis=AX.X)
            return eng.tensor_reduce(st_sb[:, 9 * NT + i:9 * NT + i + 1], xt[b],
                                     AX.X, ALU.min)
        K.step("vector", s_dv, dw)
        K.wait("vector", s_pl, s_pl.n)

        def dw2(eng, i=i, b=b):
            return eng.reduce_sum(st_sb[:, 7 * NT + i:7 * NT + i + 1], sqb[b], axis=AX.X)
        K.step("vector", s_sq, dw2)
    sc = K.gsem("st_sc")
    M = lambda j: st_sb[:, j * NT:(j + 1) * NT]
    K.wait("vector", s_dv, s_dv.n)
    K.wait("vector", s_sq, s_sq.n)
    K.dep("vector", sc, lambda e: e.tensor_scalar(M(0), M(6), 1.0 / DM, None, ALU.mult))
    K.dep("vector", sc, lambda e: e.tensor_scalar(M(7), M(7), 1.0 / DM, None, ALU.mult))
    K.dep("vector", sc, lambda e: e.tensor_tensor(M(6), M(0), M(0), ALU.mult))
    K.dep("vector", sc, lambda e: e.tensor_tensor(M(1), M(7), M(6), ALU.subtract))
    K.dep("vector", sc, lambda e: e.tensor_scalar(M(1), M(1), EPS, None, ALU.add))
    s_act = K.gsem("st_act")
    K.wait("scalar", sc, sc.n)
    K.dep("scalar", s_act, lambda e: e.activation(M(1), M(1), AF.Ln))
    K.dep("scalar", s_act, lambda e: e.activation(M(1), M(1), AF.Exp, scale=-0.5))
    K.wait("vector", s_act, s_act.n)
    K.dep("vector", sc, lambda e: e.tensor_tensor(M(3), M(0), M(1), ALU.mult))
    if with_quant:
        K.dep("vector", sc, lambda e: e.tensor_tensor(M(8), M(8), M(0), ALU.subtract))
        K.dep("vector", sc, lambda e: e.tensor_tensor(M(9), M(0), M(9), ALU.subtract))
        K.dep("vector", sc, lambda e: e.tensor_tensor(M(8), M(8), M(9), ALU.max))
        K.dep("vector", sc, lambda e: e.tensor_tensor(M(8), M(8), M(1), ALU.mult))
        K.dep("vector", sc, lambda e: e.tensor_scalar(M(8), M(8), 1e-5, None, ALU.max))
        K.dep("vector", sc, lambda e: e.tensor_scalar(M(5), M(8), 1.0 / 127.0, None, ALU.mult))
        K.dep("vector", sc, lambda e: e.reciprocal(M(8), M(8)))
        K.dep("vector", sc, lambda e: e.tensor_scalar(M(8), M(8), 127.0, None, ALU.mult))
        K.dep("vector", sc, lambda e: e.tensor_tensor(M(2), M(8), M(1), ALU.mult))
        K.dep("vector", sc, lambda e: e.tensor_tensor(M(4), M(0), M(2), ALU.mult))


def emit_xnt(K, xt_ap, st_d, n_tiles, xnt_sb, xqt_stage, xqt_d, with_quant, sT):
    NT = n_tiles
    T = NT * 128
    CH = 11 if NT % 11 == 0 else 8
    NCH = NT // CH
    TW = CH * 128
    jrows = [1, 3, 2, 4] if with_quant else [1, 3]
    bc = K.gsb("xnt_bc", [128, 4 * 1408], F32)
    xf0 = K.gsb("xnt_xf", [128, 2 * 1408], F32)
    xf = [xf0[:, 0:TW], xf0[:, 1408:1408 + TW]]
    t10 = K.gsb("xnt_t1", [128, 1408], F32)
    t1b = [t10[:, 0:TW], t10[:, 0:TW]]
    qtmp = K.gsb("xnt_qt", [128, 1408], F32) if with_quant else None
    qi = K.gsb("xnt_qi", [128, 1408], I32) if with_quant else None
    s_done = K.gsem("xnt_done")
    s_mid = K.gsem("xnt_mid")
    s_bc = K.gsem("xnt_bcld")
    sld = [K.gsem("xnt_ld0"), K.gsem("xnt_ld1")]
    sst = [K.gsem("xnt_st0"), K.gsem("xnt_st1")]
    base = s_done.n
    it = 0
    qdma = {}
    for tc in range(NCH):
        if tc > 0:
            K.wait("scalar", s_done, base + it)
        vs = 0
        for jj, j in enumerate(jrows):
            src_ = bcast_row(st_d[j:j + 1, :], tc * TW, TW)
            vs = K.dma("scalar", bc[:, jj * 1408:jj * 1408 + TW], src_, sem=s_bc)
        for fc in range(8):
            b = it % 2
            q = "sync" if b == 0 else "scalar"
            if it >= 2:
                K.wait(q, s_done, base + it - 1)
            v = K.dma(q, xf[b], xt_ap[fc * 128:(fc + 1) * 128, tc * TW:(tc + 1) * TW], sem=sld[b])
            if it >= 1:
                K.wait("vector", s_done, base + it)
            K.wait("vector", sld[b], v)
            if fc == 0:
                K.wait("vector", s_bc, vs)
            if with_quant and it >= 2:
                K.wait("vector", sst[b], qdma[it - 2])
            r_bc = bc[:, 0:TW]
            mr_bc = bc[:, 1408:1408 + TW]
            out = xnt_sb[:, fc * T + tc * TW: fc * T + (tc + 1) * TW]
            K.step("vector", s_mid,
                   lambda e, b=b: e.tensor_tensor(t1b[b], xf[b], r_bc, ALU.mult))
            if with_quant:
                al_bc = bc[:, 2 * 1408:2 * 1408 + TW]
                mal_bc = bc[:, 3 * 1408:3 * 1408 + TW]
                K.step("vector", s_mid,
                       lambda e, b=b: e.tensor_tensor(qtmp[:, 0:TW], xf[b], al_bc, ALU.mult))
                K.wait("vector", s_mid, s_mid.n - 1)
                K.step("vector", s_mid,
                       lambda e, b=b, out=out: e.tensor_tensor(out, t1b[b], mr_bc, ALU.subtract))
                K.wait("vector", s_mid, s_mid.n - 1)
                K.dep("vector", s_mid,
                      lambda e: e.tensor_tensor(qtmp[:, 0:TW], qtmp[:, 0:TW], mal_bc, ALU.subtract))
                K.dep("vector", s_mid, lambda e: e.tensor_scalar(qtmp[:, 0:TW], qtmp[:, 0:TW], RC, None, ALU.add))
                K.dep("vector", s_mid, lambda e: e.tensor_scalar(qtmp[:, 0:TW], qtmp[:, 0:TW], -RC, None, ALU.add))
                K.dep("vector", s_mid, lambda e: e.tensor_copy(qi[:, 0:TW], qtmp[:, 0:TW]))
                K.wait("vector", s_mid, s_mid.n)

                def last(e, b=b):
                    return e.tensor_copy(xqt_stage[:, b * 1408:b * 1408 + TW], qi[:, 0:TW])
                K.step("vector", s_done, last)
            else:
                K.wait("vector", s_mid, s_mid.n)
                K.step("vector", s_done,
                       lambda e, b=b, out=out: e.tensor_tensor(out, t1b[b], mr_bc, ALU.subtract))
            if with_quant:
                K.wait("gpsimd", s_done, s_done.n)
                qdma[it] = K.dma("gpsimd", xqt_d[fc * 128:(fc + 1) * 128, tc * TW:(tc + 1) * TW],
                                 xqt_stage[:, b * 1408:b * 1408 + TW], sem=sst[b])
            it += 1
    K.wait("vector", s_done, s_done.n)


def emit_gemm_fm(K, w_sb, w_col0, w_stride, xnt_sb, xnt_T, out_d, n_fo, t_lo, t_hi,
                 out_bf, stage, sT, out_row0=0):
    s_mm = K.gsem("gfm_mm")
    s_ev = K.gsem("gfm_ev")
    sst = [K.gsem("gfm_st0"), K.gsem("gfm_st1")]
    NTC2 = (t_hi - t_lo) // 512
    pall = K.pall
    base_ev = s_ev.n
    dmas = {}
    it = 0
    for fo in range(n_fo):
        for tch in range(NTC2):
            t0 = t_lo + tch * 512
            slot = it % 2
            pb = pall[:, slot * 512:(slot + 1) * 512]
            if it >= 2:
                K.wait("tensor", s_ev, base_ev + it - 1)

            def mm(eng, fo=fo, t0=t0, pb=pb):
                for fc in range(7):
                    eng.matmul(pb,
                               w_sb[:, fc * w_stride + w_col0 + fo * 128: fc * w_stride + w_col0 + (fo + 1) * 128],
                               xnt_sb[:, fc * xnt_T + t0: fc * xnt_T + t0 + 512],
                               start=(fc == 0), stop=False)
                return eng.matmul(pb,
                                  w_sb[:, 7 * w_stride + w_col0 + fo * 128: 7 * w_stride + w_col0 + (fo + 1) * 128],
                                  xnt_sb[:, 7 * xnt_T + t0: 7 * xnt_T + t0 + 512],
                                  start=False, stop=True)
            K.step("tensor", s_mm, mm)
            K.wait("scalar", s_mm, s_mm.n)
            if it >= 2:
                K.wait("scalar", sst[slot], dmas[it - 2])
            sb = stage[:, slot * 512:(slot + 1) * 512]
            K.step("scalar", s_ev, lambda e, pb=pb, sb=sb: e.copy(sb, pb))
            K.wait("gpsimd", s_ev, s_ev.n)
            dmas[it] = K.dma("gpsimd", out_d[out_row0 + fo * 128: out_row0 + (fo + 1) * 128,
                                             t0 - t_lo: t0 - t_lo + 512], sb, sem=sst[slot])
            it += 1


def emit_att0(K, T, hook=None):
    nc = K.nc
    pall = K.pall
    vaug = K.gsb("vaug", [128, 32 * 16 * 65], BF16)
    ktq = K.gsb("ktq", [128, 4 * TS], BF16)
    pt = K.gsb("att_pt", [128, 2 * 1024], BF16)
    ev = K.gsb("att_ev", [128, 2 * 512], F32)
    ast = K.gsb("att_st", [128, 2 * 512], F32)
    arc = K.gsb("att_rc", [128, 2], F32)
    s_vld = K.gsem("att_vld")
    s_ldk = [K.gsem("att_ldk0"), K.gsem("att_ldk1")]
    s_ldq = [K.gsem("att_ldq0"), K.gsem("att_ldq1")]
    s_sc = K.gsem("att_sc")
    s_ex = K.gsem("att_ex")
    s_pv = K.gsem("att_pv")
    s_evd = K.gsem("att_evd")
    s_tr = K.gsem("att_tr")
    s_ns = K.gsem("att_ns")
    s_out = [K.gsem("att_out0"), K.gsem("att_out1")]

    for kc in range(32):
        dst = vaug[:, kc * 1040:(kc + 1) * 1040].rearrange("p (h e) -> p h e", e=65)[:, :, 0:64]
        K.dma("sync", dst, T["v0"][kc], sem=s_vld)
    ones_view = vaug[:].rearrange("p (g e) -> p g e", e=65)[:, :, 64:65]
    K.wait("vector", s_vld, s_vld.n)
    smem = K.gsem("att_ones")
    K.step("vector", smem, lambda e: e.memset(ones_view, 1.0))
    K.wait("tensor", smem, smem.n)

    b_sc, b_ex, b_pv, b_evd, b_tr, b_ns = s_sc.n, s_ex.n, s_pv.n, s_evd.n, s_tr.n, s_ns.n
    QG = 8
    ldk = {}
    ldq = {}
    odma = {}
    for hp in range(2):
        ldk[hp] = K.dma("sync", ktq[:, (hp % 2) * TS:(hp % 2) * TS + TS],
                        T["k0t"][hp * 128:(hp + 1) * 128, :], sem=s_ldk[hp % 2])
        ldq[hp] = K.dma("sync", ktq[:, (2 + hp % 2) * TS:(2 + hp % 2) * TS + TS],
                        T["q0t"][hp * 128:(hp + 1) * 128, :], sem=s_ldq[hp % 2])
    it_hq = 0
    for hp in range(8):
        sl = hp % 2
        kt = ktq[:, sl * TS:(sl + 1) * TS]
        qt = ktq[:, (2 + sl) * TS:(2 + sl) * TS + TS]
        K.wait("tensor", s_ldk[sl], ldk[hp])
        K.wait("tensor", s_ldq[sl], ldq[hp])
        if hp + 2 < 8:
            K.wait("sync", s_sc, b_sc + (hp + 1) * QG * 32)
            ldk[hp + 2] = K.dma("sync", ktq[:, sl * TS:(sl + 1) * TS],
                                T["k0t"][(hp + 2) * 128:(hp + 3) * 128, :], sem=s_ldk[sl])
            ldq[hp + 2] = K.dma("sync", ktq[:, (2 + sl) * TS:(2 + sl) * TS + TS],
                                T["q0t"][(hp + 2) * 128:(hp + 3) * 128, :], sem=s_ldq[sl])
        for qg in range(QG):
            kcn = 32

            def emit_pv(pvkc, hp=hp):
                K.wait("tensor", s_ex, b_ex + it_hq * kcn + pvkc + 1)
                if pvkc == 0 and it_hq >= 1:
                    K.wait("tensor", s_evd, b_evd + it_hq)

                def pv_mm(eng, kc=pvkc, ss=pvkc % 2, hp=hp):
                    eng.matmul(pall[0:65, 2048:2560],
                               vaug[:, kc * 1040 + (2 * hp) * 65: kc * 1040 + (2 * hp) * 65 + 65],
                               pt[:, ss * 1024:ss * 1024 + 512],
                               start=(kc == 0), stop=(kc == kcn - 1))
                    return eng.matmul(pall[0:65, 2560:3072],
                                      vaug[:, kc * 1040 + (2 * hp + 1) * 65: kc * 1040 + (2 * hp + 1) * 65 + 65],
                                      pt[:, ss * 1024 + 512:ss * 1024 + 1024],
                                      start=(kc == 0), stop=(kc == kcn - 1))
                K.step("tensor", s_pv, pv_mm)

            for kc in range(kcn):
                ss = kc % 2
                pS = pall[:, ss * 1024:(ss + 1) * 1024]
                if kc >= 2:
                    K.wait("tensor", s_ex, b_ex + it_hq * kcn + kc - 1)

                def sc_mm(eng, kt=kt, qt=qt, kc=kc, qg=qg, pS=pS):
                    eng.matmul(pS[:, 0:512], kt[0:64, kc * 128:(kc + 1) * 128],
                               qt[0:64, qg * 512:(qg + 1) * 512],
                               start=True, stop=True, tile_position=(0, 0))
                    return eng.matmul(pS[:, 512:1024], kt[64:128, kc * 128:(kc + 1) * 128],
                                      qt[64:128, qg * 512:(qg + 1) * 512],
                                      start=True, stop=True, tile_position=(64, 0))
                K.step("tensor", s_sc, sc_mm)
                K.wait("scalar", s_sc, s_sc.n)
                if kc >= 2:
                    K.wait("scalar", s_pv, b_pv + it_hq * kcn + kc - 1)

                def ex_op(eng, ss=ss, pS=pS):
                    return eng.activation(pt[:, ss * 1024:(ss + 1) * 1024], pS,
                                          AF.Exp, scale=ATTN_SCALE)
                K.step("scalar", s_ex, ex_op)
                if kc >= 1:
                    emit_pv(kc - 1)
            emit_pv(kcn - 1)
            K.wait("vector", s_pv, s_pv.n)
            if it_hq >= 1:
                K.wait("vector", s_ns, b_ns + it_hq * 16)
                K.wait("vector", s_tr, b_tr + it_hq * 8)

            def evac(e):
                e.tensor_copy(ev[0:65, 0:512], pall[0:65, 2048:2560])
                return e.tensor_copy(ev[0:65, 512:1024], pall[0:65, 2560:3072])
            K.step("vector", s_evd, evac)
            K.wait("tensor", s_evd, s_evd.n)
            so = it_hq % 2
            if it_hq >= 2:
                K.wait("vector", s_out[so], odma[it_hq - 2])
            if it_hq >= 1:
                K.wait("tensor", s_ns, b_ns + it_hq * 16)
            for X in range(2):
                for j in range(4):
                    tb = (X * 4 + j) % 2
                    pT = pall[0:128, 3072 + tb * 512: 3072 + tb * 512 + 65]
                    if X * 4 + j >= 2:
                        K.wait("tensor", s_ns, b_ns + it_hq * 16 + 2 * (X * 4 + j - 1))

                    def tr(e, X=X, j=j, pT=pT):
                        return e.transpose(pT, ev[0:65, X * 512 + j * 128: X * 512 + (j + 1) * 128],
                                           T["i32sb"][0:65, 0:65])
                    K.step("tensor", s_tr, tr)
                    K.wait("vector", s_tr, s_tr.n)
                    tb2 = (X * 4 + j) % 2
                    K.step("vector", s_ns,
                           lambda e, pT=pT, tb2=tb2: e.reciprocal(arc[:, tb2:tb2 + 1], pT[:, 64:65]))

                    def nrm(e, X=X, j=j, pT=pT, so=so, tb2=tb2):
                        return e.tensor_scalar(ast[:, so * 512 + j * 128 + X * 64: so * 512 + j * 128 + X * 64 + 64],
                                               pT[:, 0:64], arc[:, tb2:tb2 + 1], None, ALU.mult)
                    K.dep("vector", s_ns, nrm)
            K.wait("gpsimd", s_ns, s_ns.n)
            dst = T["attn0o"][qg * 512:(qg + 1) * 512, hp * 128:(hp + 1) * 128]                 .rearrange("(j p) c -> p j c", p=128)
            odma[it_hq] = K.dma(
                "gpsimd", dst,
                ast[:, so * 512:(so + 1) * 512].rearrange("p (j c) -> p j c", c=128),
                sem=s_out[so])
            if hook is not None:
                hook(it_hq)
            it_hq += 1
    K.barrier()


def build_att0():
    nc = bass.Bass("TRN2", target_bir_lowering=False, debug=False)
    ins = ["k0t", "v0", "q0t", "ident32"]
    T = declare(nc, ["k0t", "v0", "q0t"], "ExternalInput")
    T.update(declare(nc, ["ident32"], "ExternalInput", {**IN_SPECS}))
    T.update(declare(nc, ["attn0o"], "ExternalOutput"))
    K = KB(nc)
    K.pall = nc.alloc_psum_tensor("pall", [128, 4096], F32)
    i32sb = K.gsb("ident32_sb", [128, 128], F32)
    K.dma("sync", i32sb[:], T["ident32"])
    T["i32sb"] = i32sb
    K.barrier()
    emit_att0(K, T)
    K.emit()
    return nc, ins, ["attn0o"]


def emit_wquant(K, w_ap, nchunks, ncols, wq_sb, tag, dram_out=None):
    wbuf = K.gsb("scr32k", [128, 2 * DF], F32)
    wsum = K.gsb("wq_sum", [128, 40], F32)
    onesc = K.gsb("ones_col", [128, 1], F32)
    onesr = K.gsb("ones_row", [1, 128], F32)
    scl = K.gsb(f"wsc_{tag}", [128, 2], F32)
    s_ld = [K.gsem("wq_ld0"), K.gsem("wq_ld1")]
    s_dv = K.gsem("wq_dv")
    s_mm = K.gsem("wq_mm")
    smi = K.gsem("wq_misc")
    b_dv = s_dv.n
    K.step("vector", smi, lambda e: e.memset(onesc[:], 1.0))
    K.step("vector", smi, lambda e: e.memset(onesr[:], 1.0))
    lds = {}
    for i in range(nchunks):
        b = i % 2
        if i >= 2:
            K.wait("sync", s_dv, b_dv + i - 1)
        lds[i] = K.dma("sync", wbuf[:, b * DF:b * DF + ncols],
                       w_ap[i * 128:(i + 1) * 128, :], sem=s_ld[b])
        K.wait("vector", s_ld[b], lds[i])
        K.step("vector", s_dv,
               lambda e, i=i, b=b: e.tensor_reduce(wsum[:, i:i + 1], wbuf[:, b * DF:b * DF + ncols],
                                                   AX.X, ALU.add, apply_absolute_value=True))
    K.wait("vector", s_dv, s_dv.n)
    sc2 = K.gsem("wq_sc")
    K.dep("vector", sc2, lambda e: e.reduce_sum(wsum[:, 36:37], wsum[:, 0:nchunks], axis=AX.X))
    K.wait("tensor", sc2, sc2.n)
    K.step("tensor", s_mm, lambda e: e.matmul(K.pall[0:1, 3584:3585], wsum[:, 36:37], onesc[:],
                                              start=True, stop=True))
    K.wait("vector", s_mm, s_mm.n)
    K.dep("vector", sc2, lambda e: e.tensor_scalar(wsum[0:1, 37:38], K.pall[0:1, 3584:3585],
                                                   1.0 / (nchunks * 128 * ncols), None, ALU.mult))
    K.dep("vector", sc2, lambda e: e.tensor_scalar(wsum[0:1, 37:38], wsum[0:1, 37:38], 1e-5, None, ALU.max))
    K.dep("vector", sc2, lambda e: e.reciprocal(wsum[0:1, 38:39], wsum[0:1, 37:38]))
    K.wait("tensor", sc2, sc2.n)
    K.step("tensor", s_mm, lambda e: e.matmul(K.pall[0:128, 3712:3713], onesr[:], wsum[0:1, 38:39],
                                              start=True, stop=True))
    K.wait("vector", s_mm, s_mm.n)
    K.dep("vector", sc2, lambda e: e.tensor_copy(scl[:, 0:1], K.pall[0:128, 3712:3713]))
    K.wait("tensor", sc2, sc2.n)
    K.step("tensor", s_mm, lambda e: e.matmul(K.pall[0:128, 3712:3713], onesr[:], wsum[0:1, 37:38],
                                              start=True, stop=True))
    K.wait("vector", s_mm, s_mm.n)
    K.dep("vector", sc2, lambda e: e.tensor_copy(scl[:, 1:2], K.pall[0:128, 3712:3713]))
    qtmp = K.gsb("scr16k", [128, 4096], F32)[:]
    qi = K.gsb("qi_arena", [128, DF], I32)[:]
    sq = K.gsem("wq_q")
    K.wait("vector", sc2, sc2.n)
    NSTEP = 7
    bq = sq.n
    qdd = {}
    K.wait("sync", s_dv, s_dv.n)
    for i in range(nchunks):
        b = i % 2
        if i >= 2:
            K.wait("sync", sq, bq + (i - 1) * NSTEP)
        lds[i] = K.dma("sync", wbuf[:, b * DF:b * DF + ncols],
                       w_ap[i * 128:(i + 1) * 128, :], sem=s_ld[b])
        K.wait("vector", s_ld[b], lds[i])
        K.dep("vector", sq, lambda e, b=b: e.tensor_scalar(
            qtmp[:, 0:ncols], wbuf[:, b * DF:b * DF + ncols], scl[:, 0:1], None, ALU.mult))
        K.dep("vector", sq, lambda e: e.tensor_scalar(qtmp[:, 0:ncols], qtmp[:, 0:ncols], RC, None, ALU.add))
        K.dep("vector", sq, lambda e: e.tensor_scalar(qtmp[:, 0:ncols], qtmp[:, 0:ncols], -RC, None, ALU.add))
        K.dep("vector", sq, lambda e: e.tensor_copy(qi[:, 0:ncols], qtmp[:, 0:ncols]))
        K.dep("vector", sq, lambda e: e.tensor_scalar(qi[:, 0:ncols], qi[:, 0:ncols], 1, None, ALU.min))
        K.dep("vector", sq, lambda e: e.tensor_scalar(qi[:, 0:ncols], qi[:, 0:ncols], -1, None, ALU.max))
        if dram_out is None:
            K.dep("vector", sq, lambda e, i=i: e.tensor_copy(
                wq_sb[:, i * ncols:(i + 1) * ncols], qi[:, 0:ncols]))
        else:
            qst = K.gsb("scr4k", [128, 1024], F32)[:].bitcast(BF16)
            sqs = [K.gsem("wq_st0"), K.gsem("wq_st1")]
            bq2 = i % 2
            if i >= 2:
                K.wait("vector", sqs[bq2], qdd.get(i - 2, 0))
            K.dep("vector", sq, lambda e, i=i, bq2=bq2: e.tensor_copy(
                qst[:, bq2 * 1024:bq2 * 1024 + ncols], qi[:, 0:ncols]))
            K.wait("gpsimd", sq, sq.n)
            qdd[i] = K.dma("gpsimd", dram_out[i * 128:(i + 1) * 128, :],
                           qst[:, bq2 * 1024:bq2 * 1024 + ncols], sem=sqs[bq2])
    K.wait("vector", sq, sq.n)
    return scl


def wquant_pf_slices(K, T, w_ap, nchunks, ncols, scl, tag, wq_sb=None, dram_out=None,
                     pre_wait=None, pre_stores=()):
    nc = K.nc
    wbuf = K.gsb("wqpf_buf", [128, 2 * DF], F32)
    wsum = K.gsb(f"wqpf_sum_{tag}", [128, nchunks + 1], F32)
    wtot = K.gsb(f"wqpf_tot_{tag}", [1, 1], F32)
    srow = K.gsb(f"wqpf_row_{tag}", [1, 2], F32)
    wrow_d = nc.dram_tensor(f"wqpf_d_{tag}", [1, 2], F32, kind="Internal").ap()
    qtmp = K.gsb("wqpf_qt", [128, DF], F32)
    qi = K.gsb("wqpf_qi", [128, DF], I32)
    s_ld = [K.gsem(f"pf_ld0_{tag}"), K.gsem(f"pf_ld1_{tag}")]
    s_dv = K.gsem(f"pf_dv_{tag}")
    s_pl = K.gsem(f"pf_pl_{tag}")
    s_rw = K.gsem(f"pf_rw_{tag}")
    s_q = K.gsem(f"pf_q_{tag}")
    s_st = [K.gsem(f"pf_st0_{tag}"), K.gsem(f"pf_st1_{tag}"), K.gsem(f"pf_st2_{tag}")]
    b_dv = s_dv.n
    lds = {}
    out = []
    p1g = 2 if nchunks == 8 else 4

    def mk_p1(i0, n):
        def go():
            for i in range(i0, i0 + n):
                b = i % 2
                if i < 2 and pre_wait is not None:
                    K.wait("scalar", pre_wait, pre_wait.n)
                if i >= 2:
                    K.wait("scalar", s_dv, b_dv + i - 1)
                lds[i] = K.dma("scalar", wbuf[:, b * DF:b * DF + ncols],
                               w_ap[i * 128:(i + 1) * 128, :], sem=s_ld[b])
                K.wait("vector", s_ld[b], lds[i])
                K.step("vector", s_dv,
                       lambda e, i=i, b=b: e.tensor_reduce(wsum[:, i:i + 1],
                                                           wbuf[:, b * DF:b * DF + ncols],
                                                           AX.X, ALU.add, apply_absolute_value=True))
        return go
    for i0 in range(0, nchunks, p1g):
        out.append(mk_p1(i0, min(p1g, nchunks - i0)))

    def finale():
        K.wait("vector", s_dv, s_dv.n)
        K.step("vector", s_dv, lambda e: e.reduce_sum(wsum[:, nchunks:nchunks + 1],
                                                      wsum[:, 0:nchunks], axis=AX.X))
        K.wait("gpsimd", s_dv, s_dv.n)
        K.step("gpsimd", s_pl, lambda e: e.tensor_reduce(wtot[:], wsum[:, nchunks:nchunks + 1],
                                                         AX.C, ALU.add))
        K.wait("vector", s_pl, s_pl.n)
        K.dep("vector", s_rw, lambda e: e.tensor_scalar(
            srow[0:1, 1:2], wtot[:], 1.0 / (nchunks * 128 * ncols), None, ALU.mult))
        K.dep("vector", s_rw, lambda e: e.tensor_scalar(srow[0:1, 1:2], srow[0:1, 1:2],
                                                        1e-5, None, ALU.max))
        K.dep("vector", s_rw, lambda e: e.reciprocal(srow[0:1, 0:1], srow[0:1, 1:2]))
        K.wait("gpsimd", s_rw, s_rw.n)
        v1 = K.dma("gpsimd", wrow_d, srow[:], sem=s_st[0])
        K.wait("gpsimd", s_st[0], v1)
        v2 = K.dma("gpsimd", scl[:], bcast_row(wrow_d, 0, 2), sem=s_st[0])
        K.wait("vector", s_st[0], v2)
    out.append(finale)

    p2g = 1 if ncols >= 4096 else 2
    bq = s_q.n
    NST = 7 if dram_out is None else 7
    qdd = {}

    def mk_p2(i0, n):
        def go():
            for i in range(i0, i0 + n):
                b = i % 2
                if i >= 2:
                    K.wait("scalar", s_q, bq + (i - 1) * NST)
                lds[i] = K.dma("scalar", wbuf[:, b * DF:b * DF + ncols],
                               w_ap[i * 128:(i + 1) * 128, :], sem=s_ld[b])
                K.wait("vector", s_ld[b], lds[i])
                K.dep("vector", s_q, lambda e, b=b: e.tensor_scalar(
                    qtmp[:, 0:ncols], wbuf[:, b * DF:b * DF + ncols], scl[:, 0:1], None, ALU.mult))
                K.dep("vector", s_q, lambda e: e.tensor_scalar(qtmp[:, 0:ncols], qtmp[:, 0:ncols],
                                                               RC, None, ALU.add))
                K.dep("vector", s_q, lambda e: e.tensor_scalar(qtmp[:, 0:ncols], qtmp[:, 0:ncols],
                                                               -RC, None, ALU.add))
                K.dep("vector", s_q, lambda e: e.tensor_copy(qi[:, 0:ncols], qtmp[:, 0:ncols]))
                K.dep("vector", s_q, lambda e: e.tensor_scalar(qi[:, 0:ncols], qi[:, 0:ncols],
                                                               1, None, ALU.min))
                K.dep("vector", s_q, lambda e: e.tensor_scalar(qi[:, 0:ncols], qi[:, 0:ncols],
                                                               -1, None, ALU.max))
                if dram_out is None:
                    K.dep("vector", s_q, lambda e, i=i: e.tensor_copy(
                        wq_sb[:, i * ncols:(i + 1) * ncols], qi[:, 0:ncols]))
                else:
                    qst = K.gsb("wqpf_qst", [128, 4096], F32)[:].bitcast(BF16)
                    bq2 = i % 2
                    if i < 2:
                        for ps_ in pre_stores:
                            K.wait("vector", ps_, ps_.n)
                    if i >= 2:
                        K.wait("vector", s_st[1 + bq2], qdd.get(i - 2, 0))
                    K.dep("vector", s_q, lambda e, i=i, bq2=bq2: e.tensor_copy(
                        qst[:, bq2 * 4096:bq2 * 4096 + ncols], qi[:, 0:ncols]))
                    K.wait("gpsimd", s_q, s_q.n)
                    qdd[i] = K.dma("gpsimd", dram_out[i * 128:(i + 1) * 128, :],
                                   qst[:, bq2 * 4096:bq2 * 4096 + ncols], sem=s_st[1 + bq2])
        return go
    first_p2 = len(out)

    def p2head():
        K.wait("vector", s_rw, s_rw.n)
        K.wait("scalar", s_dv, s_dv.n)
        mk_p2(0, p2g)()
    out.append(p2head)
    for i0 in range(p2g, nchunks, p2g):
        out.append(mk_p2(i0, min(p2g, nchunks - i0)))
    return out, s_q, s_st


def emit_ffn0(K, T):
    nc = K.nc
    pall = K.pall
    if "w1qd_g" in T:
        w2qd = T["w2qd_g"]
        scl1 = T["scl_w1"]
        scl2 = T["scl_w2"]
        w1q = K.gsb("w1q", [128, 8 * DF], BF16)
        s_wql = [K.gsem(f"f0_wq{s}") for s in range(8)]
        wl_v = {}
        for slab in range(8):
            for fc in range(8):
                wl_v[slab] = K.dma("sync" if fc % 2 == 0 else "scalar",
                                   w1q[:, fc * DF + slab * 512: fc * DF + slab * 512 + 512],
                                   T["w1qd_g"][fc * 128:(fc + 1) * 128,
                                               slab * 512:(slab + 1) * 512],
                                   sem=s_wql[slab])
    else:
        w1q = K.gsb("w1q", [128, 8 * DF], BF16)
        w2qd = nc.dram_tensor("w2qd", [DF, 1024], BF16, kind="Internal").ap()
        scl1 = emit_wquant(K, T["w1_0t"], 8, DF, w1q, "w1")
        K.barrier()
        scl2 = emit_wquant(K, T["w2_0t"], 32, 1024, None, "w2", dram_out=w2qd)
        K.barrier()
    s2row = nc.dram_tensor("s2row", [1, TS], F32, kind="Internal").ap()

    scr16k = K.gsb("scr16k", [128, 4096], F32)
    arena3 = K.gsb("scr8k", [128, 2048], F32)
    arena4 = K.gsb("scr4k", [128, 1024], F32)
    xqs = scr16k[:].bitcast(BF16)
    dsb = arena4[:]
    hst = arena3[:, 0:1024]
    s_xl = [K.gsem("f0_xl0"), K.gsem("f0_xl1")]
    s_dl = [K.gsem("f0_dl0"), K.gsem("f0_dl1")]
    s_hm = K.gsem("f0_hmm")
    s_he = K.gsem("f0_hev")
    s_hd = K.gsem("f0_hdv")
    s_hs = [K.gsem("f0_hst0"), K.gsem("f0_hst1")]
    b_he, b_hd, b_hm = s_he.n, s_hd.n, s_hm.n

    stripes = [(s * 512, 512) for s in range(8)] + [(4096, 128)]
    it = 0
    xl = {}
    dl = {}
    hs = {}
    for si, (t0, w) in enumerate(stripes):
        sl = si % 2
        if si >= 2:
            K.wait("sync", s_hm, b_hm + (si - 1) * 32)
        for fc in range(8):
            xl[(si, fc)] = K.dma("sync", xqs[:, sl * 4096 + fc * 512: sl * 4096 + fc * 512 + w],
                                 T["xqt0"][fc * 128:(fc + 1) * 128, t0:t0 + w], sem=s_xl[sl])
        dl[si] = K.dma("scalar", dsb[:, sl * 512:sl * 512 + w],
                       bcast_row(T["st0"][5:6, :], t0, w), sem=s_dl[sl])
        for cc in range(32):
            ps = pall[:, (it % 2) * 512:(it % 2) * 512 + 512]
            if it >= 2:
                K.wait("tensor", s_he, b_he + it - 1)
            K.wait("tensor", s_xl[sl], xl[(si, 7)])
            if si == 0 and "w1qd_g" in T:
                K.wait("tensor", s_wql[cc // 4], wl_v[cc // 4])

            def mm(e, sl=sl, cc=cc, w=w, ps=ps):
                inst = None
                for fc in range(8):
                    inst = e.matmul(ps[:, 0:w], w1q[:, fc * DF + cc * 128:(fc * DF + cc * 128) + 128],
                                    xqs[:, sl * 4096 + fc * 512: sl * 4096 + fc * 512 + w],
                                    start=(fc == 0), stop=(fc == 7))
                return inst
            K.step("tensor", s_hm, mm)
            K.wait("scalar", s_hm, s_hm.n)
            hb = hst[:, (it % 2) * 512:(it % 2) * 512 + 512]
            if it >= 2:
                K.wait("scalar", s_hd, b_hd + it - 1)
                K.wait("scalar", s_hs[it % 2], hs[it - 2])

            def ev(e, ps=ps, hb=hb, w=w):
                return e.activation(hb[:, 0:w], ps[:, 0:w], AF.Copy, scale=scl1[:, 1:2])
            K.step("scalar", s_he, ev)
            K.wait("vector", s_he, s_he.n)
            if it >= 2:
                K.wait("vector", s_hs[it % 2], hs[it - 2])
            K.wait("vector", s_dl[sl], dl[si])

            def dsc(e, hb=hb, sl=sl, w=w):
                return e.tensor_tensor(hb[:, 0:w], hb[:, 0:w], dsb[:, sl * 512:sl * 512 + w], ALU.mult)
            K.step("vector", s_hd, dsc)
            K.wait("gpsimd", s_hd, s_hd.n)
            if t0 < TS:
                base = 1 + t0 if t0 < TH else 2051 + (t0 - TH)
                hs[it] = K.dma("gpsimd", T["h1d"][cc * 128:(cc + 1) * 128, base:base + w],
                               hb[:, 0:w], sem=s_hs[it % 2])
            else:
                for j, dc in enumerate((0, 2049, 2050, 4099)):
                    K.dma("gpsimd", T["h1d"][cc * 128:(cc + 1) * 128, dc:dc + 1],
                          hb[:, j:j + 1], sem=s_hs[it % 2],
                          allow_slow_non_contiguous=True)
                hs[it] = s_hs[it % 2].n
            it += 1
    K.barrier()

    cwsb = K.gsb("convw_sb", [128, 96], F32)
    cbsb = K.gsb("convb_sb", [128, 32], F32)
    swl = K.gsem("f0c_wl")
    K.dma("sync", cwsb[:], T["convw"], sem=swl)
    K.dma("sync", cbsb[:], T["convb"], sem=swl)
    w2sb = w1q
    s_w2r = K.gsem("f0c_w2r")
    w2n = 0
    for cc in range(32):
        w2n = K.dma("sync" if cc % 2 == 0 else "scalar",
                    w2sb[:, cc * 1024:(cc + 1) * 1024],
                    w2qd[cc * 128:(cc + 1) * 128, :], sem=s_w2r)
    hw = K.gsb("hwin", [128, 4 * 514], F32)
    yb = K.gsb("convy", [128, 4 * 512], F32)
    mxb = K.gsb("mxbuf", [128, 128], F32)
    gfm = K.gsb("scr32k", [128, 2 * DF], F32)[:].bitcast(BF16)
    gmax = K.gsb("gmax", [128, 8], F32)
    s2b = K.gsb("scr4k", [128, 1024], F32)[:, 0:512]
    qtsl = K.gsb("qi_arena", [128, DF], I32)[:, 1024:2048].bitcast(F32)
    qisl = K.gsb("qi_arena", [128, DF], I32)[:, 2048:3072]
    rst = K.gsb("scr16k", [128, 4096], F32)[:]
    ost = K.gsb("scr8k", [128, 2048], F32)[:]
    s_wl = [K.gsem("f0c_l0"), K.gsem("f0c_l1"), K.gsem("f0c_l2"), K.gsem("f0c_l3")]
    s_cv = K.gsem("f0c_cv")
    s_cvp = K.gsem("f0c_cvp")
    s_sl = K.gsem("f0c_silu")
    s_tp = K.gsem("f0c_tp")
    s_mx = K.gsem("f0c_mx")
    s_mxp = K.gsem("f0c_mxp")
    s_qq = K.gsem("f0c_qq")
    s_qa = K.gsem("f0c_qa")
    s_qap = K.gsem("f0c_qap")
    s_rc = K.gsem("f0c_rc")
    s_qi = K.gsem("f0c_qi")
    s_qb = K.gsem("f0c_qb")
    s_f2 = K.gsem("f0c_f2")
    s_oe = K.gsem("f0c_oe")
    s_ov = K.gsem("f0c_ov")
    s_os = [K.gsem("f0c_os0"), K.gsem("f0c_os1")]
    s_rl = [K.gsem("f0c_rl0"), K.gsem("f0c_rl1")]
    s_sc2 = K.gsem("f0c_s2")
    s_s2l = K.gsem("f0c_s2l")
    K.wait("vector", swl, swl.n)
    K.wait("gpsimd", swl, swl.n)
    K.wait("scalar", swl, swl.n)
    ods = {}
    itq = 0
    b_f2_prev = 0
    b_oe0, b_ov0 = s_oe.n, s_ov.n
    for st_i in range(8):
        set_i, st = st_i // 4, st_i % 4
        base = 0 if set_i == 0 else 2050
        t0g = set_i * TH + st * 512
        sl = st_i % 2
        pf2 = b_f2_prev
        b_f2_prev = s_f2.n
        if st_i > 0:
            K.wait("tensor", s_oe, b_oe0 + itq * 4)
        b_sl = s_sl.n
        for cc in range(32):
            slot = cc % 4
            wb = hw[:, slot * 514:slot * 514 + 514]
            if cc >= 4 or st_i > 0:
                K.wait("sync" if slot < 2 else "scalar", s_sl, b_sl + cc - 3)
            lw = K.dma("sync" if slot < 2 else "scalar",
                       wb, T["h1d"][cc * 128:(cc + 1) * 128,
                                    base + st * 512: base + st * 512 + 514],
                       sem=s_wl[slot])
            K.wait("scalar", s_wl[slot], lw)
            yb_ = yb[:, slot * 512:slot * 512 + 512]

            def cv1(e, wb=wb, yb_=yb_, cc=cc):
                return e.activation(yb_, wb[:, 0:512], AF.Copy, scale=cwsb[:, cc * 3:cc * 3 + 1])
            K.step("scalar", s_cvp, cv1)
            K.wait("vector", s_cvp, s_cvp.n)

            def cv2(e, wb=wb, yb_=yb_, cc=cc):
                return e.scalar_tensor_tensor(yb_, wb[:, 1:513], cwsb[:, cc * 3 + 1:cc * 3 + 2],
                                              yb_, ALU.mult, ALU.add)
            K.step("vector", s_cv, cv2)

            def cv3(e, wb=wb, yb_=yb_, cc=cc):
                return e.scalar_tensor_tensor(yb_, wb[:, 2:514], cwsb[:, cc * 3 + 2:cc * 3 + 3],
                                              yb_, ALU.mult, ALU.add)
            K.dep("vector", s_cv, cv3)
            K.wait("scalar", s_cv, s_cv.n)

            if st_i > 0:
                K.wait("scalar", s_f2, pf2 + cc + 1)

            def sil(e, yb_=yb_, cc=cc):
                return e.activation(gfm[:, cc * 512:(cc + 1) * 512], yb_, AF.Silu,
                                    bias=cbsb[:, cc:cc + 1])
            K.step("scalar", s_sl, sil)
        b_tp, b_mx = s_tp.n, s_mx.n
        b_mxp = s_mxp.n
        for cc in range(32):
            meng = "vector"
            msem = s_mx
            tb = cc % 2
            pT = pall[:, 2048 + tb * 512: 2048 + tb * 512 + 512].bitcast(BF16)
            K.wait("tensor", s_sl, b_sl + cc + 1)
            if cc >= 2:
                K.wait("tensor", s_mx, b_mx + cc - 1)

            def tp(e, cc=cc, pT=pT):
                inst = None
                for j in range(4):
                    inst = e.transpose(pT[:, j * 128:(j + 1) * 128],
                                       gfm[:, cc * 512 + j * 128: cc * 512 + (j + 1) * 128],
                                       T["i16sb"][:])
                return inst
            K.step("tensor", s_tp, tp)
            K.wait(meng, s_tp, s_tp.n)

            def mx(e, cc=cc, pT=pT):
                e.tensor_reduce(mxb[:, cc:cc + 1], pT[:, 0:128], AX.X, ALU.max, apply_absolute_value=True)
                e.tensor_reduce(mxb[:, 32 + cc:33 + cc], pT[:, 128:256], AX.X, ALU.max, apply_absolute_value=True)
                e.tensor_reduce(mxb[:, 64 + cc:65 + cc], pT[:, 256:384], AX.X, ALU.max, apply_absolute_value=True)
                return e.tensor_reduce(mxb[:, 96 + cc:97 + cc], pT[:, 384:512], AX.X, ALU.max,
                                       apply_absolute_value=True)
            K.step(meng, msem, mx)
        K.wait("vector", s_mx, s_mx.n)
        smf = s_mx

        def fold(e):
            e.tensor_reduce(gmax[:, 0:1], mxb[:, 0:32], AX.X, ALU.max)
            e.tensor_reduce(gmax[:, 1:2], mxb[:, 32:64], AX.X, ALU.max)
            e.tensor_reduce(gmax[:, 2:3], mxb[:, 64:96], AX.X, ALU.max)
            return e.tensor_reduce(gmax[:, 3:4], mxb[:, 96:128], AX.X, ALU.max)
        K.step("vector", smf, fold)
        K.dep("vector", s_mx, lambda e: e.tensor_scalar(gmax[:, 0:4], gmax[:, 0:4], 1e-5, None, ALU.max))
        K.wait("tensor", s_mx, s_mx.n)
        K.step("tensor", s_sc2, lambda e: e.transpose(pall[0:4, 3072:3200], gmax[:, 0:4],
                                                      T["i32sb"][:]))
        K.wait("vector", s_sc2, s_sc2.n)
        K.dep("vector", s_sc2, lambda e: e.reciprocal(pall[0:4, 3072:3200], pall[0:4, 3072:3200]))
        K.dep("vector", s_sc2, lambda e: e.tensor_scalar(pall[0:4, 3072:3200], pall[0:4, 3072:3200],
                                                         127.0, None, ALU.mult))
        K.dep("vector", s_sc2, lambda e: e.tensor_copy(s2b[0:4, 0:128], pall[0:4, 3072:3200]))
        K.wait("scalar", s_sc2, s_sc2.n)
        K.dma("scalar", s2row[0:1, t0g:t0g + 512].rearrange("o (j q) -> o j q", q=128).squeeze(0),
              s2b[0:4, 0:128], sem=s_s2l)
        K.wait("scalar", s_s2l, s_s2l.n)
        K.wait("scalar", s_qa, s_qa.n)
        K.wait("scalar", s_qap, s_qap.n)
        vv = K.dma("scalar", s2b[:, 0:512], bcast_row(s2row, t0g, 512), sem=s_s2l)
        K.wait("vector", s_s2l, vv)
        K.wait("gpsimd", s_s2l, vv)
        b_qq = s_qq.n
        b_qa, b_qap, b_rc, b_qi, b_qb = s_qa.n, s_qap.n, s_rc.n, s_qi.n, s_qb.n
        for cc in range(32):
            pool_q = cc % 4 in (1, 3)
            eq = "gpsimd" if pool_q else "vector"
            sq = s_qap if pool_q else s_qa
            qts = qtsl[:, (cc % 2) * 512:(cc % 2) * 512 + 512]
            qis = qisl[:, (cc % 2) * 512:(cc % 2) * 512 + 512]
            K.wait(eq, s_sl, b_sl + cc + 1)
            if cc >= 2 or st_i > 0:
                K.wait(eq, s_qi, b_qi + cc - 1)

            def qa(e, cc=cc, qts=qts):
                return e.tensor_tensor(qts, gfm[:, cc * 512:(cc + 1) * 512], s2b[:, 0:512], ALU.mult)
            K.step(eq, sq, qa)
            K.wait("scalar", sq, sq.n)
            K.step("scalar", s_rc, lambda e, qts=qts: e.activation(qts, qts, AF.Copy, bias=RC))
            K.dep("scalar", s_rc, lambda e, qts=qts: e.activation(qts, qts, AF.Copy, bias=-RC))
            K.wait("vector", s_rc, b_rc + 2 * cc + 2)
            if cc >= 2 or st_i > 0:
                K.wait("vector", s_qb, b_qb + cc - 1)
            K.step("vector", s_qi, lambda e, qts=qts, qis=qis: e.tensor_copy(qis, qts))
            K.wait("vector", s_qi, s_qi.n)
            K.step("vector", s_qb,
                   lambda e, cc=cc, qis=qis: e.tensor_copy(gfm[:, cc * 512:(cc + 1) * 512], qis))
        for tt_ in range(4):
            K.dep("vector", s_qq, lambda e, tt_=tt_: e.tensor_scalar(
                gmax[:, 4 + tt_:5 + tt_], gmax[:, tt_:tt_ + 1], 1.0 / 127.0, None, ALU.mult))
            K.dep("vector", s_qq, lambda e, tt_=tt_: e.tensor_tensor(
                gmax[:, 4 + tt_:5 + tt_], gmax[:, 4 + tt_:5 + tt_], scl2[:, 1:2], ALU.mult))
        if itq >= 1:
            K.wait("tensor", s_oe, b_oe0 + itq * 4)
        if itq == 0:
            K.wait("tensor", s_w2r, w2n)
        for cc in range(32):
            K.wait("tensor", s_qb, b_qb + cc + 1)

            def f2(e, cc=cc):
                inst = None
                for tt_ in range(4):
                    for eh in range(2):
                        inst = e.matmul(pall[:, tt_ * 1024 + eh * 512: tt_ * 1024 + eh * 512 + 512],
                                        gfm[:, cc * 512 + tt_ * 128: cc * 512 + tt_ * 128 + 128],
                                        w2sb[:, cc * 1024 + eh * 512: cc * 1024 + eh * 512 + 512],
                                        start=(cc == 0), stop=(cc == 31))
                return inst
            K.step("tensor", s_f2, f2)
        for tt_ in range(4):
            po = pall[:, tt_ * 1024:(tt_ + 1) * 1024]
            trow = t0g + tt_ * 128
            so = tt_ % 2
            if itq * 4 + tt_ >= 2:
                K.wait("scalar", s_ov, b_ov0 + 2 * (itq * 4 + tt_) - 2)
            lr = K.dma("scalar", rst[:, so * 2048:so * 2048 + 1024],
                       T["x_cat"][trow:trow + 128, :], sem=s_rl[so])
            lr2 = K.dma("scalar", rst[:, so * 2048 + 1024:so * 2048 + 2048],
                        T["attn0o"][trow:trow + 128, :], sem=s_rl[so])
            K.wait("scalar", s_f2, s_f2.n)
            K.wait("scalar", s_qq, s_qq.n)
            if itq * 4 + tt_ >= 2:
                K.wait("scalar", s_os[so], ods[itq * 4 + tt_ - 2])

            def oev(e, po=po, so=so, tt_=tt_):
                return e.activation(ost[:, so * 1024:(so + 1) * 1024], po, AF.Copy,
                                    scale=gmax[:, 4 + tt_:5 + tt_])
            K.step("scalar", s_oe, oev)
            K.wait("vector", s_oe, s_oe.n)
            K.wait("vector", s_rl[so], lr2)

            K.dep("vector", s_ov, lambda e, so=so: e.tensor_tensor(
                ost[:, so * 1024:(so + 1) * 1024], ost[:, so * 1024:(so + 1) * 1024],
                rst[:, so * 2048:so * 2048 + 1024], ALU.add))
            K.dep("vector", s_ov, lambda e, so=so: e.tensor_tensor(
                ost[:, so * 1024:(so + 1) * 1024], ost[:, so * 1024:(so + 1) * 1024],
                rst[:, so * 2048 + 1024:so * 2048 + 2048], ALU.add))
            K.wait("gpsimd", s_ov, s_ov.n)
            ods[itq * 4 + tt_] = K.dma("gpsimd", T["x1c"][trow:trow + 128, :],
                                       ost[:, so * 1024:(so + 1) * 1024], sem=s_os[so])
        itq += 1
    K.barrier()


def build_ffn0():
    nc = bass.Bass("TRN2", target_bir_lowering=False, debug=False)
    ins = ["xqt0", "st0", "w1_0t", "convw", "convb", "w2_0t", "x_cat", "attn0o",
           "ident32", "ident16"]
    T = declare(nc, ["xqt0", "st0", "attn0o"], "ExternalInput")
    T.update(declare(nc, ["w1_0t", "convw", "convb", "w2_0t", "x_cat", "ident32",
                          "ident16"], "ExternalInput", {**IN_SPECS}))
    T.update(declare(nc, ["x1c"], "ExternalOutput"))
    T.update(declare(nc, ["h1d"], "Internal"))
    K = KB(nc)
    K.pall = nc.alloc_psum_tensor("pall", [128, 4096], F32)
    i32sb = K.gsb("ident32_sb", [128, 128], F32)
    i16sb = K.gsb("ident16_sb", [128, 128], BF16)
    sldi = K.gsem("ident_ld")
    K.dma("sync", i32sb[:], T["ident32"], sem=sldi)
    K.dma("sync", i16sb[:], T["ident16"], sem=sldi)
    T["i32sb"] = i32sb
    T["i16sb"] = i16sb
    K.barrier()
    emit_ffn0(K, T)
    K.emit()
    return nc, ins, ["x1c"]


def emit_p0(K, T, upto=99):
    nc = K.nc
    wq_sb = K.gsb("wbig", [128, 8 * 3072], BF16)
    s_wld = K.gsem("wld")
    for fc in range(8):
        K.dma("gpsimd", wq_sb[:, fc * 3072:(fc + 1) * 3072], T["w_qkv"][fc * 128:(fc + 1) * 128, :],
              sem=s_wld)
    if "i32sb" in T:
        ident32_sb = T["i32sb"]
    else:
        ident32_sb = K.gsb("ident32_sb", [128, 128], F32)
        K.dma("sync", ident32_sb[:], T["ident32"])
    st_seq = K.gsb("st_rows", [128, 17 * NT_CAT], F32)
    emit_stats(K, T["x_seq"], NT_SEQ, st_seq, False, "s")
    stseq_d = nc.dram_tensor("stseq_d", [6, TS], F32, kind="Internal").ap()
    K.barrier()
    if upto < 2: return
    emit_stat_rows(K, st_seq, NT_SEQ, [1, 3], stseq_d, ident32_sb)
    K.barrier()
    if upto < 3: return
    xnt_seq = K.gsb("xnt_arena", [128, 8 * TCAT], BF16)
    emit_xnt(K, T["x_seq_t"], stseq_d, NT_SEQ, xnt_seq, None, None, False, "s")
    K.barrier()
    if upto < 4: return
    stage = K.gsb("stage_bf", [128, 1024], BF16)
    emit_gemm_fm(K, wq_sb, DM, 3072, xnt_seq, TS, T["k0t"], 8, 0, TS, True, stage, "k")
    K.barrier()
    if upto < 5: return
    s_mm = K.gsem("v0_mm")
    s_ev = K.gsem("v0_ev")
    sstv = [K.gsem("v0_st0"), K.gsem("v0_st1")]
    it = 0
    b_v0e = s_ev.n
    vdmas = {}
    v0v = T["v0"].rearrange("kc k h d -> kc k (h d)")
    for tc in range(NT_SEQ):
        for fo2 in range(2):
            slot = it % 2
            pb = K.pall[:, 1024 + slot * 512: 1024 + (slot + 1) * 512]
            if it >= 2:
                K.wait("tensor", s_ev, b_v0e + it - 1)

            def mm(eng, tc=tc, fo2=fo2, pb=pb):
                inst = None
                for fc in range(8):
                    inst = eng.matmul(pb,
                               xnt_seq[:, fc * TS + tc * 128: fc * TS + (tc + 1) * 128],
                               wq_sb[:, fc * 3072 + 2048 + fo2 * 512: fc * 3072 + 2048 + (fo2 + 1) * 512],
                               start=(fc == 0), stop=(fc == 7))
                return inst
            K.step("tensor", s_mm, mm)
            K.wait("vector", s_mm, s_mm.n)
            if it >= 2:
                K.wait("vector", sstv[slot], vdmas[it - 2])
            sb = stage[:, slot * 512:(slot + 1) * 512]
            K.step("vector", s_ev, lambda e, pb=pb, sb=sb: e.tensor_copy(sb, pb))
            K.wait("sync", s_ev, s_ev.n)
            vdmas[it] = K.dma("sync", v0v[tc, :, fo2 * 512:(fo2 + 1) * 512], sb, sem=sstv[slot])
            it += 1
    K.barrier()
    if upto < 6: return
    st_cat = K.gsb("st_rows", [128, 17 * NT_CAT], F32)
    emit_stats(K, T["x_cat"], NT_CAT, st_cat, True, "c")
    K.barrier()
    emit_stat_rows(K, st_cat, NT_CAT, [0, 1, 2, 3, 4, 5], T["st0"], ident32_sb)
    K.barrier()
    if upto < 7: return
    xnt_cat = K.gsb("xnt_arena", [128, 8 * TCAT], BF16)
    xq_stage = K.gsb("xq_stage", [128, 2 * 1408], BF16)
    emit_xnt(K, T["x_cat_t"], T["st0"], NT_CAT, xnt_cat, xq_stage, T["xqt0"], True, "c")
    K.barrier()
    if upto < 8: return
    emit_gemm_fm(K, wq_sb, 0, 3072, xnt_cat, TCAT, T["q0t"], 8, 0, TS, True, stage, "q")
    K.barrier()


def bias_bcast(K, row_ap, dst_sb, onesr, tag):
    s = K.gsem("bb_mm")
    sl = K.gsem(f"bb_ld_{tag}")
    brow = K.gsb(f"br_{tag}", [1, DM], F32)
    v = K.dma("sync", brow[:], row_ap, sem=sl)
    K.wait("tensor", sl, v)
    K.wait("tensor", K.gsem("bb_ev"), K.gsem("bb_ev").n)
    for half in range(2):
        K.step("tensor", s, lambda e, half=half: e.matmul(
            K.pall[:, 3072 + half * 512:3072 + (half + 1) * 512], onesr[:],
            brow[:, half * 512:(half + 1) * 512], start=True, stop=True))
    K.wait("vector", s, s.n)
    se = K.gsem("bb_ev")
    K.step("vector", se, lambda e: e.tensor_copy(dst_sb[:, 0:512], K.pall[:, 3072:3584]))
    K.step("vector", se, lambda e: e.tensor_copy(dst_sb[:, 512:1024], K.pall[:, 3584:4096]))


def emit_l1p(K, T):
    nc = K.nc
    pall = K.pall
    NT = 32
    st = K.gsb("st_rows", [128, 17 * NT_CAT], F32)
    emit_stats(K, T["x1c"], NT, st, True, "l1")
    M = lambda j: st[:, j * NT:(j + 1) * NT]
    sc = K.gsem("st_sc")
    s_act = K.gsem("st_act")
    K.dep("vector", sc, lambda e: e.tensor_tensor(M(10), M(1), M(1), ALU.mult))
    K.dep("vector", sc, lambda e: e.tensor_scalar(M(10), M(10), -EPS, None, ALU.mult))
    K.dep("vector", sc, lambda e: e.tensor_scalar(M(10), M(10), 1.0 + EPS, None, ALU.add))
    K.wait("scalar", sc, sc.n)
    K.dep("scalar", s_act, lambda e: e.activation(M(11), M(10), AF.Ln))
    K.dep("scalar", s_act, lambda e: e.activation(M(11), M(11), AF.Exp, scale=-0.5))
    K.wait("vector", s_act, s_act.n)
    K.dep("vector", sc, lambda e: e.tensor_tensor(M(12), M(5), M(11), ALU.mult))
    K.dep("vector", sc, lambda e: e.tensor_scalar(M(12), M(12), 127.0, None, ALU.mult))
    K.dep("vector", sc, lambda e: e.tensor_scalar(M(12), M(12), 1e-5, None, ALU.max))
    K.dep("vector", sc, lambda e: e.tensor_scalar(M(14), M(12), 1.0 / 127.0, None, ALU.mult))
    K.dep("vector", sc, lambda e: e.reciprocal(M(12), M(12)))
    K.dep("vector", sc, lambda e: e.tensor_scalar(M(12), M(12), 127.0, None, ALU.mult))
    K.dep("vector", sc, lambda e: e.tensor_tensor(M(12), M(12), M(11), ALU.mult))
    K.dep("vector", sc, lambda e: e.tensor_tensor(M(12), M(12), M(1), ALU.mult))
    K.dep("vector", sc, lambda e: e.tensor_tensor(M(13), M(0), M(12), ALU.mult))
    K.dep("vector", sc, lambda e: e.tensor_scalar(M(15), M(3), -1.0, None, ALU.mult))
    K.dep("vector", sc, lambda e: e.tensor_scalar(M(16), M(13), -1.0, None, ALU.mult))
    s_t = K.gsem("srow_t"); s_e = K.gsem("srow_e"); s_d = K.gsem("srow_d")
    stage = K.gsb("strow_stage", [128, 128], F32)
    K.wait("tensor", sc, sc.n)
    K.wait("tensor", s_e, s_e.n)
    K.step("tensor", s_t, lambda e: e.transpose(pall[0:16, 3584:3712],
                                                st[:, 14 * NT:14 * NT + 16], T["i32sb"][:]))
    K.wait("vector", s_t, s_t.n)
    K.wait("vector", s_d, s_d.n)
    K.step("vector", s_e, lambda e: e.tensor_copy(stage[0:16, :], pall[0:16, 3584:3712]))
    K.wait("scalar", s_e, s_e.n)
    K.dma("scalar", T["st1"][5:6, :].rearrange("o (i q) -> o i q", q=128).squeeze(0),
          stage[0:16, :], sem=s_d)
    K.barrier()
    xnt1 = K.gsb("xnt_arena", [128, 8 * TCAT], BF16)
    x1b = K.gsb("scr16k", [128, 4096], F32)
    tokb = K.gsb("scr8k", [128, 2048], F32)[:].bitcast(BF16)
    qtm = K.gsb("scr4k", [128, 1024], F32)[:]
    qi3 = K.gsb("qi_arena", [128, DF], I32)[:, 0:1024]
    xstg = K.gsb("l1_xstg", [128, 2048], BF16)
    s_xl = [K.gsem("l1_xl0"), K.gsem("l1_xl1")]
    s_xn = K.gsem("l1_xn")
    s_qc = K.gsem("l1_qc")
    s_tp = K.gsem("l1_tp")
    s_te = K.gsem("l1_te")
    s_qtp = K.gsem("l1_qtp")
    s_qte = K.gsem("l1_qte")
    s_qst = [K.gsem("l1_qst0"), K.gsem("l1_qst1")]
    b_xn, b_tp, b_te, b_qc, b_qtp, b_qte = s_xn.n, s_tp.n, s_te.n, s_qc.n, s_qtp.n, s_qte.n
    xls = {}
    qsts = {}
    for i in range(NT):
        bs = i % 2
        own = i < 16
        if i >= 2:
            K.wait("sync", s_xn, b_xn + 2 * (i - 1))
            K.wait("sync", s_qc, s_qc.n)
        xls[i] = K.dma("sync", x1b[:, bs * 1024:(bs + 1) * 1024],
                       T["x1c"][i * 128:(i + 1) * 128, :], sem=s_xl[bs])
        K.wait("vector", s_xl[bs], xls[i])
        if i >= 2:
            K.wait("vector", s_tp, b_tp + (i - 1) * 2)
        K.dep("vector", s_xn, lambda e, i=i, bs=bs: e.tensor_scalar(
            qtm, x1b[:, bs * 1024:(bs + 1) * 1024], M(1)[:, i:i + 1], None, ALU.mult))
        K.dep("vector", s_xn, lambda e, i=i, bs=bs: e.tensor_scalar(
            tokb[:, bs * 2048:bs * 2048 + 1024], qtm, M(3)[:, i:i + 1], None, ALU.subtract))
        K.wait("tensor", s_xn, s_xn.n)
        for half in range(2):
            pT = pall[:, half * 512:half * 512 + 256].bitcast(BF16)
            if i * 2 + half >= 2:
                K.wait("tensor", s_te, b_te + i * 2 + half - 1)

            def tp4(e, i=i, bs=bs, half=half, pT=pT):
                inst = None
                for j in range(4):
                    fc = half * 4 + j
                    inst = e.transpose(pT[:, j * 128:(j + 1) * 128],
                                       tokb[:, bs * 2048 + fc * 128: bs * 2048 + (fc + 1) * 128],
                                       T["i16sb"][:])
                return inst
            K.step("tensor", s_tp, tp4)
            K.wait("vector", s_tp, s_tp.n)
            dv = xnt1[:, 0:8 * TS].rearrange("p (fc t) -> p fc t", t=TS)[
                :, half * 4:(half + 1) * 4, i * 128:(i + 1) * 128]
            K.step("vector", s_te, lambda e, pT=pT, dv=dv: e.tensor_copy(
                dv, pT.rearrange("p (fc t) -> p fc t", t=128)))
        if own:
            K.wait("vector", s_xl[bs], xls[i])
            K.wait("vector", s_xn, s_xn.n)
            if i >= 2:
                K.wait("vector", s_qtp, b_qtp + (i - 1) * 2)
            K.dep("vector", s_qc, lambda e, i=i, bs=bs: e.tensor_scalar(
                qtm, x1b[:, bs * 1024:(bs + 1) * 1024], M(12)[:, i:i + 1], None, ALU.mult))
            K.dep("vector", s_qc, lambda e, i=i: e.tensor_scalar(
                qtm, qtm, M(13)[:, i:i + 1], None, ALU.subtract))
            K.dep("vector", s_qc, lambda e: e.tensor_scalar(qtm, qtm, RC, None, ALU.add))
            K.dep("vector", s_qc, lambda e: e.tensor_scalar(qtm, qtm, -RC, None, ALU.add))
            K.dep("vector", s_qc, lambda e: e.tensor_copy(qi3, qtm))
            K.dep("vector", s_qc, lambda e, bs=bs: e.tensor_copy(
                tokb[:, bs * 2048 + 1024: bs * 2048 + 2048], qi3))
            K.wait("tensor", s_qc, s_qc.n)
            for half in range(2):
                pT = pall[:, 1024 + half * 512: 1024 + half * 512 + 256].bitcast(BF16)
                if i * 2 + half >= 2:
                    K.wait("tensor", s_qte, b_qte + i * 2 + half - 1)

                def tq4(e, i=i, bs=bs, half=half, pT=pT):
                    inst = None
                    for j in range(4):
                        fc = half * 4 + j
                        inst = e.transpose(pT[:, j * 128:(j + 1) * 128],
                                           tokb[:, bs * 2048 + 1024 + fc * 128:
                                                bs * 2048 + 1024 + (fc + 1) * 128],
                                           T["i16sb"][:])
                    return inst
                K.step("tensor", s_qtp, tq4)
                K.wait("vector", s_qtp, s_qtp.n)
                if i >= 2 and half == 0:
                    K.wait("vector", s_qst[bs], qsts[i - 2])
                K.step("vector", s_qte, lambda e, pT=pT, bs=bs, half=half: e.tensor_copy(
                    xstg[:, bs * 1024 + half * 512: bs * 1024 + (half + 1) * 512], pT))
            K.wait("gpsimd", s_qte, s_qte.n)
            dstq = T["xqt1"][:, i * 128:(i + 1) * 128].rearrange("(fc p) t -> p fc t", p=128)
            qsts[i] = K.dma("gpsimd", dstq,
                            xstg[:, bs * 1024:(bs + 1) * 1024].rearrange("p (fc t) -> p fc t", t=128),
                            sem=s_qst[bs])
    K.barrier()
    wb = K.gsb("wbig", [128, 8 * 3072], BF16)
    onesr = K.gsb("ones_row", [1, 128], F32)
    smi = K.gsem("wq_misc")
    K.step("vector", smi, lambda e: e.memset(onesr[:], 1.0))
    bkbc = K.gsb("bkbc", [128, DM], F32)
    bvbc = K.gsb("bvbc", [128, DM], F32)
    K.wait("tensor", smi, smi.n)
    bias_bcast(K, T["bk1r"], bkbc, onesr, "bk")
    bias_bcast(K, T["bv1r"], bvbc, onesr, "bv")
    bqp = K.gsb("bqp", [128, 16], F32)
    swld = K.gsem("l1_wld")
    K.dma("sync", bqp[:, 0:8], T["bq1p"], sem=swld)
    K.dma("sync", bqp[:, 8:16], T["bk1p"], sem=swld)
    s_pm = K.gsem("l1_pmm")
    s_pe = K.gsem("l1_pev")
    s_ps = [K.gsem("l1_pst0"), K.gsem("l1_pst1")]
    pstg = K.gsb("l1_xstg", [128, 2048], BF16)
    swldg = K.gsem("l1_wldg")
    for wname, dram, bcol0 in (("wq1t", "q1t", 0), ("wk1t", "k1t", 8)):
        for fc in range(8):
            K.dma("gpsimd", wb[:, fc * 1024:(fc + 1) * 1024],
                  T[wname][fc * 128:(fc + 1) * 128, :], sem=swldg)
        K.barrier()
        K.wait("scalar", swld, swld.n)
        it = 0
        pds = {}
        for fo in range(8):
            for tch in range(4):
                sl = it % 2
                pb = pall[:, 1024 + sl * 512:1024 + (sl + 1) * 512]
                if it >= 2:
                    K.wait("tensor", s_pe, s_pe.n - 1)

                def mm(e, fo=fo, tch=tch, pb=pb):
                    inst = None
                    for fc in range(8):
                        inst = e.matmul(pb, wb[:, fc * 1024 + fo * 128:fc * 1024 + (fo + 1) * 128],
                                        xnt1[:, fc * TS + tch * 512:fc * TS + (tch + 1) * 512],
                                        start=(fc == 0), stop=(fc == 7))
                    return inst
                K.step("tensor", s_pm, mm)
                K.wait("vector", s_pm, s_pm.n)
                if it >= 2:
                    K.wait("vector", s_ps[sl], pds[it - 2])

                def ev(e, pb=pb, sl=sl, fo=fo, bcol0=bcol0):
                    return e.tensor_scalar(pstg[:, sl * 512:(sl + 1) * 512], pb,
                                           bqp[:, bcol0 + fo:bcol0 + fo + 1], None, ALU.add)
                K.step("vector", s_pe, ev)
                K.wait("gpsimd", s_pe, s_pe.n)
                pds[it] = K.dma("gpsimd", T[dram][fo * 128:(fo + 1) * 128, tch * 512:(tch + 1) * 512],
                                pstg[:, sl * 512:(sl + 1) * 512], sem=s_ps[sl])
                it += 1
    s_pv = K.gsem("l1_pv")
    for wname, dram, bbc in (("wk1t", "k1m", bkbc), ("wv1t", "v1m", bvbc)):
        for fc in range(8):
            K.dma("gpsimd", wb[:, fc * 1024:(fc + 1) * 1024],
                  T[wname][fc * 128:(fc + 1) * 128, :], sem=swldg)
        K.barrier()
        it = 0
        pds = {}
        for tc in range(NT):
            for fo2 in range(2):
                sl = it % 2
                pb = pall[:, 2048 + sl * 512:2048 + (sl + 1) * 512]
                if it >= 2:
                    K.wait("tensor", s_pv, s_pv.n - 1)

                def mm(e, tc=tc, fo2=fo2, pb=pb):
                    inst = None
                    for fc in range(8):
                        inst = e.matmul(pb, xnt1[:, fc * TS + tc * 128:fc * TS + (tc + 1) * 128],
                                        wb[:, fc * 1024 + fo2 * 512:fc * 1024 + (fo2 + 1) * 512],
                                        start=(fc == 0), stop=(fc == 7))
                    return inst
                K.step("tensor", s_pm, mm)
                K.wait("vector", s_pm, s_pm.n)
                if it >= 2:
                    K.wait("vector", s_ps[sl], pds[it - 2])

                def ev(e, pb=pb, sl=sl, fo2=fo2, bbc=bbc):
                    return e.tensor_tensor(pstg[:, sl * 512:(sl + 1) * 512], pb,
                                           bbc[:, fo2 * 512:(fo2 + 1) * 512], ALU.add)
                K.step("vector", s_pv, ev)
                K.wait("gpsimd", s_pv, s_pv.n)
                pds[it] = K.dma("gpsimd", T[dram][tc * 128:(tc + 1) * 128, fo2 * 512:(fo2 + 1) * 512],
                                pstg[:, sl * 512:(sl + 1) * 512], sem=s_ps[sl])
                it += 1
    K.barrier()
    kvb = K.gsb("scr16k", [128, 4096], F32)[:].bitcast(BF16)
    s_kl = [K.gsem("l1_kl0"), K.gsem("l1_kl1")]
    s_sm = K.gsem("l1_smm")
    s_sa = K.gsem("l1_sacc")
    rms = K.gsb("recv_sb", [64, 1], F32)
    acc = K.gsb("spre_acc", [64, DM], F32)
    K.dma("sync", rms[:], T["recvmask"], sem=swld)
    sini = K.gsem("l1_sini")
    K.step("vector", sini, lambda e: e.memset(acc[:], 0.0))
    K.wait("tensor", sini, sini.n)
    pview = pall[0:64, :].rearrange("p (b c) -> p b c", c=512)[:, :, 0:64]
    kls = {}
    b_sm, b_sa = s_sm.n, s_sa.n
    for ci in range(16):
        bs = ci % 2
        if ci >= 2:
            K.wait("sync", s_sm, b_sm + 2 * (ci - 1))
        K.dma("sync", kvb[:, bs * 4096:bs * 4096 + 1024],
              T["k1m"][2048 + ci * 128:2048 + (ci + 1) * 128, :], sem=s_kl[bs])
        kls[ci] = K.dma("sync", kvb[:, bs * 4096 + 1024:bs * 4096 + 2048],
                        T["v1m"][2048 + ci * 128:2048 + (ci + 1) * 128, :], sem=s_kl[bs])
        K.wait("tensor", s_kl[bs], kls[ci])
        for r in range(2):
            if ci + r > 0:
                K.wait("tensor", s_sa, b_sa + 2 * ci + r - 1 + 1)

            def smm(e, ci=ci, bs=bs, r=r):
                inst = None
                for hl in range(8):
                    h = r * 8 + hl
                    inst = e.matmul(pall[0:64, hl * 512:hl * 512 + 64],
                                    kvb[:, bs * 4096 + h * 64:bs * 4096 + (h + 1) * 64],
                                    kvb[:, bs * 4096 + 1024 + h * 64:bs * 4096 + 1024 + (h + 1) * 64],
                                    start=True, stop=True)
                return inst
            K.step("tensor", s_sm, smm)
            K.wait("vector", s_sm, s_sm.n)
            av = acc[:, r * 512:(r + 1) * 512].rearrange("p (h c) -> p h c", c=64)
            K.dep("vector", s_sa, lambda e, av=av: e.tensor_tensor(av, av, pview, ALU.add))
    sse = K.gsem("l1_sev")
    K.wait("vector", s_sa, s_sa.n)
    K.wait("vector", swld, swld.n)
    spb = K.gsb("spre_sb", [64, DM], F32)
    K.dep("vector", sse, lambda e: e.tensor_scalar(spb[:], acc[:], rms[:], None, ALU.mult))
    K.wait("gpsimd", sse, sse.n)
    K.dma("gpsimd", T["spre"], spb[:], sem=K.gsem("l1_sst"))
    K.barrier()


def build_l1p():
    nc = bass.Bass("TRN2", target_bir_lowering=False, debug=False)
    ins = ["x1c", "wq1t", "wk1t", "wv1t", "bq1p", "bk1p", "bk1r", "bv1r",
           "recvmask", "ident32", "ident16"]
    T = declare(nc, ["x1c"], "ExternalInput")
    T.update(declare(nc, ins[1:], "ExternalInput", {**IN_SPECS}))
    T.update(declare(nc, ["q1t", "k1t", "k1m", "v1m", "xqt1", "st1", "spre"],
                     "ExternalOutput"))
    K = KB(nc)
    K.pall = nc.alloc_psum_tensor("pall", [128, 4096], F32)
    i32sb = K.gsb("ident32_sb", [128, 128], F32)
    i16sb = K.gsb("ident16_sb", [128, 128], BF16)
    sldi = K.gsem("ident_ld")
    K.dma("sync", i32sb[:], T["ident32"], sem=sldi)
    K.dma("sync", i16sb[:], T["ident16"], sem=sldi)
    T["i32sb"] = i32sb
    T["i16sb"] = i16sb
    K.barrier()
    emit_l1p(K, T)
    K.emit()
    return nc, ins, ["q1t", "k1t", "k1m", "v1m", "xqt1", "st1", "spre"]


def emit_att1(K, T, upto=99):
    nc = K.nc
    pall = K.pall
    k1sb = K.gsb("att1_k1t", [128, 8 * 2048], BF16)
    at1 = K.gsb("attn1T", [128, 8 * 2048], BF16)
    ptt = K.gsb("att1_pt", [128, 512], BF16)
    kv = K.gsb("att1_kv", [128, 2 * 2048], BF16)
    mask2 = K.gsb("mask2", [128, 256], F32)
    sext = K.gsb("sext", [64, DM], F32)
    snap = K.gsb("snap", [64, DM], BF16)
    s_ld = K.gsem("a1_ld")
    s_kl = [K.gsem("a1_kl0"), K.gsem("a1_kl1")]
    s_sc = K.gsem("a1_sc")
    s_mk = K.gsem("a1_mk")
    s_pv = K.gsem("a1_pv")
    s_ev = K.gsem("a1_ev")
    s_st = K.gsem("a1_st")
    s_sa = K.gsem("a1_sa")
    sacc = K.gsb("sacc", [64, DM], F32)
    q1t_sb = K.gsb("att1_q1t", [128, 8 * 2048], BF16)
    q1tB = K.gsb("att1_q1tB", [64, 8 * 2048], BF16)
    for fc in range(8):
        K.dma("sync", q1t_sb[:, fc * 2048:(fc + 1) * 2048],
              T["q1t"][fc * 128:(fc + 1) * 128, :], sem=s_ld)
        K.dma("sync", q1tB[:, fc * 2048:(fc + 1) * 2048],
              T["q1t"][fc * 128 + 64:(fc + 1) * 128, :], sem=s_ld)
        K.dma("sync", k1sb[:, fc * 2048:(fc + 1) * 2048],
              T["k1t"][fc * 128:(fc + 1) * 128, :], sem=s_ld)
    K.dma("sync", mask2[:, 0:128], T["triu"], sem=s_ld)
    K.dma("sync", mask2[:, 128:256], T["triu"], sem=s_ld)
    K.dma("sync", sext[:], T["spre"], sem=s_ld)
    K.wait("vector", s_ld, s_ld.n)
    sini = K.gsem("a1_ini")
    K.step("vector", sini, lambda e: e.tensor_copy(snap[:], sext[:]))
    K.dep("vector", sini, lambda e: e.memset(sacc[:], 0.0))
    K.wait("tensor", sini, sini.n)
    pview = pall[0:64, :].rearrange("p (b c) -> p b c", c=512)[:, :, 0:64]
    b_sc, b_mk, b_pv, b_ev, b_st, b_sa = s_sc.n, s_mk.n, s_pv.n, s_ev.n, s_st.n, s_sa.n
    if upto < 1:
        K.barrier()
        return
    kls = {}
    it = 0
    NCI = 16 if upto >= 2 else 2
    for ci in range(NCI):
        bs = ci % 2
        if ci >= 2 and upto >= 4:
            K.wait("sync", s_st, b_st + 2 * (ci - 1))
        if ci >= 2 and upto >= 3:
            K.wait("sync", s_pv, b_pv + (ci - 1) * 8)
        K.dma("sync", kv[:, bs * 2048:bs * 2048 + 1024],
              T["k1m"][ci * 128:(ci + 1) * 128, :], sem=s_kl[bs])
        kls[ci] = K.dma("sync", kv[:, bs * 2048 + 1024:bs * 2048 + 2048],
                        T["v1m"][ci * 128:(ci + 1) * 128, :], sem=s_kl[bs])
        K.wait("tensor", s_kl[bs], kls[ci])
        if ci >= 1 and upto >= 4:
            K.wait("tensor", s_sa, b_sa + 3 * ci)
        for hp in range(8):
            ss = it % 2
            pSv = pall[:, ss * 1024:(ss + 1) * 1024].rearrange(
                "p (b c) -> p b c", c=512)[:, :, 0:128]
            if it >= 2:
                K.wait("tensor", s_mk, b_mk + it - 1)

            def scm(e, ci=ci, hp=hp, ss=ss):
                e.matmul(pall[:, ss * 1024:ss * 1024 + 128],
                         k1sb[0:64, hp * 2048 + ci * 128:hp * 2048 + (ci + 1) * 128],
                         q1t_sb[0:64, hp * 2048 + ci * 128:hp * 2048 + (ci + 1) * 128],
                         start=True, stop=True, tile_position=(0, 0))
                return e.matmul(pall[:, ss * 1024 + 512:ss * 1024 + 640],
                                k1sb[64:128, hp * 2048 + ci * 128:hp * 2048 + (ci + 1) * 128],
                                q1t_sb[64:128, hp * 2048 + ci * 128:hp * 2048 + (ci + 1) * 128],
                                start=True, stop=True, tile_position=(64, 0))
            K.step("tensor", s_sc, scm)
            K.wait("vector", s_sc, s_sc.n)
            if it >= 2 and upto >= 3:
                K.wait("vector", s_pv, b_pv + it - 1)
            K.step("vector", s_mk, lambda e, pSv=pSv, ss=ss: e.tensor_tensor(
                ptt[:, ss * 256:(ss + 1) * 256].rearrange("p (b c) -> p b c", c=128),
                pSv, mask2[:].rearrange("p (b c) -> p b c", c=128), ALU.mult))
            poA = pall[0:64, 2048 + ss * 1024:2048 + ss * 1024 + 128]
            poB = pall[0:64, 2560 + ss * 1024:2560 + ss * 1024 + 128]
            if upto < 3:
                it += 1
                continue
            K.wait("tensor", s_mk, s_mk.n)
            if it >= 2:
                K.wait("tensor", s_ev, b_ev + it - 1)

            def pvm(e, ci=ci, hp=hp, ss=ss, bs=bs, poA=poA, poB=poB):
                hA, hB = 2 * hp, 2 * hp + 1
                e.matmul(poA, kv[:, bs * 2048 + 1024 + hA * 64:bs * 2048 + 1024 + hA * 64 + 64],
                         ptt[:, ss * 256:ss * 256 + 128], start=True, stop=False)
                e.matmul(poA, snap[:, hA * 64:(hA + 1) * 64],
                         q1t_sb[0:64, hp * 2048 + ci * 128:hp * 2048 + (ci + 1) * 128],
                         start=False, stop=True)
                e.matmul(poB, kv[:, bs * 2048 + 1024 + hB * 64:bs * 2048 + 1024 + hB * 64 + 64],
                         ptt[:, ss * 256 + 128:ss * 256 + 256], start=True, stop=False)
                return e.matmul(poB, snap[:, hB * 64:(hB + 1) * 64],
                                q1tB[:, hp * 2048 + ci * 128:hp * 2048 + (ci + 1) * 128],
                                start=False, stop=True)
            K.step("tensor", s_pv, pvm)
            K.wait("vector", s_pv, s_pv.n)

            def ev2(e, ci=ci, hp=hp, poA=poA, poB=poB):
                e.tensor_copy(at1[0:64, hp * 2048 + ci * 128:hp * 2048 + (ci + 1) * 128], poA)
                return e.tensor_copy(at1[64:128, hp * 2048 + ci * 128:hp * 2048 + (ci + 1) * 128], poB)
            K.step("vector", s_ev, ev2)
            it += 1
        if upto < 4:
            continue
        K.wait("tensor", s_ev, s_ev.n)
        K.wait("tensor", s_mk, s_mk.n)
        for r in range(2):
            if ci + r > 0:
                K.wait("tensor", s_sa, b_sa + 3 * ci + (1 if r == 1 else -1))

            def stm(e, ci=ci, bs=bs, r=r):
                inst = None
                for hl in range(8):
                    h = r * 8 + hl
                    inst = e.matmul(pall[0:64, hl * 512:hl * 512 + 64],
                                    kv[:, bs * 2048 + h * 64:bs * 2048 + (h + 1) * 64],
                                    kv[:, bs * 2048 + 1024 + h * 64:bs * 2048 + 1024 + (h + 1) * 64],
                                    start=True, stop=True)
                return inst
            K.step("tensor", s_st, stm)
            K.wait("vector", s_st, s_st.n)
            av = sacc[:, r * 512:(r + 1) * 512].rearrange("p (h c) -> p h c", c=64)
            K.dep("vector", s_sa, lambda e, av=av: e.tensor_tensor(av, av, pview, ALU.add))
        K.wait("vector", s_pv, s_pv.n)
        K.dep("vector", s_sa, lambda e: e.tensor_tensor(snap[:], sacc[:], sext[:], ALU.add))
    K.barrier()
    if upto < 5:
        return
    wb = K.gsb("wo_sb", [128, 8 * 1024], BF16)
    onesr = K.gsb("ones_row", [1, 128], F32)
    smi = K.gsem("wq_misc")
    K.step("vector", smi, lambda e: e.memset(onesr[:], 1.0))
    K.wait("tensor", smi, smi.n)
    bobc = K.gsb("bobc", [128, DM], F32)
    bias_bcast(K, T["bo1r"], bobc, onesr, "bo")
    swld = K.gsem("l1_wldg")
    for fc in range(8):
        K.dma("gpsimd", wb[:, fc * 1024:(fc + 1) * 1024],
              T["wo1t"][fc * 128:(fc + 1) * 128, :], sem=swld)
    K.barrier()
    s_pm = K.gsem("op_mm")
    s_oe = K.gsem("op_ev")
    s_os = [K.gsem("op_st0"), K.gsem("op_st1")]
    ost = K.gsb("scr8k", [128, 2048], F32)[:]
    ods = {}
    K.wait("tensor", K.gsem("bb_ev"), K.gsem("bb_ev").n)
    for tc in range(16):
        sl = tc % 2
        pb2 = pall[:, 2048 + sl * 1024:2048 + (sl + 1) * 1024]
        if tc >= 2:
            K.wait("tensor", s_oe, s_oe.n - 1)

        def mm(e, tc=tc, pb2=pb2):
            inst = None
            for fc in range(8):
                for eh in range(2):
                    inst = e.matmul(pb2[:, eh * 512:(eh + 1) * 512],
                                    at1[:, fc * 2048 + tc * 128:fc * 2048 + (tc + 1) * 128],
                                    wb[:, fc * 1024 + eh * 512:fc * 1024 + (eh + 1) * 512],
                                    start=(fc == 0), stop=(fc == 7))
            return inst
        K.step("tensor", s_pm, mm)
        K.wait("vector", s_pm, s_pm.n)
        if tc >= 2:
            K.wait("vector", s_os[sl], ods[tc - 2])
        K.dep("vector", s_oe, lambda e, pb2=pb2, sl=sl: e.tensor_tensor(
            ost[:, sl * 1024:(sl + 1) * 1024], pb2, bobc[:], ALU.add))
        K.wait("gpsimd", s_oe, s_oe.n)
        ods[tc] = K.dma("gpsimd", T["attn1o"][tc * 128:(tc + 1) * 128, :],
                        ost[:, sl * 1024:(sl + 1) * 1024], sem=s_os[sl])
    K.barrier()


def build_att1(upto=99):
    nc = bass.Bass("TRN2", target_bir_lowering=False, debug=False)
    ins = ["q1t", "k1t", "k1m", "v1m", "spre", "wo1t", "bo1r", "triu"]
    T = declare(nc, ["q1t", "k1t", "k1m", "v1m", "spre"], "ExternalInput")
    T.update(declare(nc, ["wo1t", "bo1r", "triu"], "ExternalInput", {**IN_SPECS}))
    T.update(declare(nc, ["attn1o"], "ExternalOutput"))
    K = KB(nc)
    K.pall = nc.alloc_psum_tensor("pall", [128, 4096], F32)
    emit_att1(K, T, upto)
    K.emit()
    return nc, ins, ["attn1o"]


def emit_ffn1(K, T, upto=99):
    nc = K.nc
    pall = K.pall
    w1q = K.gsb("w1q", [128, 8 * DF], BF16)
    w2qd = nc.dram_tensor("w2qd1", [DF, 1024], BF16, kind="Internal").ap()
    scl1 = emit_wquant(K, T["w1_1t"], 8, DF, w1q, "w11")
    K.barrier()
    scl2 = emit_wquant(K, T["w2_1t"], 32, 1024, None, "w21", dram_out=w2qd)
    K.barrier()
    s2row = nc.dram_tensor("s2row1", [1, TH], F32, kind="Internal").ap()
    onesr = K.gsb("ones_row", [1, 128], F32)
    smi = K.gsem("wq_misc")
    K.step("vector", smi, lambda e: e.memset(onesr[:], 1.0))
    K.wait("tensor", smi, smi.n)
    b2bc = K.gsb("b2bc", [128, DM], F32)
    bias_bcast(K, T["b2r"], b2bc, onesr, "b2")
    xq1sb = K.gsb("att1_q1t", [128, 8 * 2048], BF16)
    b1sb = K.gsb("b1sb", [128, 32], F32)
    swld = K.gsem("l1_wld")
    for fc in range(8):
        K.dma("sync", xq1sb[:, fc * 2048:(fc + 1) * 2048],
              T["xqt1"][fc * 128:(fc + 1) * 128, :], sem=swld)
    K.dma("sync", b1sb[:], T["b1p"], sem=swld)
    K.barrier()
    dsb = K.gsb("scr4k", [128, 1024], F32)[:, 0:512]
    hst = K.gsb("scr8k", [128, 2048], F32)[:, 0:1024]
    gfm = K.gsb("scr32k", [128, 2 * DF], F32)[:].bitcast(BF16)
    gmax = K.gsb("gmax", [128, 24], F32)
    sqs = K.gsb("ff1_sq", [128, 2 * 512], F32)
    s2b = K.gsb("scr4k", [128, 1024], F32)[:, 512:1024]
    qtsl = K.gsb("qi_arena", [128, DF], I32)[:, 1024:2048].bitcast(F32)
    qisl = K.gsb("qi_arena", [128, DF], I32)[:, 2048:3072]
    w2s = K.gsb("w2s", [128, 2048], BF16)
    gtok = K.gsb("f1_gtok", [128, 2 * 512], BF16)
    mxb = K.gsb("mxbuf", [128, 128], F32)
    sqb2 = K.gsb("f1_sqb2", [128, 128], F32)
    rst = K.gsb("scr16k", [128, 4096], F32)[:]
    ost2 = K.gsb("f1_ost", [128, 2048], F32)
    s_dl = K.gsem("f1_dl")
    s_hm = K.gsem("f1_hmm")
    s_he = K.gsem("f1_hev")
    s_hd = K.gsem("f1_hdv")
    s_sl = K.gsem("f1_gelu")
    s_tp = K.gsem("f1_tp")
    s_mx = K.gsem("f1_mx")
    s_qq = K.gsem("f1_qq")
    s_qa = K.gsem("f1_qa")
    s_qap = K.gsem("f1_qap")
    s_rc = K.gsem("f1_rc")
    s_qi = K.gsem("f1_qi")
    s_qb = K.gsem("f1_qb")
    s_f2 = K.gsem("f1_f2")
    s_oe = K.gsem("f1_oe")
    s_ov = K.gsem("f1_ov")
    s_sc2 = K.gsem("f1_s2")
    s_s2l = K.gsem("f1_s2l")
    s_w2l = [K.gsem("f1_w2l0"), K.gsem("f1_w2l1")]
    s_os = [K.gsem("f1_os0"), K.gsem("f1_os1")]
    s_rl = [K.gsem("f1_rl0"), K.gsem("f1_rl1")]
    ods = {}
    b_oe0, b_ov0 = s_oe.n, s_ov.n
    b_f2_prev = 0
    K.wait("tensor", K.gsem("bb_ev"), K.gsem("bb_ev").n)
    if upto < 2:
        K.barrier()
        return
    NSI = 4 if upto >= 5 else 1
    for si in range(NSI):
        t0 = si * 512
        pf2 = b_f2_prev
        b_f2_prev = s_f2.n
        if si > 0:
            K.wait("tensor", s_oe, b_oe0 + si * 4)
        dv = K.dma("scalar", dsb, bcast_row(T["st1"][5:6, :], t0, 512), sem=s_dl)
        b_sl = s_sl.n
        b_he = s_he.n
        b_hd = s_hd.n
        for cc in range(32):
            ps = pall[:, (cc % 2) * 512:(cc % 2) * 512 + 512]
            hsl = hst[:, (cc % 2) * 512:(cc % 2) * 512 + 512]
            if cc >= 2:
                K.wait("tensor", s_he, b_he + cc - 1)

            def mm(e, cc=cc, t0=t0, ps=ps):
                inst = None
                for fc in range(8):
                    inst = e.matmul(ps, w1q[:, fc * DF + cc * 128:fc * DF + cc * 128 + 128],
                                    xq1sb[:, fc * 2048 + t0:fc * 2048 + t0 + 512],
                                    start=(fc == 0), stop=(fc == 7))
                return inst
            K.step("tensor", s_hm, mm)
            K.wait("scalar", s_hm, s_hm.n)
            if cc >= 2:
                K.wait("scalar", s_sl, b_sl + cc - 1)

            def ev(e, ps=ps, hsl=hsl):
                return e.activation(hsl, ps, AF.Copy, scale=scl1[:, 1:2])
            K.step("scalar", s_he, ev)
            K.wait("vector", s_he, s_he.n)
            if cc == 0:
                K.wait("vector", s_dl, dv)
            K.step("vector", s_hd, lambda e, hsl=hsl: e.tensor_tensor(hsl, hsl, dsb, ALU.mult))
            K.dep("vector", s_hd, lambda e, cc=cc, hsl=hsl: e.tensor_scalar(
                hsl, hsl, b1sb[:, cc:cc + 1], None, ALU.add))
            K.wait("scalar", s_hd, b_hd + 2 * cc + 2)
            if si > 0:
                K.wait("scalar", s_f2, pf2 + cc + 1)

            def gl(e, cc=cc, hsl=hsl):
                return e.activation(gfm[:, cc * 512:(cc + 1) * 512], hsl, AF.Gelu)
            K.step("scalar", s_sl, gl)
        if upto < 3:
            continue
        b_tp, b_mx = s_tp.n, s_mx.n
        s_pe2 = K.gsem("f1_pev")
        s_sq2 = K.gsem("f1_sq2")
        s_ssm = K.gsem("f1_ssm")
        b_pe2, b_sq2, b_ssm = s_pe2.n, s_sq2.n, s_ssm.n
        for cc in range(32):
            tb = cc % 2
            pT = pall[:, 2048 + tb * 512:2048 + tb * 512 + 256].bitcast(BF16)
            gsl = gtok[:, tb * 512:tb * 512 + 512]
            ssl = sqs[:, tb * 512:tb * 512 + 512]
            K.wait("tensor", s_sl, b_sl + cc + 1)
            if cc >= 2:
                K.wait("tensor", s_pe2, b_pe2 + cc - 1)

            def tp(e, cc=cc, pT=pT):
                inst = None
                for j in range(4):
                    inst = e.transpose(pT[:, j * 128:(j + 1) * 128],
                                       gfm[:, cc * 512 + j * 128:cc * 512 + (j + 1) * 128],
                                       T["i16sb"][:])
                return inst
            K.step("tensor", s_tp, tp)
            K.wait("vector", s_tp, s_tp.n)
            if cc >= 2:
                K.wait("vector", s_mx, b_mx + cc - 1)
                K.wait("vector", s_sq2, b_sq2 + cc - 1)
            elif si > 0:
                K.wait("vector", s_sq2, b_sq2)
            K.step("vector", s_pe2, lambda e, pT=pT, gsl=gsl: e.tensor_copy(gsl, pT[:, 0:512]))
            K.wait("vector", s_pe2, s_pe2.n)

            def mx(e, cc=cc, gsl=gsl):
                inst = None
                for j in range(4):
                    inst = e.tensor_reduce(mxb[:, j * 32 + cc:j * 32 + cc + 1],
                                           gsl[:, j * 128:(j + 1) * 128],
                                           AX.X, ALU.max, apply_absolute_value=True)
                return inst
            K.step("vector", s_mx, mx)
            K.wait("scalar", s_pe2, s_pe2.n)
            if cc >= 2:
                K.wait("scalar", s_ssm, b_ssm + cc - 1)
            elif si > 0:
                K.wait("scalar", s_ssm, b_ssm)
            K.step("scalar", s_sq2, lambda e, gsl=gsl, ssl=ssl: e.activation(ssl, gsl, AF.Square))
            K.wait("vector", s_sq2, s_sq2.n)

            def sms(e, cc=cc, ssl=ssl):
                inst = None
                for j in range(4):
                    inst = e.reduce_sum(sqb2[:, j * 32 + cc:j * 32 + cc + 1],
                                        ssl[:, j * 128:(j + 1) * 128], axis=AX.X)
                return inst
            K.step("vector", s_ssm, sms)
        K.wait("vector", s_mx, s_mx.n)
        K.wait("vector", s_ssm, s_ssm.n)

        def fold1(e):
            inst = None
            for j in range(4):
                e.tensor_reduce(gmax[:, j:j + 1], mxb[:, j * 32:(j + 1) * 32], AX.X, ALU.max)
                inst = e.reduce_sum(gmax[:, 8 + j:9 + j], sqb2[:, j * 32:(j + 1) * 32], axis=AX.X)
            return inst
        K.step("vector", s_mx, fold1)
        K.dep("vector", s_mx, lambda e: e.tensor_scalar(gmax[:, 16:20], gmax[:, 8:12], 1.0 / DF, None, ALU.mult))
        K.dep("vector", s_mx, lambda e: e.tensor_scalar(gmax[:, 16:20], gmax[:, 16:20], EPS, None, ALU.add))
        s_ac = K.gsem("f1_ac")
        K.wait("scalar", s_mx, s_mx.n)
        K.dep("scalar", s_ac, lambda e: e.activation(gmax[:, 16:20], gmax[:, 16:20], AF.Ln))
        K.dep("scalar", s_ac, lambda e: e.activation(gmax[:, 16:20], gmax[:, 16:20], AF.Exp, scale=-0.5))
        K.wait("vector", s_ac, s_ac.n)
        K.dep("vector", s_mx, lambda e: e.tensor_tensor(gmax[:, 4:8], gmax[:, 0:4], gmax[:, 16:20], ALU.mult))
        K.dep("vector", s_mx, lambda e: e.tensor_scalar(gmax[:, 4:8], gmax[:, 4:8], 1e-5, None, ALU.max))
        K.dep("vector", s_mx, lambda e: e.tensor_scalar(gmax[:, 20:24], gmax[:, 4:8], 1.0 / 127.0, None, ALU.mult))
        for j in range(4):
            K.dep("vector", s_mx, lambda e, j=j: e.tensor_tensor(
                gmax[:, 20 + j:21 + j], gmax[:, 20 + j:21 + j], scl2[:, 1:2], ALU.mult))
        K.dep("vector", s_mx, lambda e: e.reciprocal(gmax[:, 4:8], gmax[:, 4:8]))
        K.dep("vector", s_mx, lambda e: e.tensor_scalar(gmax[:, 4:8], gmax[:, 4:8], 127.0, None, ALU.mult))
        K.dep("vector", s_mx, lambda e: e.tensor_tensor(gmax[:, 4:8], gmax[:, 4:8], gmax[:, 16:20], ALU.mult))
        K.wait("tensor", s_mx, s_mx.n)
        K.step("tensor", s_sc2, lambda e: e.transpose(pall[0:4, 3072:3200], gmax[:, 4:8], T["i32sb"][:]))
        K.wait("vector", s_sc2, s_sc2.n)
        K.dep("vector", s_sc2, lambda e: e.tensor_copy(s2b[0:4, 0:128], pall[0:4, 3072:3200]))
        K.wait("scalar", s_sc2, s_sc2.n)
        K.dma("scalar", s2row[0:1, t0:t0 + 512].rearrange("o (j q) -> o j q", q=128).squeeze(0),
              s2b[0:4, 0:128], sem=s_s2l)
        K.wait("scalar", s_s2l, s_s2l.n)
        K.wait("scalar", s_qa, s_qa.n)
        K.wait("scalar", s_qap, s_qap.n)
        vv = K.dma("scalar", s2b[:, 0:512], bcast_row(s2row, t0, 512), sem=s_s2l)
        K.wait("vector", s_s2l, vv)
        if upto < 4:
            continue
        b_qa, b_qap, b_rc, b_qi, b_qb = s_qa.n, s_qap.n, s_rc.n, s_qi.n, s_qb.n
        K.wait("gpsimd", s_s2l, vv)
        for cc in range(32):
            pool_q = cc % 4 in (1, 3)
            eq = "gpsimd" if pool_q else "vector"
            sq_ = s_qap if pool_q else s_qa
            qts = qtsl[:, (cc % 2) * 512:(cc % 2) * 512 + 512]
            qis = qisl[:, (cc % 2) * 512:(cc % 2) * 512 + 512]
            K.wait(eq, s_sl, b_sl + cc + 1)
            if cc >= 2 or si > 0:
                K.wait(eq, s_qi, b_qi + cc - 1)

            def qa(e, cc=cc, qts=qts):
                return e.tensor_tensor(qts, gfm[:, cc * 512:(cc + 1) * 512], s2b[:, 0:512], ALU.mult)
            K.step(eq, sq_, qa)
            K.wait("scalar", sq_, sq_.n)
            K.step("scalar", s_rc, lambda e, qts=qts: e.activation(qts, qts, AF.Copy, bias=RC))
            K.dep("scalar", s_rc, lambda e, qts=qts: e.activation(qts, qts, AF.Copy, bias=-RC))
            K.wait("vector", s_rc, b_rc + 2 * cc + 2)
            if cc >= 2 or si > 0:
                K.wait("vector", s_qb, b_qb + cc - 1)
            K.step("vector", s_qi, lambda e, qts=qts, qis=qis: e.tensor_copy(qis, qts))
            K.wait("vector", s_qi, s_qi.n)
            K.step("vector", s_qb,
                   lambda e, cc=cc, qis=qis: e.tensor_copy(gfm[:, cc * 512:(cc + 1) * 512], qis))
        if si >= 1:
            K.wait("tensor", s_oe, b_oe0 + si * 4)
        b_w2 = s_f2.n
        for cc in range(32):
            b2 = cc % 2
            if cc >= 2:
                K.wait("sync" if b2 == 0 else "scalar", s_f2, b_w2 + cc - 1)
            lw2 = K.dma("sync" if b2 == 0 else "scalar", w2s[:, b2 * 1024:(b2 + 1) * 1024],
                        w2qd[cc * 128:(cc + 1) * 128, :], sem=s_w2l[b2])
            K.wait("tensor", s_w2l[b2], lw2)
            K.wait("tensor", s_qb, b_qb + cc + 1)

            def f2(e, cc=cc, b2=b2):
                inst = None
                for tt_ in range(4):
                    for eh in range(2):
                        inst = e.matmul(pall[:, tt_ * 1024 + eh * 512:tt_ * 1024 + eh * 512 + 512],
                                        gfm[:, cc * 512 + tt_ * 128:cc * 512 + tt_ * 128 + 128],
                                        w2s[:, b2 * 1024 + eh * 512:b2 * 1024 + eh * 512 + 512],
                                        start=(cc == 0), stop=(cc == 31))
                return inst
            K.step("tensor", s_f2, f2)
        for tt_ in range(4):
            po = pall[:, tt_ * 1024:(tt_ + 1) * 1024]
            trow = t0 + tt_ * 128
            so = tt_ % 2
            if si * 4 + tt_ >= 2:
                K.wait("sync", s_ov, b_ov0 + 3 * (si * 4 + tt_) - 3)
            lr = K.dma("sync", rst[:, so * 2048:so * 2048 + 1024],
                       T["x1c"][trow:trow + 128, :], sem=s_rl[so])
            lr2 = K.dma("sync", rst[:, so * 2048 + 1024:so * 2048 + 2048],
                        T["attn1o"][trow:trow + 128, :], sem=s_rl[so])
            K.wait("scalar", s_f2, s_f2.n)
            K.wait("scalar", s_mx, s_mx.n)
            if si * 4 + tt_ >= 2:
                K.wait("scalar", s_os[so], ods[si * 4 + tt_ - 2])

            osl = ost2[:, so * 1024:(so + 1) * 1024]

            def oev(e, po=po, tt_=tt_, osl=osl):
                return e.activation(osl, po, AF.Copy, scale=gmax[:, 20 + tt_:21 + tt_])
            K.step("scalar", s_oe, oev)
            K.wait("vector", s_oe, s_oe.n)
            K.wait("vector", s_rl[so], lr2)
            K.dep("vector", s_ov, lambda e, osl=osl: e.tensor_tensor(osl, osl, b2bc[:], ALU.add))
            K.dep("vector", s_ov, lambda e, so=so, osl=osl: e.tensor_tensor(
                osl, osl, rst[:, so * 2048:so * 2048 + 1024], ALU.add))
            K.dep("vector", s_ov, lambda e, so=so, osl=osl: e.tensor_tensor(
                osl, osl, rst[:, so * 2048 + 1024:so * 2048 + 2048], ALU.add))
            K.wait("gpsimd", s_ov, s_ov.n)
            ods[si * 4 + tt_] = K.dma("gpsimd", T["y_own"][trow:trow + 128, :], osl,
                                      sem=s_os[so])
    K.barrier()


def build_ffn1(upto=99):
    nc = bass.Bass("TRN2", target_bir_lowering=False, debug=False)
    ins = ["xqt1", "st1", "w1_1t", "b1p", "w2_1t", "b2r", "x1c", "attn1o",
           "ident32", "ident16"]
    T = declare(nc, ["xqt1", "st1", "attn1o", "x1c"], "ExternalInput")
    T.update(declare(nc, ["w1_1t", "b1p", "w2_1t", "b2r", "ident32", "ident16"],
                     "ExternalInput", {**IN_SPECS}))
    T.update(declare(nc, ["y_own"], "ExternalOutput"))
    K = KB(nc)
    K.pall = nc.alloc_psum_tensor("pall", [128, 4096], F32)
    i32sb = K.gsb("ident32_sb", [128, 128], F32)
    i16sb = K.gsb("ident16_sb", [128, 128], BF16)
    sldi = K.gsem("ident_ld")
    K.dma("sync", i32sb[:], T["ident32"], sem=sldi)
    K.dma("sync", i16sb[:], T["ident16"], sem=sldi)
    T["i32sb"] = i32sb
    T["i16sb"] = i16sb
    K.barrier()
    emit_ffn1(K, T, upto)
    K.emit()
    return nc, ins, ["y_own"]


def build_p0(upto=99):
    nc = bass.Bass("TRN2", target_bir_lowering=False, debug=False)
    ins = ["x_seq", "x_seq_t", "x_cat", "x_cat_t", "w_qkv", "ident32"]
    T = declare(nc, ins, "ExternalInput")
    T.update(declare(nc, ["k0t", "v0", "q0t", "xqt0", "st0"], "ExternalOutput"))
    K = KB(nc)
    K.pall = nc.alloc_psum_tensor("pall", [128, 4096], F32)
    emit_p0(K, T, upto)
    K.emit()
    return nc, ins, ["k0t", "v0", "q0t", "xqt0", "st0"]


def build_fused():
    nc = bass.Bass("TRN2", target_bir_lowering=False, debug=False)
    T = declare(nc, list(IN_SPECS.keys()), "ExternalInput")
    scr = [k for k in SCR_SPECS if k != "y_own"]
    T.update(declare(nc, scr, "Internal"))
    T.update(declare(nc, ["y_own"], "ExternalOutput"))
    K = KB(nc)
    K.pall = nc.alloc_psum_tensor("pall", [128, 4096], F32)
    i32sb = K.sb("ident32_sb", [128, 128], F32)
    i16sb = K.sb("ident16_sb", [128, 128], BF16)
    sldi = K.gsem("ident_ld")
    K.dma("sync", i32sb[:], T["ident32"], sem=sldi)
    K.dma("sync", i16sb[:], T["ident16"], sem=sldi)
    T["i32sb"] = i32sb
    T["i16sb"] = i16sb
    scl1_g = K.sb("scl_w1_g", [128, 2], F32)
    scl2_g = K.sb("scl_w2_g", [128, 2], F32)
    w1qd_g = nc.dram_tensor("w1qd", [1024, DF], BF16, kind="Internal").ap()
    w2qd_g = nc.dram_tensor("w2qd", [DF, 1024], BF16, kind="Internal").ap()
    T["scl_w1"] = scl1_g[:]
    T["scl_w2"] = scl2_g[:]
    T["w1qd_g"] = w1qd_g
    T["w2qd_g"] = w2qd_g
    K.barrier()
    for emitter in (emit_p0, emit_att0, emit_ffn0, emit_l1p, emit_att1, emit_ffn1):
        K.begin_phase()
        if emitter is emit_att0:
            sl1, sq1, st1_ = wquant_pf_slices(K, T, T["w1_0t"], 8, DF, T["scl_w1"], "pw1",
                                              dram_out=w1qd_g)
            sl2, _, _ = wquant_pf_slices(K, T, T["w2_0t"], 32, 1024, T["scl_w2"], "pw2",
                                         dram_out=w2qd_g, pre_wait=sq1,
                                         pre_stores=(st1_[1], st1_[2]))
            slices = sl1 + sl2
            idx = [0]

            def hook(it):
                if idx[0] < len(slices):
                    slices[idx[0]]()
                    idx[0] += 1
            emit_att0(K, T, hook)
            while idx[0] < len(slices):
                slices[idx[0]]()
                idx[0] += 1
        else:
            emitter(K, T)
        K.end_phase()
    K.emit()
    ins = list(IN_SPECS.keys())
    return nc, ins, ["y_own"]



def _run(build, cores, extra):
    nc, ins, outs = build()
    in_maps = []
    for c, ex in zip(cores, extra):
        d = {}
        for k in ins:
            v = ex.get(k) if k in ex else c.get(k)
            d[k] = v
        in_maps.append(d)
    res = run_bass_kernel_spmd(nc, in_maps, core_ids=list(range(8)))
    return res.results


FUSED = True


def kernel(**inputs):
    cores = prepare_core_inputs(inputs)
    if FUSED:
        r = _run(build_fused, cores, [{} for _ in range(8)])
        out = np.zeros((4, 4096, 1024), dtype=np.float32)
        for c in range(8):
            b, h = c // 2, c % 2
            out[b, h * TH:(h + 1) * TH] = np.asarray(r[c]["y_own"], dtype=np.float32)
        return out
    none8 = [{} for _ in range(8)]
    r1 = _run(build_p0, cores, none8)
    r2 = _run(build_att0, cores, r1)
    r3 = _run(build_ffn0, cores, [{**a, **b} for a, b in zip(r1, r2)])
    r4 = _run(build_l1p, cores, r3)
    r5 = _run(build_att1, cores, r4)
    r6 = _run(build_ffn1, cores, [{**a, **b, **c} for a, b, c in zip(r3, r4, r5)])
    out = np.zeros((4, 4096, 1024), dtype=np.float32)
    for c in range(8):
        b, h = c // 2, c % 2
        out[b, h * TH:(h + 1) * TH] = np.asarray(r6[c]["y_own"], dtype=np.float32)
    return out
